# version 02
# baseline (speedup 1.0000x reference)
import numpy as np
import ml_dtypes
import concourse.bass as bass
import concourse.mybir as mybir
from concourse.bass_utils import run_bass_kernel_spmd

F32 = mybir.dt.float32
BF16 = mybir.dt.bfloat16
U8 = mybir.dt.uint8
AF = mybir.ActivationFunctionType
ALU = mybir.AluOpType
AX = mybir.AxisListType.X
BF = ml_dtypes.bfloat16

NCORE = 8
B, S, D = 4, 4096, 2048
TPC = 2048
H0, D0, BS0 = 32, 64, 256
NB0 = S // BS0
H1, DH, DC, DQC = 16, 128, 512, 1024
FF0 = 8192
FFE = 1408
NE = 8
EPS = 1e-5
DEBUG_B = False



class _Rec:
    __slots__ = ("key", "meth", "args", "kwargs", "sem_incs")

    def __init__(self, key, meth, args, kwargs):
        self.key, self.meth, self.args, self.kwargs = key, meth, args, kwargs
        self.sem_incs = []

    def then_inc(self, sem, amt):
        self.sem_incs.append((sem, amt))
        return self


class _EngRec:
    def __init__(self, ctx, key):
        self._ctx, self._key = ctx, key

    def __getattr__(self, meth):
        ctx, key = self._ctx, self._key

        def call(*a, **k):
            r = _Rec(key, meth, a, k)
            ctx.ops.append(r)
            return r

        return call


_ENG_NAMES = {"t": "tensor", "v": "vector", "s": "scalar", "g": "gpsimd", "y": "sync"}


class Ctx:

    def __init__(self, name):
        self.nc = bass.Bass("TRN2", target_bir_lowering=False, debug=False)
        self.name = name
        self._exit = []
        self.cnt = {}
        self.sem = {}
        self.ops = []
        self.prefix = ""
        self.io = {}
        self.t = _EngRec(self, "t")
        self.v = _EngRec(self, "v")
        self.s = _EngRec(self, "s")
        self.g = _EngRec(self, "g")
        self.y = _EngRec(self, "y")

    def __enter__(self):
        return self

    def __exit__(self, *a):
        return False

    def alloc_sems(self, names):
        for n in names:
            if n in self.sem:
                continue
            cm = self.nc.semaphore(f"sem_{n}")
            self.sem[n] = cm.__enter__()
            self._exit.append(cm)
            self.cnt[n] = 0

    def mark(self):
        return len(self._exit)

    def release(self, m):
        for cm in reversed(self._exit[m:]):
            cm.__exit__(None, None, None)
        del self._exit[m:]

    def inc(self, rec, key, amt=1):
        rec.then_inc(self.sem[key], amt)
        self.cnt[key] += amt
        return self.cnt[key]

    def tick(self, key):
        return self.cnt[key]

    def finalize(self):
        ops = self.ops
        with self.nc.Block() as block:
            for key, name in _ENG_NAMES.items():
                def body(eng, key=key):
                    for r in ops:
                        if r.key != key:
                            continue
                        instr = getattr(eng, r.meth)(*r.args, **r.kwargs)
                        for sem, amt in r.sem_incs:
                            instr = instr.then_inc(sem, amt)
                getattr(block, name)(body)
        for cm in reversed(self._exit):
            cm.__exit__(None, None, None)
        self._exit = []
        return self.nc

    def sb(self, name, shape, dtype):
        cm = self.nc.sbuf_tensor(self.prefix + name, shape, dtype)
        t = cm.__enter__()
        self._exit.append(cm)
        return t

    def ps(self, name, shape, dtype=F32):
        cm = self.nc.psum_tensor(self.prefix + name, shape, dtype)
        t = cm.__enter__()
        self._exit.append(cm)
        return t

    def din(self, name, shape, dtype):
        if name in self.io:
            return self.io[name]
        return self.nc.dram_tensor(self.prefix + name, list(shape), dtype,
                                   kind="ExternalInput").ap()

    def dout(self, name, shape, dtype):
        if name in self.io:
            return self.io[name]
        return self.nc.dram_tensor(self.prefix + name, list(shape), dtype,
                                   kind="ExternalOutput").ap()


def rsqrt_cols(ctx, out_col, var_col, eps, eng_wait_key=None):
    s = ctx.s
    s.activation(out_col, var_col, AF.Ln, bias=float(eps), scale=1.0)
    n = ctx.inc(s.activation(out_col, out_col, AF.Exp, scale=-0.5), "s")
    return n



def _host_const():
    c = {}
    c["ident_bf"] = np.eye(128, dtype=np.float32)
    f32 = np.float32
    base = f32(1.0) / f32(2.0) ** f32(8.0 / H0)
    rate = base ** (np.arange(H0, dtype=f32) + 1.0)
    factor = f32(1.0 + 1e-05)
    slope = (rate * factor).astype(f32)
    r = np.arange(BS0, dtype=f32) + 1.0
    qd = np.exp(-slope[:, None] * r[None, :]).astype(f32)
    kd = np.exp(-slope[:, None] * (BS0 - r[None, :])).astype(f32)
    dd = slope[:, None, None] * (r[:, None] - r[None, :])[None]
    dd = np.where(dd >= 0, np.exp(-dd), 0.0).astype(f32)
    bd = np.exp(-slope * f32(BS0)).astype(f32)
    c["qd_rep"] = np.repeat(qd.reshape(1, H0 * BS0), 128, 0).astype(BF)
    c["kd_rep"] = np.repeat(kd.reshape(1, H0 * BS0), 128, 0).astype(BF)
    ddT = np.transpose(dd, (0, 2, 1))
    ddT = ddT.reshape(H0, 2, 128, BS0)
    c["ddT"] = np.transpose(ddT, (2, 0, 1, 3)).reshape(128, H0 * 2 * BS0).astype(BF)
    c["bd"] = bd
    pos = np.arange(S, dtype=np.float64)
    idx = np.arange(DH // 2, dtype=np.float64)
    theta = (pos[:, None] * (10000.0 ** (-2.0 * idx / DH))[None, :])
    cos = np.cos(theta).astype(f32)
    sin = np.sin(theta).astype(f32)
    C2 = np.empty((S, DH), f32)
    S2 = np.empty((S, DH), f32)
    C2[:, 0::2] = cos; C2[:, 1::2] = cos
    S2[:, 0::2] = sin; S2[:, 1::2] = sin
    c["ropeC"] = C2
    c["ropeS"] = S2
    m = np.zeros((4, 128, 512), f32)
    for o in range(4):
        for k in range(128):
            m[o, k, :] = (np.arange(512) >= (o * 128 + k)).astype(f32)
    c["diagmask"] = m.reshape(4 * 128, 512)
    return c


_CONST = None


def host_tables():
    global _CONST
    if _CONST is None:
        _CONST = _host_const()
    return _CONST



def build_A0(ctx=None):
    own = ctx is None
    with (Ctx("A0") if own else ctx) as ctx:
        nc = ctx.nc
        xin = ctx.din("xin", [TPC, D], F32)
        identin = ctx.din("identin", [128, 128], F32)
        xn0T_d = ctx.dout("xn0T", [128, 16 * TPC], BF16)

        ctx.alloc_sems(["t", "v", "s", "dy", "dg"])
        t, v, s, g, y = ctx.t, ctx.v, ctx.s, ctx.g, ctx.y

        xt = [ctx.sb(f"xt{i}", [128, D], F32) for i in range(2)]
        sq = ctx.sb("sq", [128, D], F32)
        xnb = ctx.sb("xnb", [128, D], BF16)
        st_ = ctx.sb("stats", [128, 8], F32)
        ident = ctx.sb("ident", [128, 128], BF16)
        outT = ctx.sb("outT", [128, 16 * TPC], BF16)
        ptr = [ctx.ps(f"ptr{i}", [128, 128], BF16) for i in range(2)]

        ident_done = ctx.inc(g.dma_start(out=ident[:], in_=identin[:, :]), "dg", 16)

        ctx.inc(y.dma_start(out=xt[0][:], in_=xin[0:128, :]), "dy", 16)

        evac_done = [0] * 2
        tile_consumed = [0, 0]

        for k in range(16):
            buf = xt[k % 2]
            if k + 1 < 16:
                nbuf = (k + 1) % 2
                if k >= 1:
                    y.wait_ge(ctx.sem["v"], tile_consumed[nbuf])
                ctx.inc(y.dma_start(out=xt[nbuf][:],
                                    in_=xin[(k + 1) * 128:(k + 2) * 128, :]), "dy", 16)
            v.wait_ge(ctx.sem["dy"], 16 * (k + 1))
            v.reduce_sum(st_[:, 0:1], buf[:], axis=AX)
            v.tensor_mul(sq[:], buf[:], buf[:])
            n_v = ctx.inc(v.reduce_sum(st_[:, 1:2], sq[:], axis=AX), "v")
            v.wait_ge(ctx.sem["v"], n_v)
            v.tensor_scalar(st_[:, 2:3], st_[:, 0:1], 1.0 / D, None, ALU.mult)
            n_v = ctx.inc(v.tensor_scalar(st_[:, 3:4], st_[:, 1:2], 1.0 / D, None,
                                          ALU.mult), "v")
            v.wait_ge(ctx.sem["v"], n_v)
            n_v = ctx.inc(v.tensor_mul(st_[:, 4:5], st_[:, 2:3], st_[:, 2:3]), "v")
            v.wait_ge(ctx.sem["v"], n_v)
            n_v = ctx.inc(v.scalar_tensor_tensor(st_[:, 4:5], st_[:, 3:4], float(EPS),
                                                 st_[:, 4:5], ALU.add, ALU.subtract), "v")
            s.wait_ge(ctx.sem["v"], n_v)
            s.activation(st_[:, 5:6], st_[:, 4:5], AF.Ln, scale=1.0)
            n_s = ctx.inc(s.activation(st_[:, 5:6], st_[:, 5:6], AF.Exp, scale=-0.5), "s")
            v.wait_ge(ctx.sem["s"], n_s)
            n_v = ctx.inc(v.scalar_tensor_tensor(st_[:, 6:7], st_[:, 2:3], -1.0,
                                                 st_[:, 5:6], ALU.mult, ALU.mult), "v")
            v.wait_ge(ctx.sem["v"], n_v)
            n_s = ctx.inc(
                v.tensor_scalar(xnb[:], buf[:], st_[:, 5:6], st_[:, 6:7],
                                ALU.mult, ALU.add), "v")
            tile_consumed[k % 2] = n_s
            t.wait_ge(ctx.sem["v"], n_s)
            if k == 0:
                t.wait_ge(ctx.sem["dg"], ident_done)
            for f in range(16):
                pb = ptr[f % 2]
                t.wait_ge(ctx.sem["v"], evac_done[f % 2])
                n_t = ctx.inc(t.transpose(pb[:], xnb[:, f * 128:(f + 1) * 128],
                                          ident[:]), "t")
                v.wait_ge(ctx.sem["t"], n_t)
                evac_done[f % 2] = ctx.inc(
                    v.tensor_copy(outT[:, f * TPC + k * 128: f * TPC + (k + 1) * 128],
                                  pb[:]), "v")

        y.wait_ge(ctx.sem["v"], ctx.tick("v"))
        ctx.inc(y.dma_start(out=xn0T_d[:, :], in_=outT[:]), "dy", 16)
        y.wait_ge(ctx.sem["dy"], ctx.tick("dy"))

        return ctx.finalize() if own else None



def build_A1(ctx=None):
    NT = 8
    own = ctx is None
    with (Ctx("A1") if own else ctx) as ctx:
        xn0T_d = ctx.din("xn0T", [128, 16 * S], BF16)
        wqkv_d = [ctx.din(f"wqkv{gidx}", [128, 16 * 12 * 128], F32) for gidx in range(2)]
        qdrep_d = ctx.din("qdrep", [128, 16 * BS0], BF16)
        kdrep_d = ctx.din("kdrep", [128, 16 * BS0], BF16)
        ddT_d = ctx.din("ddT", [128, 16 * 2 * BS0], BF16)
        identin = ctx.din("identin", [128, 128], F32)
        bdcol_d = ctx.din("bdcol", [64, 16], F32)
        attnT_d = ctx.dout("attnT", [128, 8 * S], BF16)

        ctx.alloc_sems(["t", "v", "s", "dy", "dg"])
        t, v, s, g, y = ctx.t, ctx.v, ctx.s, ctx.g, ctx.y
        sem, inc, tick = ctx.sem, ctx.inc, ctx.tick

        wsb = ctx.sb("wsb", [128, 16 * 12 * 128], BF16)
        xtile = [ctx.sb(f"xtile{i}", [128, 16 * 512], BF16) for i in range(2)]
        qkvT = ctx.sb("qkvT", [128, 12 * 512], BF16)
        attnT = ctx.sb("attnT_sb", [128, 4 * S], BF16)
        qd_sb = ctx.sb("qd_sb", [128, 16 * BS0], BF16)
        kd_sb = ctx.sb("kd_sb", [128, 16 * BS0], BF16)
        dd_sb = ctx.sb("dd_sb", [128, 16 * 2 * BS0], BF16)
        bd_sb = ctx.sb("bd_sb", [64, 16], F32)
        ident = ctx.sb("ident", [128, 128], BF16)
        qq_sb = ctx.sb("qq_sb", [64, BS0], BF16)
        kk_sb = ctx.sb("kk_sb", [64, BS0], BF16)
        kt_sb = ctx.sb("kt_sb", [128, 2 * 64], BF16)
        vt_sb = ctx.sb("vt_sb", [128, 2 * 64], BF16)
        awdd = ctx.sb("awdd", [128, 2 * BS0], BF16)
        st_f = ctx.sb("st_f", [64, 8 * 64], F32)
        st_b = ctx.sb("st_b", [64, 64], BF16)

        ps_mm = [ctx.ps(f"psmm{i}", [128, 512], F32) for i in range(2)]
        ps_aw = ctx.ps("psaw", [128, 512], F32)
        ps_tr = ctx.ps("pstr", [128, 256], BF16)
        ps_ys = ctx.ps("psys", [64, 256], F32)
        ps_kv = ctx.ps("pskv", [64, 64], F32)

        inc(g.dma_start(out=ident[:], in_=identin[:, :]), "dg", 16)
        inc(g.dma_start(out=qd_sb[:], in_=qdrep_d[:, :]), "dg", 16)
        inc(g.dma_start(out=kd_sb[:], in_=kdrep_d[:, :]), "dg", 16)
        inc(g.dma_start(out=dd_sb[:], in_=ddT_d[:, :]), "dg", 16)
        inc(g.dma_start(out=bd_sb[:], in_=bdcol_d[:, :]), "dg", 16)
        const_ok = tick("dg")

        v.wait_ge(sem["dg"], const_ok)
        inc(v.memset(st_f[:], 0.0), "v")

        evac_tick = [0, 0]
        last_attn_store = 0
        for grp in range(2):
            g.wait_ge(sem["t"], tick("t"))
            wg_done = inc(g.dma_start(out=wsb[:], in_=wqkv_d[grp][:, :]), "dg", 16)

            if grp > 0:
                s.wait_ge(sem["dy"], tick("dy"))
                v.wait_ge(sem["t"], tick("t"))
                inc(v.memset(st_f[:], 0.0), "v")

            for tt in range(NT):
                xb = xtile[tt % 2]
                y.wait_ge(sem["t"], tick("t"))
                if xn0T_d.shape[0] == 256:
                    rb = 128 * (tt // 4)
                    t4 = (tt % 4) * 512
                    xload = inc(y.dma_start(
                        out=xb[:].rearrange("p (c t) -> p c t", c=16),
                        in_=xn0T_d[rb:rb + 128, :].rearrange("p (c t) -> p c t", c=16)[:, :, t4:t4 + 512]), "dy", 16)
                else:
                    xload = inc(y.dma_start(
                        out=xb[:].rearrange("p (c t) -> p c t", c=16),
                        in_=xn0T_d.rearrange("p (c t) -> p c t", c=16)[:, :, tt * 512:(tt + 1) * 512]), "dy", 16)
                qkvT_free_v = tick("v")
                for oc in range(12):
                    pb = ps_mm[oc % 2]
                    t.wait_ge(sem["dy"], xload)
                    t.wait_ge(sem["dg"], wg_done)
                    t.wait_ge(sem["s"], evac_tick[oc % 2])
                    mm = None
                    for fc in range(16):
                        mm = t.matmul(pb[:], wsb[:, (fc * 12 + oc) * 128:(fc * 12 + oc + 1) * 128],
                                      xb[:, fc * 512:(fc + 1) * 512],
                                      start=(fc == 0), stop=(fc == 15))
                    n_t = inc(mm, "t")
                    s.wait_ge(sem["t"], n_t)
                    s.wait_ge(sem["v"], qkvT_free_v)
                    evac_tick[oc % 2] = inc(
                        s.activation(qkvT[:, oc * 512:(oc + 1) * 512], pb[:], AF.Silu), "s")
                qkv_ready = tick("s")
                for half in range(2):
                    for hl in range(8):
                        h16 = grp * 8 + hl
                        ro0 = (hl % 2) * 64
                        qch, kch, vch = hl // 2, 4 + hl // 2, 8 + hl // 2
                        qT = qkvT[ro0:ro0 + 64, qch * 512 + half * 256: qch * 512 + half * 256 + 256]
                        kT = qkvT[ro0:ro0 + 64, kch * 512 + half * 256: kch * 512 + half * 256 + 256]
                        vT = qkvT[ro0:ro0 + 64, vch * 512 + half * 256: vch * 512 + half * 256 + 256]
                        idq = ident[ro0:ro0 + 64, ro0:ro0 + 64]

                        v.wait_ge(sem["s"], qkv_ready)
                        v.wait_ge(sem["t"], tick("t"))
                        v.tensor_tensor(qq_sb[:], qT, qd_sb[ro0:ro0 + 64, h16 * BS0:(h16 + 1) * BS0], ALU.mult)
                        nv_qq = inc(v.tensor_tensor(kk_sb[:], kT, kd_sb[ro0:ro0 + 64, h16 * BS0:(h16 + 1) * BS0], ALU.mult), "v")

                        t.wait_ge(sem["s"], qkv_ready)
                        t.wait_ge(sem["v"], tick("v"))
                        mm = None
                        for jc in range(2):
                            mm = t.matmul(ps_aw[:, jc * 256:(jc + 1) * 256],
                                          kT[:, jc * 128:(jc + 1) * 128], qT,
                                          start=True, stop=True)
                        n_aw = inc(mm, "t")
                        mm = None
                        for jc in range(2):
                            mm = t.transpose(ps_tr[:, jc * 64:(jc + 1) * 64],
                                             kk_sb[:, jc * 128:(jc + 1) * 128],
                                             ident[0:64, 0:64])
                        for jc in range(2):
                            mm = t.transpose(ps_tr[:, 128 + jc * 64:128 + (jc + 1) * 64],
                                             vT[:, jc * 128:(jc + 1) * 128], idq)
                        n_tr = inc(mm, "t")

                        v.wait_ge(sem["t"], n_tr)
                        for jc in range(2):
                            v.tensor_tensor(
                                awdd[:, jc * 256:(jc + 1) * 256],
                                ps_aw[:, jc * 256:(jc + 1) * 256],
                                dd_sb[:, (h16 * 2 + jc) * 256:(h16 * 2 + jc + 1) * 256],
                                ALU.mult)
                        v.tensor_copy(kt_sb[:], ps_tr[:, 0:128])
                        nv_aw = inc(v.tensor_copy(vt_sb[:], ps_tr[:, 128:256]), "v")

                        nv_st = inc(v.tensor_copy(st_b[:], st_f[:, hl * 64:(hl + 1) * 64]), "v")

                        t.wait_ge(sem["v"], nv_st)
                        t.wait_ge(sem["s"], tick("s"))
                        for jc in range(2):
                            t.matmul(ps_ys[:], vt_sb[:, jc * 64:(jc + 1) * 64],
                                     awdd[:, jc * 256:(jc + 1) * 256],
                                     start=(jc == 0), stop=False)
                        mm = t.matmul(ps_ys[:], st_b[:], qq_sb[:], start=False, stop=True)
                        n_ys = inc(mm, "t")
                        mm = None
                        for jc in range(2):
                            mm = t.matmul(ps_kv[:], kt_sb[:, jc * 64:(jc + 1) * 64],
                                          vt_sb[:, jc * 64:(jc + 1) * 64],
                                          start=(jc == 0), stop=(jc == 1))
                        n_kv = inc(mm, "t")

                        s.wait_ge(sem["t"], n_ys)
                        fch = hl // 2
                        ro = (hl % 2) * 64
                        blk0 = tt * 512 + half * 256
                        last_attn_store = inc(
                            s.copy(attnT[ro:ro + 64, fch * S + blk0: fch * S + blk0 + 256],
                                   ps_ys[:]), "s")

                        v.wait_ge(sem["t"], n_kv)
                        nv_up = inc(v.scalar_tensor_tensor(
                            st_f[:, hl * 64:(hl + 1) * 64],
                            st_f[:, hl * 64:(hl + 1) * 64],
                            bd_sb[0:64, h16:h16 + 1],
                            ps_kv[:], ALU.mult, ALU.add), "v")
                        v.wait_ge(sem["v"], nv_up)

            y.wait_ge(sem["s"], last_attn_store)
            if attnT_d.shape[0] == 256:
                inc(y.dma_start(
                    out=attnT_d[0:128, :].rearrange("p (c t) -> p c t", t=TPC)[:, grp * 4:(grp + 1) * 4, :],
                    in_=attnT[:].rearrange("p (c t) -> p c t", t=S)[:, :, 0:TPC]), "dy", 16)
                inc(y.dma_start(
                    out=attnT_d[128:256, :].rearrange("p (c t) -> p c t", t=TPC)[:, grp * 4:(grp + 1) * 4, :],
                    in_=attnT[:].rearrange("p (c t) -> p c t", t=S)[:, :, TPC:S]), "dy", 16)
            else:
                inc(y.dma_start(out=attnT_d[:, grp * 4 * S:(grp + 1) * 4 * S], in_=attnT[:]),
                    "dy", 16)
        y.wait_ge(sem["dy"], tick("dy"))
        return ctx.finalize() if own else None



def build_B(ctx=None, upto=99):
    NT4 = 4
    own = ctx is None
    with (Ctx("B") if own else ctx) as ctx:
        x_d = ctx.din("x", [TPC, D], F32)
        xn0T_d = ctx.din("xn0T", [128, 16 * TPC], BF16)
        attnT_d = ctx.din("attnT", [128, 16 * TPC], BF16)
        normcol_d = ctx.din("normcol", [128, 16], F32)
        gatewT_d = ctx.din("gatewT", [128, 16 * 16 * 128], F32)
        outwT_d = ctx.din("outwT", [128, 16 * 16 * 128], F32)
        wg8_d = ctx.din("wg8", [128, 16 * 64 * 128], U8)
        wu8_d = ctx.din("wu8", [128, 16 * 64 * 128], U8)
        wd8_d = ctx.din("wd8", [128, 64 * 16 * 128], U8)
        gscol_d = ctx.din("gscol", [128, 64], F32)
        uscol_d = ctx.din("uscol", [128, 64], F32)
        dsrow_d = ctx.din("dsrow", [1, D], F32)
        identin = ctx.din("identin", [128, 128], F32)
        ropeC_d = ctx.din("ropeC", [TPC, DH], F32)
        ropeS_d = ctx.din("ropeS", [TPC, DH], F32)
        gwT_d = ctx.din("gwT", [128, 16 * 8], F32)
        gbrep_d = ctx.din("gbrep", [128, 8], F32)
        wdkvT_d = ctx.din("wdkvT", [128, 16 * 4 * 128], F32)
        wdqT_d = ctx.din("wdqT", [128, 16 * 8 * 128], F32)
        wkrT_d = ctx.din("wkrT", [128, 16 * 16 * 128], F32)
        bkv_col_d = ctx.din("bkvcol", [128, 12], F32)
        bkr_rep_d = ctx.din("bkrrep", [128, D], F32)
        selcol_d = ctx.din("selcol", [128, 2], F32)

        x1_d = ctx.dout("x1", [TPC, D], BF16)
        xn1T_d = ctx.dout("xn1T", [128, 16 * TPC], BF16)
        ckvT_d = ctx.dout("ckvT", [128, 4 * TPC], BF16)
        cqT_d = ctx.dout("cqT", [128, 8 * TPC], BF16)
        krT_d = ctx.dout("krT", [128, 16 * TPC], BF16)
        cwrows_d = ctx.dout("cwrows", [8, TPC], F32)

        hw_scr = ctx.nc.dram_tensor("hw_scr", [128, 64 * TPC], BF16).ap()
        hdbg_d = ctx.dout("hdbg", [128, 64 * TPC], BF16) if DEBUG_B else None
        zdbg_d = ctx.dout("zdbg", [128, 16 * TPC], BF16) if DEBUG_B else None
        outw_scr = ctx.nc.dram_tensor("outw_scr", [128, 16 * 16 * 128], BF16).ap()

        ctx.alloc_sems(["t", "v", "s", "dy", "dg", "dz"])
        t, v, s, g, y = ctx.t, ctx.v, ctx.s, ctx.g, ctx.y
        sem, inc, tick = ctx.sem, ctx.inc, ctx.tick

        bufA = ctx.sb("bufA", [128, 16 * TPC], BF16)
        bufB = ctx.sb("bufB", [128, 16 * TPC], BF16)
        hsb = ctx.sb("hsb", [128, 64 * 128], BF16)
        scratch16 = ctx.sb("scratch16", [128, 16384], U8)
        sc_bf = scratch16[:].bitcast(BF16)
        sc_f = scratch16[:].bitcast(F32)
        gwtile = [sc_bf[:, i * 2048:(i + 1) * 2048] for i in range(2)]
        wgu = [sc_bf[:, i * 4096:(i + 1) * 4096] for i in range(2)]
        xtile = sc_f[:, 0:2048]
        scrA = ctx.sb("scrA", [128, D], F32)
        scrB = ctx.sb("scrB", [128, D], F32)
        scrD = ctx.sb("scrD", [128, D], BF16)
        owdm = ctx.sb("owdm", [128, D], BF16)
        rowA = ctx.sb("rowA", [1, D], F32)
        rowB = ctx.sb("rowB", [1, D], F32)
        rsrep = sc_f[:, 2048:4096]
        ident = ctx.sb("ident", [128, 128], F32)
        ncol = ctx.sb("ncol", [128, 16], F32)
        gscol = ctx.sb("gscolsb", [128, 64], F32)
        uscol = ctx.sb("uscolsb", [128, 64], F32)
        onesb = ctx.sb("onesb", [128, 1], BF16)
        ones1f = ctx.sb("ones1f", [1, 512], F32)
        st_ = ctx.sb("stats", [128, 8], F32)
        gwsb = ctx.sb("gwsb", [128, 16 * 8], BF16)
        gbrep = ctx.sb("gbrepsb", [128, 8], F32)
        sctok = ctx.sb("sctok", [128, 8], F32)
        cwtok = ctx.sb("cwtok", [128, 8], F32)
        r8a = ctx.sb("r8a", [128, 8], F32)
        r8b = ctx.sb("r8b", [128, 8], F32)
        r8c = ctx.sb("r8c", [128, 8], F32)
        bkvcol = ctx.sb("bkvcolsb", [128, 12], F32)
        selcol = ctx.sb("selcolsb", [128, 2], F32)

        psA = ctx.ps("psA", [128, 1024], F32)
        psB = ctx.ps("psB", [128, 1024], F32)
        psC = ctx.ps("psC", [128, 2048], F32)

        for ap_sb, ap_d in ((ident[:], identin), (ncol[:], normcol_d),
                            (gscol[:], gscol_d), (uscol[:], uscol_d),
                            (gwsb[:], gwT_d), (gbrep[:], gbrep_d),
                            (bkvcol[:], bkv_col_d)):
            inc(g.dma_start(out=ap_sb, in_=ap_d[:, :]), "dg", 16)
        inc(g.dma_start(out=selcol[:], in_=selcol_d[:, :]), "dg", 16)
        n_c0 = tick("dg")
        inc(y.dma_start(out=bufA[:], in_=xn0T_d[:, :]), "dy", 16)
        if attnT_d.shape[0] == 512:
            inc(y.dma_start(out=bufB[:, 0:8 * TPC], in_=attnT_d[0:128, :]), "dy", 16)
            inc(y.dma_start(out=bufB[:, 8 * TPC:16 * TPC], in_=attnT_d[256:384, :]),
                "dy", 16)
            n_res = tick("dy")
            scab = scrA[:].bitcast(BF16)
            ev_bl = 0
            for fcb in range(16):
                rb2 = 128 if fcb < 8 else 384
                fcl = fcb % 8
                y.wait_ge(sem["v"], ev_bl)
                lhi = inc(y.dma_start(
                    out=scab[:, 0:TPC],
                    in_=attnT_d[rb2:rb2 + 128, fcl * TPC:(fcl + 1) * TPC]), "dy", 16)
                v.wait_ge(sem["dy"], lhi)
                if fcb == 0:
                    v.wait_ge(sem["dg"], n_c0)
                v.tensor_scalar(bufB[:, fcb * TPC:(fcb + 1) * TPC],
                                bufB[:, fcb * TPC:(fcb + 1) * TPC],
                                selcol[:, 0:1], None, ALU.mult)
                ev_bl = inc(v.scalar_tensor_tensor(
                    bufB[:, fcb * TPC:(fcb + 1) * TPC], scab[:, 0:TPC],
                    selcol[:, 1:2], bufB[:, fcb * TPC:(fcb + 1) * TPC],
                    ALU.mult, ALU.add), "v")
            blend_done = ev_bl
        else:
            inc(y.dma_start(out=bufB[:], in_=attnT_d[:, :]), "dy", 16)
            n_res = tick("dy")
            blend_done = 0
        v.wait_ge(sem["dg"], n_c0)
        v.memset(onesb[:], 1.0)
        nv0 = inc(v.memset(ones1f[:], 1.0), "v")
        for fc in range(16):
            dst = gwtile[fc % 2]
            g.wait_ge(sem["dz"], max(0, (fc - 1)) * 16)
            ld = inc(g.dma_start(out=dst, in_=outwT_d[:, fc * 2048:(fc + 1) * 2048]),
                     "dg", 16)
            y.wait_ge(sem["dg"], ld)
            inc(y.dma_start(out=outw_scr[:, fc * 2048:(fc + 1) * 2048], in_=dst),
                "dz", 16)
        outw_ready = tick("dz")

        if upto <= 0:
            y.wait_ge(sem["dz"], tick("dz")); y.wait_ge(sem["dy"], tick("dy"))
            return ctx.finalize() if own else None if own else None
        t.wait_ge(sem["dy"], n_res)
        t.wait_ge(sem["v"], nv0)
        for qt in range(4):
            t.wait_ge(sem["v"], tick("v"))
            mm = None
            for fc in range(16):
                mm = t.matmul(psA[0:1, 0:512], onesb[:],
                              bufA[:, fc * TPC + qt * 512: fc * TPC + (qt + 1) * 512],
                              start=(fc == 0), stop=(fc == 15))
            n_t = inc(mm, "t")
            v.wait_ge(sem["t"], n_t)
            inc(v.tensor_scalar(rowB[:, qt * 512:(qt + 1) * 512], psA[0:1, 0:512],
                                -128.0, None, ALU.mult), "v")
        sq = scrD[:, 0:512]
        s.wait_ge(sem["dy"], n_res)
        if blend_done:
            s.wait_ge(sem["v"], blend_done)
        for qt in range(4):
            for fc in range(16):
                ns = inc(s.activation(
                    sq, bufB[:, fc * TPC + qt * 512: fc * TPC + (qt + 1) * 512],
                    AF.Square), "s")
                t.wait_ge(sem["s"], ns)
                if fc == 0:
                    t.wait_ge(sem["v"], tick("v"))
                n_t = inc(t.matmul(psA[0:1, 0:512], onesb[:], sq,
                                   start=(fc == 0), stop=(fc == 15)), "t")
                if fc < 15:
                    s.wait_ge(sem["t"], n_t)
            v.wait_ge(sem["t"], tick("t"))
            n_v = inc(v.tensor_scalar(rowA[:, 0:512], psA[0:1, 0:512],
                                      1.0 / D, float(EPS), ALU.mult, ALU.add), "v")
            s.wait_ge(sem["v"], n_v)
            s.activation(rowA[:, 0:512], rowA[:, 0:512], AF.Ln, scale=1.0)
            n_s = inc(s.activation(rowA[:, 0:512], rowA[:, 0:512], AF.Exp, scale=-0.5),
                      "s")
            t.wait_ge(sem["s"], n_s)
            t.wait_ge(sem["v"], tick("v"))
            nb = inc(t.matmul(psB[:, 0:512], ones1f[:, 0:128], rowA[:, 0:512],
                              start=True, stop=True), "t")
            v.wait_ge(sem["t"], nb)
            inc(v.tensor_copy(rsrep[:, qt * 512:(qt + 1) * 512], psB[:, 0:512]), "v")
        rs_done_v = tick("v")

        if upto <= 1:
            y.wait_ge(sem["dz"], tick("dz")); y.wait_ge(sem["dy"], tick("dy"))
            return ctx.finalize() if own else None if own else None
        g.wait_ge(sem["dz"], outw_ready)
        evac_sg = [0, 0]
        zt1 = scrA
        gsig = scrD
        for oc in range(16):
            wt = gwtile[oc % 2]
            g.wait_ge(sem["t"], tick("t"))
            g.wait_ge(sem["s"], tick("s"))
            ldg = inc(g.dma_start(
                out=wt.rearrange("p (f o) -> p f o", o=128),
                in_=gatewT_d.rearrange("p (f o) -> p f o", o=128)[:, oc::16, :]), "dg", 16)
            t.wait_ge(sem["dg"], ldg)
            for tt in range(NT4):
                pb = (psA if tt % 2 == 0 else psB)[:, 0:512]
                t.wait_ge(sem["s"], evac_sg[tt % 2])
                mm = None
                for fc in range(16):
                    mm = t.matmul(pb, wt[:, fc * 128:(fc + 1) * 128],
                                  bufA[:, fc * TPC + tt * 512: fc * TPC + (tt + 1) * 512],
                                  start=(fc == 0), stop=(fc == 15))
                n_g = inc(mm, "t")
                s.wait_ge(sem["t"], n_g)
                if oc == 0 and tt == 0:
                    s.wait_ge(sem["v"], tick("v"))
                evac_sg[tt % 2] = inc(
                    s.activation(gsig[:, tt * 512:(tt + 1) * 512], pb, AF.Sigmoid), "s")
            n_gs = tick("s")
            v.wait_ge(sem["s"], n_gs)
            v.tensor_tensor(zt1[:], bufB[:, oc * TPC:(oc + 1) * TPC], rsrep, ALU.mult)
            v.tensor_scalar(zt1[:], zt1[:], ncol[:, oc:oc + 1], None, ALU.mult)
            inc(v.tensor_tensor(bufB[:, oc * TPC:(oc + 1) * TPC], zt1[:], gsig[:],
                                ALU.mult), "v")
        z_done_v = tick("v")
        if DEBUG_B:
            y.wait_ge(sem["v"], z_done_v)
            inc(y.dma_start(out=zdbg_d[:, :], in_=bufB[:]), "dz", 16)

        if upto <= 2:
            y.wait_ge(sem["dz"], tick("dz")); y.wait_ge(sem["dy"], tick("dy"))
            return ctx.finalize() if own else None if own else None
        silu_g = scrA
        hrow = scrD
        g.wait_ge(sem["v"], z_done_v)
        wgu_last = [0, 0]
        rq = [psB[0:1, 0:512], psB[0:1, 512:1024], psC[0:1, 0:512], psC[0:1, 512:1024]]
        n_rsum = 0
        for ic in range(64):
            wb = wgu[ic % 2]
            g.wait_ge(sem["t"], wgu_last[ic % 2])
            l1 = inc(g.dma_start(
                out=wb[:, 0:2048].rearrange("p (f o) -> p f o", o=128),
                in_=wg8_d.rearrange("p (f i) -> p f i", i=64 * 128)[:, :, ic * 128:(ic + 1) * 128]),
                "dg", 16)
            l2 = inc(g.dma_start(
                out=wb[:, 2048:4096].rearrange("p (f o) -> p f o", o=128),
                in_=wu8_d.rearrange("p (f i) -> p f i", i=64 * 128)[:, :, ic * 128:(ic + 1) * 128]),
                "dg", 16)
            t.wait_ge(sem["dg"], l2)
            for tt in range(NT4):
                pg = psA[:, 0:512]
                pu = psA[:, 512:1024]
                t.wait_ge(sem["s"], tick("s"))
                t.wait_ge(sem["v"], tick("v"))
                for fc in range(16):
                    t.matmul(pg, wb[:, fc * 128:(fc + 1) * 128],
                             bufA[:, fc * TPC + tt * 512: fc * TPC + (tt + 1) * 512],
                             start=(fc == 0), stop=False)
                n_gm = inc(t.matmul(pg, ones1f[:, 0:128],
                                    rowB[:, tt * 512:(tt + 1) * 512],
                                    start=False, stop=True), "t")
                for fc in range(16):
                    t.matmul(pu, wb[:, 2048 + fc * 128:2048 + (fc + 1) * 128],
                             bufA[:, fc * TPC + tt * 512: fc * TPC + (tt + 1) * 512],
                             start=(fc == 0), stop=False)
                n_um = inc(t.matmul(pu, ones1f[:, 0:128],
                                    rowB[:, tt * 512:(tt + 1) * 512],
                                    start=False, stop=True), "t")
                s.wait_ge(sem["t"], n_gm)
                n_sg = inc(s.activation(silu_g[:, tt * 512:(tt + 1) * 512], pg,
                                        AF.Silu, scale=gscol[:, ic:ic + 1]), "s")
                v.wait_ge(sem["t"], n_um)
                v.wait_ge(sem["s"], n_sg)
                inc(v.scalar_tensor_tensor(hrow[:, tt * 512:(tt + 1) * 512], pu,
                                           uscol[:, ic:ic + 1],
                                           silu_g[:, tt * 512:(tt + 1) * 512],
                                           ALU.mult, ALU.mult), "v")
            wgu_last[ic % 2] = tick("t")
            nv_h = tick("v")
            t.wait_ge(sem["v"], nv_h)
            mm = None
            for tt in range(NT4):
                mm = t.matmul(rq[tt], onesb[:], hrow[:, tt * 512:(tt + 1) * 512],
                              start=(ic == 0), stop=(ic == 63))
            n_rsum = inc(mm, "t")
            y.wait_ge(sem["v"], nv_h)
            inc(y.dma_start(out=hw_scr[:, ic * TPC:(ic + 1) * TPC], in_=hrow[:]), "dz", 16)
        hw_done = tick("dz")
        if DEBUG_B:
            y.wait_ge(sem["dz"], hw_done)
            inc(y.dma_start(out=hdbg_d[:, :], in_=hw_scr[:, :]), "dz", 16)
        v.wait_ge(sem["t"], n_rsum)
        for tt in range(NT4):
            inc(v.tensor_scalar(rowA[:, tt * 512:(tt + 1) * 512], rq[tt],
                                -128.0, None, ALU.mult), "v")
        nv_rh = tick("v")
        g.wait_ge(sem["t"], tick("t"))
        lds = inc(g.dma_start(out=rowB[:], in_=dsrow_d[:, :]), "dg", 16)
        t.wait_ge(sem["dg"], lds)
        t.wait_ge(sem["v"], nv_rh)
        for qt in range(4):
            t.wait_ge(sem["v"], tick("v"))
            nb = inc(t.matmul(psB[:, 0:512], ones1f[:, 0:128],
                              rowB[:, qt * 512:(qt + 1) * 512], start=True, stop=True),
                     "t")
            v.wait_ge(sem["t"], nb)
            inc(v.tensor_copy(rsrep[:, qt * 512:(qt + 1) * 512], psB[:, 0:512]), "v")
        ds_done_v = tick("v")
        dsrep = rsrep

        if upto <= 3:
            y.wait_ge(sem["dz"], tick("dz")); y.wait_ge(sem["dy"], tick("dy"))
            return ctx.finalize() if own else None if own else None
        x1f = scrA
        xn1tok = scrB
        x1b = scrD
        psum_free_v = ds_done_v
        hsb_last_t = 0
        owdm_last = 0
        for tc in range(16):
            trow = tc * 128
            y.wait_ge(sem["dz"], hw_done)
            y.wait_ge(sem["t"], hsb_last_t)
            hld = inc(y.dma_start(
                out=hsb[:].rearrange("p (i c) -> p i c", c=128),
                in_=hw_scr.rearrange("p (i c) -> p i c", c=TPC)[:, :, trow:trow + 128]),
                "dy", 16)
            y.wait_ge(sem["v"], tick("v"))
            xld = inc(y.dma_start(out=xtile, in_=x_d[trow:trow + 128, :]), "dy", 16)
            for fc in range(16):
                g.wait_ge(sem["t"], owdm_last)
                lo = inc(g.dma_start(
                    out=owdm[:], in_=outw_scr[:, fc * 2048:(fc + 1) * 2048]), "dg", 16)
                t.wait_ge(sem["dg"], lo)
                if fc == 0:
                    t.wait_ge(sem["v"], psum_free_v)
                    if tc == 0:
                        t.wait_ge(sem["s"], tick("s"))
                mm = None
                for ot in range(4):
                    pa = (psA if ot < 2 else psB)[:, (ot % 2) * 512:(ot % 2) * 512 + 512]
                    mm = t.matmul(pa,
                                  bufB[:, fc * TPC + trow: fc * TPC + trow + 128],
                                  owdm[:, ot * 512:(ot + 1) * 512],
                                  start=(fc == 0), stop=(fc == 15))
                owdm_last = inc(mm, "t")
            for ic in range(64):
                g.wait_ge(sem["t"], owdm_last)
                lw = inc(g.dma_start(
                    out=owdm[:], in_=wd8_d[:, ic * 2048:(ic + 1) * 2048]), "dg", 16)
                t.wait_ge(sem["dg"], lw)
                if ic == 0:
                    t.wait_ge(sem["dy"], hld)
                mm = None
                for ot in range(4):
                    mm = t.matmul(psC[:, ot * 512:(ot + 1) * 512],
                                  hsb[:, ic * 128:(ic + 1) * 128],
                                  owdm[:, ot * 512:(ot + 1) * 512],
                                  start=(ic == 0), stop=False)
                owdm_last = inc(mm, "t")
            mm = None
            for ot in range(4):
                mm = t.matmul(psC[:, ot * 512:(ot + 1) * 512],
                              rowA[:, trow:trow + 128],
                              ones1f[:, 0:512],
                              start=False, stop=True)
            inc(mm, "t")
            n_beta = tick("t")
            hsb_last_t = n_beta
            v.wait_ge(sem["t"], n_beta)
            v.wait_ge(sem["dy"], xld)
            v.tensor_tensor(x1f[:], psC[:], dsrep, ALU.mult)
            v.tensor_tensor(x1f[:, 0:1024], x1f[:, 0:1024], psA[:], ALU.add)
            v.tensor_tensor(x1f[:, 1024:2048], x1f[:, 1024:2048], psB[:], ALU.add)
            nv_x1 = inc(v.tensor_tensor(x1f[:], x1f[:], xtile, ALU.add), "v")
            v.wait_ge(sem["v"], nv_x1)
            nv_x1b = inc(v.tensor_copy(x1b[:], x1f[:]), "v")
            y.wait_ge(sem["v"], nv_x1b)
            inc(y.dma_start(out=x1_d[trow:trow + 128, :], in_=x1b[:]), "dz", 16)
            v.reduce_sum(st_[:, 0:1], x1f[:], axis=AX)
            v.tensor_mul(xn1tok[:], x1f[:], x1f[:])
            n_v = inc(v.reduce_sum(st_[:, 1:2], xn1tok[:], axis=AX), "v")
            v.wait_ge(sem["v"], n_v)
            v.tensor_scalar(st_[:, 2:3], st_[:, 0:1], 1.0 / D, None, ALU.mult)
            n_v = inc(v.tensor_scalar(st_[:, 3:4], st_[:, 1:2], 1.0 / D, None,
                                      ALU.mult), "v")
            v.wait_ge(sem["v"], n_v)
            n_v = inc(v.tensor_mul(st_[:, 4:5], st_[:, 2:3], st_[:, 2:3]), "v")
            v.wait_ge(sem["v"], n_v)
            n_v = inc(v.scalar_tensor_tensor(st_[:, 4:5], st_[:, 3:4], float(EPS),
                                             st_[:, 4:5], ALU.add, ALU.subtract), "v")
            s.wait_ge(sem["v"], n_v)
            s.activation(st_[:, 5:6], st_[:, 4:5], AF.Ln, scale=1.0)
            n_s = inc(s.activation(st_[:, 5:6], st_[:, 5:6], AF.Exp, scale=-0.5), "s")
            v.wait_ge(sem["s"], n_s)
            n_v = inc(v.scalar_tensor_tensor(st_[:, 6:7], st_[:, 2:3], -1.0,
                                             st_[:, 5:6], ALU.mult, ALU.mult), "v")
            v.wait_ge(sem["v"], n_v)
            nv_xn1 = inc(v.tensor_scalar(xn1tok[:], x1f[:], st_[:, 5:6], st_[:, 6:7],
                                         ALU.mult, ALU.add), "v")
            t.wait_ge(sem["v"], nv_xn1)
            evts = [nv_xn1, nv_xn1]
            for fc in range(16):
                pb = (psA if fc % 2 == 0 else psB)[:, 0:128]
                t.wait_ge(sem["v"], evts[fc % 2])
                n_t = inc(t.transpose(pb, xn1tok[:, fc * 128:(fc + 1) * 128],
                                      ident[:]), "t")
                v.wait_ge(sem["t"], n_t)
                evts[fc % 2] = inc(v.tensor_copy(
                    bufA[:, fc * TPC + trow: fc * TPC + trow + 128], pb), "v")
            nv_strip = tick("v")
            t.wait_ge(sem["v"], nv_strip)
            mm = None
            for fc in range(16):
                mm = t.matmul(psC[0:8, 0:128], gwsb[:, fc * 8:(fc + 1) * 8],
                              bufA[:, fc * TPC + trow: fc * TPC + trow + 128],
                              start=(fc == 0), stop=(fc == 15))
            n_sc = inc(mm, "t")
            v.wait_ge(sem["t"], n_sc)
            sc_row = xn1tok[0:8, 128:256]
            n_v = inc(v.tensor_copy(sc_row, psC[0:8, 0:128]), "v")
            t.wait_ge(sem["v"], n_v)
            n_t = inc(t.transpose(psA[0:128, 0:8], sc_row, ident[0:8, 0:8]), "t")
            v.wait_ge(sem["t"], n_t)
            n_v = inc(v.tensor_copy(sctok[:], psA[0:128, 0:8]), "v")
            v.wait_ge(sem["v"], n_v)
            n_v = inc(v.tensor_scalar(r8c[:], sctok[:], -1.0, None, ALU.mult), "v")
            v.wait_ge(sem["v"], n_v)
            n_v = inc(v.tensor_tensor(r8a[:], sctok[:], r8c[:], ALU.max), "v")
            s.wait_ge(sem["v"], n_v)
            s.activation(r8a[:], r8a[:], AF.Exp, scale=-1.0)
            n_s = inc(s.activation(r8a[:], r8a[:], AF.Ln, bias=1.0, scale=1.0), "s")
            v.wait_ge(sem["s"], n_s)
            n_v = inc(v.tensor_scalar(r8b[:], sctok[:], 0.0, None, ALU.max), "v")
            v.wait_ge(sem["v"], n_v)
            n_v = inc(v.tensor_add(r8a[:], r8a[:], r8b[:]), "v")
            s.wait_ge(sem["v"], n_v)
            s.activation(r8a[:], r8a[:], AF.Ln, scale=1.0)
            n_s = inc(s.activation(r8a[:], r8a[:], AF.Exp, scale=0.5), "s")
            v.wait_ge(sem["s"], n_s)
            n_v = inc(v.tensor_add(r8b[:], r8a[:], gbrep[:]), "v")
            v.wait_ge(sem["v"], n_v)
            n_v = inc(v.reduce_max(st_[:, 0:1], r8b[:], axis=AX), "v")
            v.wait_ge(sem["v"], n_v)
            n_v = inc(v.tensor_scalar(r8c[:], r8b[:], st_[:, 0:1], None, ALU.is_equal),
                      "v")
            v.wait_ge(sem["v"], n_v)
            n_v = inc(v.scalar_tensor_tensor(r8b[:], r8c[:], -1e30, r8b[:],
                                             ALU.mult, ALU.add), "v")
            v.wait_ge(sem["v"], n_v)
            n_v = inc(v.reduce_max(st_[:, 1:2], r8b[:], axis=AX), "v")
            v.wait_ge(sem["v"], n_v)
            n_v = inc(v.tensor_scalar(r8b[:], r8b[:], st_[:, 1:2], None, ALU.is_equal),
                      "v")
            v.wait_ge(sem["v"], n_v)
            n_v = inc(v.tensor_add(r8c[:], r8c[:], r8b[:]), "v")
            v.wait_ge(sem["v"], n_v)
            n_v = inc(v.tensor_mul(cwtok[:], r8c[:], r8a[:]), "v")
            t.wait_ge(sem["v"], n_v)
            n_t = inc(t.transpose(psA[0:8, 0:128], cwtok[:], ident[:]), "t")
            v.wait_ge(sem["t"], n_t)
            v.wait_ge(sem["dz"], tick("dz"))
            cw_row = xn1tok[0:8, 0:128]
            psum_free_v = inc(v.tensor_copy(cw_row, psA[0:8, 0:128]), "v")
            y.wait_ge(sem["v"], psum_free_v)
            inc(y.dma_start(out=cwrows_d[:, trow:trow + 128], in_=cw_row), "dz", 16)

        y.wait_ge(sem["v"], tick("v"))
        inc(y.dma_start(out=xn1T_d[:, :], in_=bufA[:]), "dz", 16)

        if upto <= 4:
            y.wait_ge(sem["dz"], tick("dz")); y.wait_ge(sem["dy"], tick("dy"))
            return ctx.finalize() if own else None if own else None
        cvq = scrA
        for dcc in range(12):
            wsrc = wdkvT_d if dcc < 4 else wdqT_d
            nch = 4 if dcc < 4 else 8
            wofs = dcc if dcc < 4 else dcc - 4
            wt = gwtile[dcc % 2]
            g.wait_ge(sem["t"], tick("t"))
            g.wait_ge(sem["dz"], tick("dz"))
            ldw = inc(g.dma_start(
                out=wt[:, 0:16 * 128].rearrange("p (f o) -> p f o", o=128),
                in_=wsrc.rearrange("p (f o) -> p f o", o=128)[:, wofs::nch, :]), "dg", 16)
            t.wait_ge(sem["dg"], ldw)
            v.wait_ge(sem["dz"], tick("dz"))
            for tt in range(NT4):
                pb = (psA if tt % 2 == 0 else psB)[:, 0:512]
                t.wait_ge(sem["v"], tick("v"))
                mm = None
                for fc in range(16):
                    mm = t.matmul(pb, wt[:, fc * 128:(fc + 1) * 128],
                                  bufA[:, fc * TPC + tt * 512: fc * TPC + (tt + 1) * 512],
                                  start=(fc == 0), stop=(fc == 15))
                n_t = inc(mm, "t")
                v.wait_ge(sem["t"], n_t)
                inc(v.tensor_scalar(cvq[:, tt * 512:(tt + 1) * 512], pb,
                                    bkvcol[:, dcc:dcc + 1], None, ALU.add), "v")
            nvp = tick("v")
            g.wait_ge(sem["v"], nvp)
            if dcc < 4:
                inc(g.dma_start(out=ckvT_d[:, dcc * TPC:(dcc + 1) * TPC], in_=cvq[:]),
                    "dz", 16)
            else:
                inc(g.dma_start(out=cqT_d[:, (dcc - 4) * TPC:(dcc - 3) * TPC],
                                in_=cvq[:]), "dz", 16)
        if upto <= 5:
            y.wait_ge(sem["dz"], tick("dz")); y.wait_ge(sem["dy"], tick("dy"))
            return ctx.finalize() if own else None if own else None
        g.wait_ge(sem["t"], tick("t"))
        g.wait_ge(sem["v"], tick("v"))
        nwkr = inc(g.dma_start(out=bufB[:], in_=wkrT_d[:, :]), "dg", 16)
        lbk = inc(g.dma_start(out=rsrep, in_=bkr_rep_d[:, :]), "dg", 16)
        ropeC = owdm[:].bitcast(F32)[:, 0:128]
        ropeS = owdm[:].bitcast(F32)[:, 128:256]
        bkrrep = rsrep
        krtok = scrA
        krsw = scrB
        krTb = scrD
        kr_ev = tick("v")
        t.wait_ge(sem["dg"], lbk)
        for tc in range(16):
            trow = tc * 128
            g.wait_ge(sem["v"], tick("v"))
            lr1 = inc(g.dma_start(out=ropeC, in_=ropeC_d[trow:trow + 128, :]), "dg", 16)
            lr2 = inc(g.dma_start(out=ropeS, in_=ropeS_d[trow:trow + 128, :]), "dg", 16)
            t.wait_ge(sem["dg"], lr2)
            t.wait_ge(sem["v"], kr_ev)
            mm = None
            for ot in range(4):
                for fc in range(16):
                    mm = t.matmul(psC[:, ot * 512:(ot + 1) * 512],
                                  bufA[:, fc * TPC + trow: fc * TPC + trow + 128],
                                  bufB[:, fc * TPC + ot * 512: fc * TPC + (ot + 1) * 512],
                                  start=(fc == 0), stop=(fc == 15))
            n_kr = inc(mm, "t")
            v.wait_ge(sem["t"], n_kr)
            v.wait_ge(sem["dz"], tick("dz"))
            kr_ev = inc(v.tensor_tensor(krtok[:], psC[:], bkrrep, ALU.add), "v")
            v.wait_ge(sem["v"], kr_ev)
            v.tensor_scalar(krsw[:, 0:D:2], krtok[:, 1:D:2], -1.0, None, ALU.mult)
            n_v = inc(v.tensor_copy(krsw[:, 1:D:2], krtok[:, 0:D:2]), "v")
            v.wait_ge(sem["v"], n_v)
            last = None
            for hh in range(16):
                sl = slice(hh * 128, (hh + 1) * 128)
                v.tensor_tensor(krtok[:, sl], krtok[:, sl], ropeC, ALU.mult)
                last = v.tensor_tensor(krsw[:, sl], krsw[:, sl], ropeS, ALU.mult)
            n_v = inc(last, "v")
            v.wait_ge(sem["v"], n_v)
            nv_rope = inc(v.tensor_tensor(krtok[:], krtok[:], krsw[:], ALU.add), "v")
            t.wait_ge(sem["v"], nv_rope)
            evts = [nv_rope, nv_rope]
            for fc in range(16):
                pb = (psA if fc % 2 == 0 else psB)[:, 0:128]
                t.wait_ge(sem["v"], evts[fc % 2])
                n_t = inc(t.transpose(pb, krtok[:, fc * 128:(fc + 1) * 128], ident[:]),
                          "t")
                v.wait_ge(sem["t"], n_t)
                evts[fc % 2] = inc(
                    v.tensor_copy(krTb[:, fc * 128:(fc + 1) * 128], pb), "v")
            nv_kr = tick("v")
            y.wait_ge(sem["v"], nv_kr)
            inc(y.dma_start(
                out=krT_d.rearrange("p (f t) -> p f t", t=TPC)[:, :, trow:trow + 128],
                in_=krTb[:].rearrange("p (f t) -> p f t", t=128)), "dz", 16)
        y.wait_ge(sem["dz"], tick("dz"))
        y.wait_ge(sem["dy"], tick("dy"))
        return ctx.finalize() if own else None



def build_C(ctx=None):
    own = ctx is None
    with (Ctx("C") if own else ctx) as ctx:
        ckvT_d = ctx.din("ckvT", [128, 4 * S], BF16)
        cqT_d = ctx.din("cqT", [128, 8 * S], BF16)
        krT_d = ctx.din("krT", [128, 8 * S], BF16)
        wukT_d = ctx.din("wukT", [128, 4 * 8 * 128], F32)
        wuvT_d = ctx.din("wuvT", [128, 4 * 8 * 128], F32)
        wuqT_d = ctx.din("wuqT", [128, 8 * 8 * 128], F32)
        buk_d = ctx.din("bukcol", [128, 8], F32)
        buvrep_d = ctx.din("buvrep", [128, 8 * 128], F32)
        buq_d = ctx.din("buqcol", [128, 8], F32)
        ropeCF_d = ctx.din("ropeCF", [S, DH], F32)
        ropeSF_d = ctx.din("ropeSF", [S, DH], F32)
        dmask_d = ctx.din("dmask", [4 * 128, 512], F32)
        identin = ctx.din("identin", [128, 128], F32)
        selcol_d = ctx.din("selcol", [128, 2], F32)
        oT_d = ctx.dout("oT", [128, 8 * S], BF16)

        ctx.alloc_sems(["t", "v", "s", "dy", "dg", "dz"])
        t, v, s, g, y = ctx.t, ctx.v, ctx.s, ctx.g, ctx.y
        sem, inc, tick = ctx.sem, ctx.inc, ctx.tick

        ckv = ctx.sb("ckv", [128, 4 * S], BF16)
        cq = ctx.sb("cq", [128, 8 * S], BF16)
        kC = ctx.sb("kC", [128, S], BF16)
        kR = ctx.sb("kR", [128, S], BF16)
        qC = ctx.sb("qC", [128, S], BF16)
        qR = ctx.sb("qR", [128, S], BF16)
        vsb = ctx.sb("vsb", [128, S], BF16)
        oT = ctx.sb("oT_sb", [128, S], BF16)
        wuk = ctx.sb("wuk", [128, 4 * 8 * 128], BF16)
        wuv = ctx.sb("wuv", [128, 4 * 8 * 128], BF16)
        wuq = ctx.sb("wuq", [128, 8 * 8 * 128], BF16)
        bukc = ctx.sb("bukc", [128, 8], F32)
        buvrep = ctx.sb("buvrep_sb", [128, 8 * 128], F32)
        buqc = ctx.sb("buqc", [128, 8], F32)
        dmask = ctx.sb("dmask_sb", [128, 4 * 512], BF16)
        ident = ctx.sb("identsb", [128, 128], F32)
        identb = ctx.sb("identbsb", [128, 128], BF16)
        ropeC = ctx.sb("ropeCsb", [128, DH], F32)
        ropeS = ctx.sb("ropeSsb", [128, DH], F32)
        qtok = ctx.sb("qtok", [128, 128], BF16)
        qsw = ctx.sb("qsw", [128, 128], F32)
        qrt = ctx.sb("qrt", [128, 128], BF16)
        ptile = [ctx.sb(f"ptile{i}", [128, 512], BF16) for i in range(3)]
        onesb = ctx.sb("onesbc", [128, 1], BF16)
        ones1f = ctx.sb("ones1fc", [1, 128], F32)
        sqrow = ctx.sb("sqrow", [128, 512], BF16)
        nrow = ctx.sb("nrow", [1, 512], F32)
        sc1 = ctx.sb("sc1", [1, 8], F32)
        selcolC = ctx.sb("selcolC", [128, 2], F32)
        mcol = ctx.sb("mcol", [128, 1], F32)
        zrow = ctx.sb("zrow", [1, 512], F32)
        zrep = ctx.sb("zrep", [128, 512], F32)

        psS = ctx.ps("psS", [128, 1024], F32)
        psO = ctx.ps("psO", [128, 512], F32)
        psK = ctx.ps("psK", [128, 512], F32)
        psZ = ctx.ps("psZ", [1, 512], F32)
        psD = ctx.ps("psD", [128, 256], BF16)

        SCALE = float(1.0 / np.sqrt(np.float32(2 * DH)).astype(np.float32))

        FUSED_C = ckvT_d.shape[0] == 256
        if FUSED_C:
            for nch, dst, srcd in ((4, ckv, ckvT_d), (8, cq, cqT_d)):
                inc(y.dma_start(
                    out=dst[:].rearrange("p (c t) -> p c t", t=S)[:, :, 0:TPC],
                    in_=srcd[0:128, :].rearrange("p (c t) -> p c t", t=TPC)), "dy", 16)
                inc(y.dma_start(
                    out=dst[:].rearrange("p (c t) -> p c t", t=S)[:, :, TPC:S],
                    in_=srcd[128:256, :].rearrange("p (c t) -> p c t", t=TPC)), "dy", 16)
        else:
            inc(y.dma_start(out=ckv[:], in_=ckvT_d[:, :]), "dy", 16)
            inc(y.dma_start(out=cq[:], in_=cqT_d[:, :]), "dy", 16)
        inc(g.dma_start(out=selcolC[:], in_=selcol_d[:, :]), "dg", 16)
        for ap_sb, ap_d in ((wuk, wukT_d), (wuv, wuvT_d), (wuq, wuqT_d),
                            (ident, identin)):
            inc(g.dma_start(out=ap_sb[:], in_=ap_d[:, :]), "dg", 16)
        inc(g.dma_start(out=dmask[:].rearrange("k (o q) -> k o q", q=512),
                        in_=dmask_d.rearrange("(o k) q -> k o q", k=128)), "dg", 16)
        for ap_sb, ap_d in ((bukc, buk_d), (buvrep, buvrep_d), (buqc, buq_d)):
            inc(g.dma_start(out=ap_sb[:], in_=ap_d[:, :]), "dg", 16)
        ld0 = tick("dg")
        ldy0 = tick("dy")
        v.wait_ge(sem["dg"], ld0)
        v.wait_ge(sem["dy"], ldy0)
        v.memset(onesb[:], 1.0)
        v.memset(ones1f[:], 1.0)
        nv0 = inc(v.tensor_copy(identb[:], ident[:]), "v")

        for hl in range(8):
            t_prev = tick("t")
            g.wait_ge(sem["t"], t_prev)
            if FUSED_C:
                g.wait_ge(sem["v"], tick("v"))
                for cand, dst in ((hl, kR), (8 + hl, qR)):
                    inc(g.dma_start(out=dst[:, 0:TPC],
                                    in_=krT_d[0:128, cand * TPC:(cand + 1) * TPC]),
                        "dg", 16)
                    inc(g.dma_start(out=dst[:, TPC:S],
                                    in_=krT_d[128:256, cand * TPC:(cand + 1) * TPC]),
                        "dg", 16)
                krl0 = tick("dg")
                v.wait_ge(sem["dg"], krl0)
                v.tensor_scalar(kR[:], kR[:], selcolC[:, 0:1], None, ALU.mult)
                nbl = inc(v.scalar_tensor_tensor(kR[:], qR[:], selcolC[:, 1:2], kR[:],
                                                 ALU.mult, ALU.add), "v")
                t.wait_ge(sem["v"], nbl)
                krl = 0
            else:
                krl = inc(g.dma_start(out=kR[:], in_=krT_d[:, hl * S:(hl + 1) * S]), "dg", 16)
            evk = [tick("v")] * 2
            for tt in range(8):
                pb = psS[:, (tt % 2) * 512:(tt % 2) * 512 + 512]
                t.wait_ge(sem["v"], evk[tt % 2])
                if hl == 0 and tt == 0:
                    t.wait_ge(sem["v"], nv0)
                mm = None
                for dcc in range(4):
                    mm = t.matmul(pb, wuk[:, (dcc * 8 + hl) * 128:(dcc * 8 + hl + 1) * 128],
                                  ckv[:, dcc * S + tt * 512: dcc * S + (tt + 1) * 512],
                                  start=(dcc == 0), stop=(dcc == 3))
                n_t = inc(mm, "t")
                v.wait_ge(sem["t"], n_t)
                evk[tt % 2] = inc(v.tensor_scalar(kC[:, tt * 512:(tt + 1) * 512], pb,
                                                  bukc[:, hl:hl + 1], None, ALU.add), "v")
            for tt in range(8):
                pb = psS[:, (tt % 2) * 512:(tt % 2) * 512 + 512]
                t.wait_ge(sem["v"], evk[tt % 2])
                mm = None
                for dcc in range(8):
                    mm = t.matmul(pb, wuq[:, (dcc * 8 + hl) * 128:(dcc * 8 + hl + 1) * 128],
                                  cq[:, dcc * S + tt * 512: dcc * S + (tt + 1) * 512],
                                  start=(dcc == 0), stop=(dcc == 7))
                n_t = inc(mm, "t")
                v.wait_ge(sem["t"], n_t)
                evk[tt % 2] = inc(v.tensor_scalar(qC[:, tt * 512:(tt + 1) * 512], pb,
                                                  buqc[:, hl:hl + 1], None, ALU.add), "v")
            evv = tick("v")
            for tc in range(32):
                t.wait_ge(sem["v"], evv)
                mm = None
                for dcc in range(4):
                    mm = t.matmul(psK[:, 0:128],
                                  ckv[:, dcc * S + tc * 128: dcc * S + (tc + 1) * 128],
                                  wuv[:, (dcc * 8 + hl) * 128:(dcc * 8 + hl + 1) * 128],
                                  start=(dcc == 0), stop=(dcc == 3))
                n_t = inc(mm, "t")
                v.wait_ge(sem["t"], n_t)
                evv = inc(v.tensor_tensor(vsb[:, tc * 128:(tc + 1) * 128], psK[:, 0:128],
                                          buvrep[:, hl * 128:(hl + 1) * 128], ALU.add),
                          "v")
            ev_q = tick("v")
            for tc in range(32):
                g.wait_ge(sem["v"], ev_q)
                lr1 = inc(g.dma_start(out=ropeC[:], in_=ropeCF_d[tc * 128:(tc + 1) * 128, :]), "dg", 16)
                lr2 = inc(g.dma_start(out=ropeS[:], in_=ropeSF_d[tc * 128:(tc + 1) * 128, :]), "dg", 16)
                t.wait_ge(sem["v"], ev_q)
                n_t = inc(t.transpose(psD[:, 0:128], qC[:, tc * 128:(tc + 1) * 128],
                                      identb[:]), "t")
                v.wait_ge(sem["t"], n_t)
                nv = inc(v.tensor_copy(qtok[:], psD[:, 0:128]), "v")
                v.wait_ge(sem["v"], nv)
                v.wait_ge(sem["dg"], lr2)
                v.tensor_scalar(qsw[:, 0:128:2], qtok[:, 1:128:2], -1.0, None, ALU.mult)
                nv = inc(v.tensor_copy(qsw[:, 1:128:2], qtok[:, 0:128:2]), "v")
                v.wait_ge(sem["v"], nv)
                v.tensor_tensor(qsw[:], qsw[:], ropeS[:], ALU.mult)
                nv = inc(v.scalar_tensor_tensor(qrt[:], qtok[:], 1.0, ropeC[:],
                                                ALU.mult, ALU.mult), "v")
                v.wait_ge(sem["v"], nv)
                nv = inc(v.tensor_tensor(qrt[:], qrt[:], qsw[:], ALU.add), "v")
                t.wait_ge(sem["v"], nv)
                n_t = inc(t.transpose(psD[:, 128:256], qrt[:], identb[:]), "t")
                v.wait_ge(sem["t"], n_t)
                ev_q = inc(v.tensor_copy(qR[:, tc * 128:(tc + 1) * 128], psD[:, 128:256]),
                           "v")
            nrm_parts = [(qC, qR, 0), (kC, kR, 1)]
            for (Ca, Ra, idx) in nrm_parts:
                for tt in range(8):
                    sl = slice(tt * 512, (tt + 1) * 512)
                    ns = inc(s.activation(sqrow[:], Ca[:, sl], AF.Square), "s")
                    t.wait_ge(sem["s"], ns)
                    if tt == 0:
                        t.wait_ge(sem["v"], tick("v"))
                    n_t = inc(t.matmul(psZ[:], onesb[:], sqrow[:],
                                       start=True, stop=False), "t")
                    s.wait_ge(sem["t"], n_t)
                    ns = inc(s.activation(sqrow[:], Ra[:, sl], AF.Square), "s")
                    t.wait_ge(sem["s"], ns)
                    n_t = inc(t.matmul(psZ[:], onesb[:], sqrow[:],
                                       start=False, stop=True), "t")
                    s.wait_ge(sem["t"], n_t)
                    v.wait_ge(sem["t"], n_t)
                    nv = inc(v.tensor_copy(nrow[:], psZ[:]), "v")
                    v.wait_ge(sem["v"], nv)
                    nv = inc(v.reduce_max(sc1[:, 4 + idx:5 + idx], nrow[:], axis=AX), "v")
                    v.wait_ge(sem["v"], nv)
                    if tt == 0:
                        nv = inc(v.tensor_copy(sc1[:, idx:idx + 1],
                                               sc1[:, 4 + idx:5 + idx]), "v")
                    else:
                        nv = inc(v.tensor_tensor(sc1[:, idx:idx + 1],
                                                 sc1[:, idx:idx + 1],
                                                 sc1[:, 4 + idx:5 + idx], ALU.max), "v")
                    v.wait_ge(sem["v"], nv)
                    t.wait_ge(sem["v"], nv)
            s.wait_ge(sem["v"], tick("v"))
            s.activation(sc1[:, 0:1], sc1[:, 0:1], AF.Ln, scale=1.0)
            ns = inc(s.activation(sc1[:, 1:2], sc1[:, 1:2], AF.Ln, scale=1.0), "s")
            v.wait_ge(sem["s"], ns)
            nv = inc(v.tensor_add(sc1[:, 2:3], sc1[:, 0:1], sc1[:, 1:2]), "v")
            s.wait_ge(sem["v"], nv)
            ns = inc(s.activation(sc1[:, 2:3], sc1[:, 2:3], AF.Exp, scale=0.5), "s")
            v.wait_ge(sem["s"], ns)
            nv = inc(v.tensor_scalar(sc1[:, 3:4], sc1[:, 2:3], -SCALE, None, ALU.mult),
                     "v")
            t.wait_ge(sem["v"], nv)
            n_t = inc(t.matmul(psK[:, 0:1], ones1f[:, 0:128], sc1[:, 3:4],
                               start=True, stop=True), "t")
            v.wait_ge(sem["t"], n_t)
            nv_m = inc(v.tensor_copy(mcol[:], psK[:, 0:1]), "v")

            t.wait_ge(sem["dg"], krl)
            ev_o = tick("v")
            for qt in range(8):
                nk = 4 * (qt + 1)
                pidx = 0
                pt_free = [tick("t")] * 3
                for kc in range(nk):
                    half = kc % 2
                    pbS = psS[:, half * 512: half * 512 + 512]
                    t.wait_ge(sem["s"], max(0, tick("s") - 1))
                    mm = t.matmul(pbS, kC[:, kc * 128:(kc + 1) * 128],
                                  qC[:, qt * 512:(qt + 1) * 512], start=True, stop=False)
                    mm = t.matmul(pbS, kR[:, kc * 128:(kc + 1) * 128],
                                  qR[:, qt * 512:(qt + 1) * 512], start=False, stop=True)
                    n_S = inc(mm, "t")
                    pt = ptile[pidx]
                    s.wait_ge(sem["t"], n_S)
                    s.wait_ge(sem["t"], pt_free[pidx])
                    if kc == 0 and qt == 0:
                        s.wait_ge(sem["v"], nv_m)
                    n_e = inc(s.activation(pt[:], pbS, AF.Exp, bias=mcol[:],
                                           scale=SCALE), "s")
                    off = kc - 4 * qt
                    if off >= 0:
                        v.wait_ge(sem["s"], n_e)
                        n_e_v = inc(v.tensor_tensor(pt[:], pt[:],
                                                    dmask[:, off * 512:(off + 1) * 512],
                                                    ALU.mult), "v")
                        t.wait_ge(sem["v"], n_e_v)
                    else:
                        t.wait_ge(sem["s"], n_e)
                    mm = t.matmul(psZ[:], onesb[:], pt[:],
                                  start=(kc == 0), stop=(kc == nk - 1))
                    mm = t.matmul(psO[:], vsb[:, kc * 128:(kc + 1) * 128], pt[:],
                                  start=(kc == 0), stop=(kc == nk - 1))
                    pt_free[pidx] = inc(mm, "t")
                    pidx = (pidx + 1) % 3
                n_fl = tick("t")
                v.wait_ge(sem["t"], n_fl)
                nv = inc(v.tensor_copy(zrow[:], psZ[:]), "v")
                v.wait_ge(sem["v"], nv)
                nv = inc(v.reciprocal(zrow[:], zrow[:]), "v")
                t.wait_ge(sem["v"], nv)
                n_t = inc(t.matmul(psK[:, 0:512], ones1f[:], zrow[:],
                                   start=True, stop=True), "t")
                v.wait_ge(sem["t"], n_t)
                nv = inc(v.tensor_copy(zrep[:], psK[:, 0:512]), "v")
                v.wait_ge(sem["v"], nv)
                ev_o = inc(v.tensor_tensor(oT[:, qt * 512:(qt + 1) * 512], psO[:],
                                           zrep[:], ALU.mult), "v")
                t.wait_ge(sem["v"], ev_o)
            y.wait_ge(sem["v"], ev_o)
            if oT_d.shape[0] == 256:
                inc(y.dma_start(out=oT_d[0:128, hl * TPC:(hl + 1) * TPC],
                                in_=oT[:, 0:TPC]), "dz", 16)
                inc(y.dma_start(out=oT_d[128:256, hl * TPC:(hl + 1) * TPC],
                                in_=oT[:, TPC:S]), "dz", 16)
            else:
                inc(y.dma_start(out=oT_d[:, hl * S:(hl + 1) * S], in_=oT[:]), "dz", 16)
            v.wait_ge(sem["dz"], tick("dz"))
        y.wait_ge(sem["dz"], tick("dz"))
        y.wait_ge(sem["dy"], tick("dy"))
        return ctx.finalize() if own else None



def build_D(ctx=None):
    NT4 = 4
    NIC = 11
    own = ctx is None
    with (Ctx("D") if own else ctx) as ctx:
        x1_d = ctx.din("x1", [TPC, D], BF16)
        xn1T_d = ctx.din("xn1T", [128, 16 * TPC], BF16)
        oT_d = ctx.din("oT", [128, 16 * TPC], BF16)
        cwrows_d = ctx.din("cwrows", [8, TPC], F32)
        wOT_d = ctx.din("wOT", [128, 16 * 16 * 128], F32)
        bOrow_d = ctx.din("bOrow", [1, D], F32)
        w1all_d = ctx.din("w1all", [128, NE * 16 * NIC * 128], BF16)
        w3all_d = ctx.din("w3all", [128, NE * 16 * NIC * 128], BF16)
        w2all_d = ctx.din("w2all", [128, NE * NIC * 16 * 128], BF16)
        sw1_d = ctx.din("sw1", [128, 16 * NIC * 128], BF16)
        sw3_d = ctx.din("sw3", [128, 16 * NIC * 128], BF16)
        sw2_d = ctx.din("sw2", [128, NIC * 16 * 128], BF16)
        selcol_d = ctx.din("selcol", [128, 2], F32)
        out_d = ctx.dout("out", [TPC, D], F32)

        wO_scr = ctx.nc.dram_tensor("wO_scr", [128, 16 * 16 * 128], BF16).ap()

        ctx.alloc_sems(["t", "v", "s", "dy", "dg", "dz"])
        t, v, s, g, y = ctx.t, ctx.v, ctx.s, ctx.g, ctx.y
        sem, inc, tick = ctx.sem, ctx.inc, ctx.tick

        xn1T = ctx.sb("xn1Tsb", [128, 16 * TPC], BF16)
        ysb = ctx.sb("ysb", [128, 16 * TPC], BF16)
        hsb_e = ctx.sb("hsb_e", [128, NIC * TPC], BF16)
        wst = [ctx.sb("wstD0", [128, 2 * 16 * 128], BF16)]
        wmov = ctx.sb("wmovD", [128, D], BF16)
        otile = hsb_e[:, 0:2048]
        x1t = hsb_e[:, 0:2048]
        outf = hsb_e[:].bitcast(F32)[:, 2048:4096]
        cwrep = ctx.sb("cwrepD", [128, TPC], BF16)
        silu_g = ctx.sb("silugD", [128, 512], F32)
        rowD = ctx.sb("rowD", [1, D], F32)
        bOrow = rowD
        cw1 = rowD
        ones1f = ctx.sb("ones1fD", [1, 128], F32)
        selcolD = ctx.sb("selcolD", [128, 2], F32)

        psA = ctx.ps("psAD", [128, 1024], F32)
        psB = ctx.ps("psBD", [128, 1024], F32)
        psC = ctx.ps("psCD", [128, 2048], F32)

        FUSED_D = oT_d.shape[0] == 512
        inc(y.dma_start(out=xn1T[:], in_=xn1T_d[:, :]), "dy", 16)
        inc(g.dma_start(out=bOrow[:], in_=bOrow_d[:, :]), "dg", 16)
        inc(g.dma_start(out=selcolD[:], in_=selcol_d[:, :]), "dg", 16)
        ldc = tick("dg")
        v.wait_ge(sem["dg"], ldc)
        nv0 = inc(v.memset(ones1f[:], 1.0), "v")
        for fc in range(16):
            dst = wst[0][:, 0:2048]
            g.wait_ge(sem["dz"], max(0, fc) * 16)
            ld = inc(g.dma_start(out=dst, in_=wOT_d[:, fc * 2048:(fc + 1) * 2048]),
                     "dg", 16)
            y.wait_ge(sem["dg"], ld)
            inc(y.dma_start(out=wO_scr[:, fc * 2048:(fc + 1) * 2048], in_=dst), "dz", 16)
        wo_ready = tick("dz")
        xn1_ready = 16

        ev_y = nv0
        for tc in range(16):
            trow = tc * 128
            y.wait_ge(sem["t"], tick("t"))
            if FUSED_D:
                osc = hsb_e[:, 2048:4096]
                y.wait_ge(sem["v"], tick("v"))
                inc(y.dma_start(
                    out=otile[:].rearrange("p (f c) -> p f c", c=128)[:, 0:8, :],
                    in_=oT_d[0:128, :].rearrange("p (f t) -> p f t", t=TPC)[:, :, trow:trow + 128]), "dy", 16)
                inc(y.dma_start(
                    out=otile[:].rearrange("p (f c) -> p f c", c=128)[:, 8:16, :],
                    in_=oT_d[256:384, :].rearrange("p (f t) -> p f t", t=TPC)[:, :, trow:trow + 128]), "dy", 16)
                inc(y.dma_start(
                    out=osc.rearrange("p (f c) -> p f c", c=128)[:, 0:8, :],
                    in_=oT_d[128:256, :].rearrange("p (f t) -> p f t", t=TPC)[:, :, trow:trow + 128]), "dy", 16)
                lo4 = inc(y.dma_start(
                    out=osc.rearrange("p (f c) -> p f c", c=128)[:, 8:16, :],
                    in_=oT_d[384:512, :].rearrange("p (f t) -> p f t", t=TPC)[:, :, trow:trow + 128]),
                    "dy", 16)
                lo = lo4
                v.wait_ge(sem["dy"], lo4)
                v.tensor_scalar(otile, otile, selcolD[:, 0:1], None, ALU.mult)
                nbl = inc(v.scalar_tensor_tensor(otile, osc, selcolD[:, 1:2], otile,
                                                 ALU.mult, ALU.add), "v")
                t.wait_ge(sem["v"], nbl)
            else:
                lo = inc(y.dma_start(
                    out=otile[:].rearrange("p (f c) -> p f c", c=128),
                    in_=oT_d.rearrange("p (f t) -> p f t", t=TPC)[:, :, trow:trow + 128]),
                    "dy", 16)
            mm = None
            for fc in range(16):
                g.wait_ge(sem["t"], tick("t"))
                if fc == 0:
                    g.wait_ge(sem["dz"], wo_ready)
                lw = inc(g.dma_start(out=wmov[:], in_=wO_scr[:, fc * 2048:(fc + 1) * 2048]),
                         "dg", 16)
                t.wait_ge(sem["dg"], lw)
                t.wait_ge(sem["dy"], lo)
                if fc == 0:
                    t.wait_ge(sem["v"], ev_y)
                for ot in range(4):
                    mm = t.matmul(psC[:, ot * 512:(ot + 1) * 512],
                                  otile[:, fc * 128:(fc + 1) * 128],
                                  wmov[:, ot * 512:(ot + 1) * 512],
                                  start=(fc == 0), stop=False)
                inc(mm, "t")
            for ot in range(4):
                mm = t.matmul(psC[:, ot * 512:(ot + 1) * 512], ones1f[:],
                              bOrow[:, ot * 512:(ot + 1) * 512],
                              start=False, stop=True)
            n_t = inc(mm, "t")
            v.wait_ge(sem["t"], n_t)
            ev_y = inc(v.tensor_copy(ysb[:, tc * D:(tc + 1) * D], psC[:]), "v")

        for e in range(NE + 1):
            shared = (e == NE)
            if shared:
                w1src, w3src, w2src = sw1_d, sw3_d, sw2_d
                eoff1 = 0
                eoff2 = 0
            else:
                w1src, w3src, w2src = w1all_d, w3all_d, w2all_d
                eoff1 = e * 16 * NIC * 128
                eoff2 = e * NIC * 16 * 128
            if not shared:
                g.wait_ge(sem["v"], tick("v"))
                g.wait_ge(sem["t"], tick("t"))
                lcw = inc(g.dma_start(out=cw1[:], in_=cwrows_d[e:e + 1, :]), "dg", 16)
                t.wait_ge(sem["dg"], lcw)
                for qt in range(4):
                    t.wait_ge(sem["v"], tick("v"))
                    n_t = inc(t.matmul(psB[:, 0:512], ones1f[:],
                                       cw1[:, qt * 512:(qt + 1) * 512],
                                       start=True, stop=True), "t")
                    v.wait_ge(sem["t"], n_t)
                    inc(v.tensor_copy(cwrep[:, qt * 512:(qt + 1) * 512], psB[:, 0:512]),
                        "v")
            cw_ready = tick("v")
            wst_last = [0]
            for ic in range(NIC):
                wb = wst[0]
                g.wait_ge(sem["t"], wst_last[0])
                l1 = inc(g.dma_start(
                    out=wb[:, 0:2048].rearrange("p (f o) -> p f o", o=128),
                    in_=w1src.rearrange("p (f o) -> p f o", o=128)[:, eoff1 // 128 + ic::NIC, :]
                    if False else
                    w1src.rearrange("p (f o) -> p f o", o=128)[:, eoff1 // 128 + ic: eoff1 // 128 + 16 * NIC: NIC, :]),
                    "dg", 16)
                l2 = inc(g.dma_start(
                    out=wb[:, 2048:4096].rearrange("p (f o) -> p f o", o=128),
                    in_=w3src.rearrange("p (f o) -> p f o", o=128)[:, eoff1 // 128 + ic: eoff1 // 128 + 16 * NIC: NIC, :]),
                    "dg", 16)
                t.wait_ge(sem["dg"], l2)
                for tt in range(NT4):
                    pg = psA[:, 0:512]
                    pu = psA[:, 512:1024]
                    t.wait_ge(sem["s"], tick("s"))
                    t.wait_ge(sem["v"], tick("v"))
                    if e == 0 and ic == 0 and tt == 0:
                        t.wait_ge(sem["dy"], 16)
                    mm = None
                    for fc in range(16):
                        mm = t.matmul(pg, wb[:, fc * 128:(fc + 1) * 128],
                                      xn1T[:, fc * TPC + tt * 512: fc * TPC + (tt + 1) * 512],
                                      start=(fc == 0), stop=(fc == 15))
                    n_gm = inc(mm, "t")
                    mm = None
                    for fc in range(16):
                        mm = t.matmul(pu, wb[:, 2048 + fc * 128:2048 + (fc + 1) * 128],
                                      xn1T[:, fc * TPC + tt * 512: fc * TPC + (tt + 1) * 512],
                                      start=(fc == 0), stop=(fc == 15))
                    n_um = inc(mm, "t")
                    s.wait_ge(sem["t"], n_gm)
                    n_sg = inc(s.activation(silu_g[:], pg, AF.Silu), "s")
                    v.wait_ge(sem["t"], n_um)
                    v.wait_ge(sem["s"], n_sg)
                    if not shared:
                        nv = inc(v.tensor_tensor(silu_g[:], silu_g[:],
                                                 cwrep[:, tt * 512:(tt + 1) * 512],
                                                 ALU.mult), "v")
                        v.wait_ge(sem["v"], nv)
                    inc(v.tensor_tensor(
                        hsb_e[:, ic * TPC + tt * 512: ic * TPC + (tt + 1) * 512],
                        pu, silu_g[:], ALU.mult), "v")
                wst_last[0] = tick("t")
            nv_h = tick("v")
            ev_yb = 0
            for tc in range(16):
                trow = tc * 128
                mm = None
                for ic in range(NIC):
                    g.wait_ge(sem["t"], tick("t"))
                    lw = inc(g.dma_start(
                        out=wmov[:],
                        in_=w2src[:, eoff2 + ic * 2048: eoff2 + (ic + 1) * 2048]),
                        "dg", 16)
                    t.wait_ge(sem["dg"], lw)
                    if ic == 0:
                        t.wait_ge(sem["v"], max(nv_h, ev_yb))
                    for ot in range(4):
                        mm = t.matmul(psC[:, ot * 512:(ot + 1) * 512],
                                      hsb_e[:, ic * TPC + trow: ic * TPC + trow + 128],
                                      wmov[:, ot * 512:(ot + 1) * 512],
                                      start=(ic == 0), stop=(ic == NIC - 1))
                    inc(mm, "t")
                n_t = tick("t")
                v.wait_ge(sem["t"], n_t)
                ev_yb = inc(v.tensor_tensor(ysb[:, tc * D:(tc + 1) * D],
                                            ysb[:, tc * D:(tc + 1) * D], psC[:],
                                            ALU.add), "v")

        ev_fin = tick("v")
        for tc in range(16):
            trow = tc * 128
            y.wait_ge(sem["v"], ev_fin)
            lx = inc(y.dma_start(out=x1t[:], in_=x1_d[trow:trow + 128, :]), "dy", 16)
            v.wait_ge(sem["dy"], lx)
            v.wait_ge(sem["dz"], tick("dz"))
            nv = inc(v.tensor_tensor(outf[:], ysb[:, tc * D:(tc + 1) * D], x1t[:],
                                     ALU.add), "v")
            y.wait_ge(sem["v"], nv)
            inc(y.dma_start(out=out_d[trow:trow + 128, :], in_=outf[:]), "dz", 16)
            ev_fin = nv
        y.wait_ge(sem["dz"], tick("dz"))
        y.wait_ge(sem["dy"], tick("dy"))
        return ctx.finalize() if own else None



def _barrier_all(ctx):
    for eng in (ctx.t, ctx.v, ctx.s, ctx.g, ctx.y):
        for k in list(ctx.sem):
            vv = ctx.tick(k)
            if vv:
                eng.wait_ge(ctx.sem[k], vv)


def _wait_all(ctx, key, val):
    for eng in (ctx.t, ctx.v, ctx.s, ctx.g, ctx.y):
        eng.wait_ge(ctx.sem[key], val)


def build_fused():
    ctx = Ctx("FUSED")
    nc = ctx.nc
    ctx.alloc_sems(["t", "v", "s", "dy", "dg", "dz", "c1", "c2", "c3", "c4"])
    rg = [[0, 1], [2, 3], [4, 5], [6, 7]]
    BF_ = BF16
    ag1i = nc.dram_tensor("ag1i", [128, 16 * TPC], BF_).ap()
    ag1o = nc.dram_tensor("ag1o", [256, 16 * TPC], BF_).ap()
    ag2i = nc.dram_tensor("ag2i", [256, 8 * TPC], BF_).ap()
    ag2o = nc.dram_tensor("ag2o", [512, 8 * TPC], BF_).ap()
    ag3i = nc.dram_tensor("ag3i", [128, 28 * TPC], BF_).ap()
    ag3o = nc.dram_tensor("ag3o", [256, 28 * TPC], BF_).ap()
    ag4i = nc.dram_tensor("ag4i", [256, 8 * TPC], BF_).ap()
    ag4o = nc.dram_tensor("ag4o", [512, 8 * TPC], BF_).ap()
    x1_i = nc.dram_tensor("x1_i", [TPC, D], BF_).ap()
    xn1T_i = nc.dram_tensor("xn1T_i", [128, 16 * TPC], BF_).ap()
    cw_i = nc.dram_tensor("cw_i", [8, TPC], F32).ap()

    def agcall(key, ain, aout):
        ctx.g.wait_ge(ctx.sem["dz"], ctx.tick("dz"))
        ctx.g.wait_ge(ctx.sem["dy"], ctx.tick("dy"))
        ctx.inc(ctx.g.collective_compute(
            "AllGather", ALU.bypass, replica_groups=rg,
            ins=[ain.opt()], outs=[aout.opt()]), key, 1)
        _wait_all(ctx, key, ctx.tick(key))

    ctx.prefix = "a0_"; ctx.io = {"xn0T": ag1i}
    m = ctx.mark(); build_A0(ctx); ctx.release(m); _barrier_all(ctx)
    agcall("c1", ag1i, ag1o)
    ctx.prefix = "a1_"; ctx.io = {"xn0T": ag1o, "attnT": ag2i}
    m = ctx.mark(); build_A1(ctx); ctx.release(m); _barrier_all(ctx)
    agcall("c2", ag2i, ag2o)
    ctx.prefix = "b_"
    ctx.io = {"attnT": ag2o, "xn0T": ag1i, "x1": x1_i, "xn1T": xn1T_i, "cwrows": cw_i,
              "ckvT": ag3i[:, 0:4 * TPC], "cqT": ag3i[:, 4 * TPC:12 * TPC],
              "krT": ag3i[:, 12 * TPC:28 * TPC]}
    m = ctx.mark(); build_B(ctx); ctx.release(m); _barrier_all(ctx)
    agcall("c3", ag3i, ag3o)
    ctx.prefix = "c_"
    ctx.io = {"ckvT": ag3o[:, 0:4 * TPC], "cqT": ag3o[:, 4 * TPC:12 * TPC],
              "krT": ag3o[:, 12 * TPC:28 * TPC], "oT": ag4i}
    m = ctx.mark(); build_C(ctx); ctx.release(m); _barrier_all(ctx)
    agcall("c4", ag4i, ag4o)
    ctx.prefix = "d_"
    ctx.io = {"oT": ag4o, "x1": x1_i, "xn1T": xn1T_i, "cwrows": cw_i}
    m = ctx.mark(); build_D(ctx); ctx.release(m)
    ctx.prefix = ""
    ctx.io = {}
    return ctx.finalize()


def run_fused(inputs):
    con = host_tables()
    ncF = _prog("FUSED", build_fused)
    x = np.ascontiguousarray(
        np.asarray(inputs["L0_attn_hidden_states"]).reshape(B * S, D))
    qkv_w = inputs["L0_attn_qkv_w"]
    norm_w = np.asarray(inputs["L0_attn_norm_w"])
    wg8 = _u8view(np.ascontiguousarray(inputs["L0_ffn_gate_w"]))
    wu8 = _u8view(np.ascontiguousarray(inputs["L0_ffn_up_w"]))
    wd8 = _u8view(np.ascontiguousarray(inputs["L0_ffn_down_w"]))
    gatewT = p16w(np.ascontiguousarray(inputs["L0_attn_gate_w"].T))
    outwT = p16w(np.ascontiguousarray(inputs["L0_attn_out_w"].T))
    wg8T = p16w(np.ascontiguousarray(wg8.T))
    wu8T = p16w(np.ascontiguousarray(wu8.T))
    wd8T = p16w(np.ascontiguousarray(wd8.T))
    gscol = np.ascontiguousarray(inputs["L0_ffn_gate_scale"].reshape(64, 128).T)
    uscol = np.ascontiguousarray(inputs["L0_ffn_up_scale"].reshape(64, 128).T)
    dsrow = np.ascontiguousarray(inputs["L0_ffn_down_scale"].reshape(1, D))
    normcol = np.ascontiguousarray(norm_w.reshape(16, 128).T)
    gwT = np.ascontiguousarray(
        inputs["L1_ffn_gate_w"].T.reshape(16, 128, 8).transpose(1, 0, 2).reshape(128, 16 * 8))
    gbrep = np.ascontiguousarray(np.repeat(np.asarray(inputs["L1_ffn_gate_b"])[None, :], 128, 0))
    wdkvT = p16w(np.ascontiguousarray(inputs["L1_attn_W_DKV_w"].T))
    wdqT = p16w(np.ascontiguousarray(inputs["L1_attn_W_DQ_w"].T))
    wkrT = p16w(np.ascontiguousarray(inputs["L1_attn_W_KR_w"].T))
    bkvcol = np.ascontiguousarray(np.concatenate([
        np.asarray(inputs["L1_attn_W_DKV_b"]).reshape(4, 128).T,
        np.asarray(inputs["L1_attn_W_DQ_b"]).reshape(8, 128).T], axis=1).astype(np.float32))
    bkrrep = np.ascontiguousarray(
        np.repeat(np.asarray(inputs["L1_attn_W_KR_b"])[None, :], 128, 0))
    wOT = p16w(np.ascontiguousarray(inputs["L1_attn_W_O_w"].T))
    bOrow = np.ascontiguousarray(np.asarray(inputs["L1_attn_W_O_b"]).reshape(1, D))
    w1 = np.asarray(inputs["L1_ffn_w1"]); w3 = np.asarray(inputs["L1_ffn_w3"])
    w2 = np.asarray(inputs["L1_ffn_w2"])
    w1all = np.concatenate([p16w(np.ascontiguousarray(w1[e].T)) for e in range(NE)], 1)
    w3all = np.concatenate([p16w(np.ascontiguousarray(w3[e].T)) for e in range(NE)], 1)
    w2all = np.concatenate([p16w(np.ascontiguousarray(w2[e].T)) for e in range(NE)], 1)
    sw1 = p16w(np.ascontiguousarray(np.asarray(inputs["L1_ffn_sw1"]).T))
    sw3 = p16w(np.ascontiguousarray(np.asarray(inputs["L1_ffn_sw3"]).T))
    sw2 = p16w(np.ascontiguousarray(np.asarray(inputs["L1_ffn_sw2"]).T))
    in_maps = []
    for c in range(NCORE):
        b, half = c // 2, c % 2
        hh = 16 * half
        t0 = half * TPC
        wq = []
        for grp in range(2):
            rows = np.concatenate(
                [qkv_w[(hh + grp * 8 + i) * 192 + part * 64:
                       (hh + grp * 8 + i) * 192 + (part + 1) * 64]
                 for part in range(3) for i in range(8)], axis=0)
            wq.append(p16w(np.ascontiguousarray(rows.T)))
        bdcol = np.repeat(con["bd"][hh:hh + 16].reshape(1, 16), 64, 0).astype(np.float32)
        hh8 = 8 * half
        hrows = slice(hh8 * DH, (hh8 + 8) * DH)
        wuk = p16w(np.ascontiguousarray(inputs["L1_attn_W_UK_w"][hrows].T))
        wuv = p16w(np.ascontiguousarray(inputs["L1_attn_W_UV_w"][hrows].T))
        wuq = p16w(np.ascontiguousarray(inputs["L1_attn_W_UQ_w"][hrows].T))
        buk = np.ascontiguousarray(
            np.asarray(inputs["L1_attn_W_UK_b"])[hrows].reshape(8, 128).T.astype(np.float32))
        buq = np.ascontiguousarray(
            np.asarray(inputs["L1_attn_W_UQ_b"])[hrows].reshape(8, 128).T.astype(np.float32))
        buv = np.ascontiguousarray(np.repeat(
            np.asarray(inputs["L1_attn_W_UV_b"])[hrows].reshape(1, 8 * 128), 128, 0))
        sel = np.zeros((128, 2), np.float32)
        sel[:, half] = 1.0
        in_maps.append({
            "a0_xin": x[c * TPC:(c + 1) * TPC], "a0_identin": con["ident_bf"],
            "a1_wqkv0": wq[0], "a1_wqkv1": wq[1],
            "a1_qdrep": np.ascontiguousarray(con["qd_rep"][:, hh * BS0:(hh + 16) * BS0]),
            "a1_kdrep": np.ascontiguousarray(con["kd_rep"][:, hh * BS0:(hh + 16) * BS0]),
            "a1_ddT": np.ascontiguousarray(con["ddT"][:, hh * 2 * BS0:(hh + 16) * 2 * BS0]),
            "a1_bdcol": bdcol, "a1_identin": con["ident_bf"],
            "b_x": x[c * TPC:(c + 1) * TPC],
            "b_normcol": normcol, "b_gatewT": gatewT, "b_outwT": outwT,
            "b_wg8": wg8T, "b_wu8": wu8T, "b_wd8": wd8T,
            "b_gscol": gscol, "b_uscol": uscol, "b_dsrow": dsrow,
            "b_identin": con["ident_bf"],
            "b_ropeC": np.ascontiguousarray(con["ropeC"][t0:t0 + TPC]),
            "b_ropeS": np.ascontiguousarray(con["ropeS"][t0:t0 + TPC]),
            "b_gwT": gwT, "b_gbrep": gbrep,
            "b_wdkvT": wdkvT, "b_wdqT": wdqT, "b_wkrT": wkrT,
            "b_bkvcol": bkvcol, "b_bkrrep": bkrrep, "b_selcol": sel,
            "c_wukT": wuk, "c_wuvT": wuv, "c_wuqT": wuq,
            "c_bukcol": buk, "c_buvrep": buv, "c_buqcol": buq,
            "c_ropeCF": con["ropeC"], "c_ropeSF": con["ropeS"],
            "c_dmask": con["diagmask"], "c_identin": con["ident_bf"],
            "c_selcol": sel,
            "d_wOT": wOT, "d_bOrow": bOrow,
            "d_w1all": w1all, "d_w3all": w3all, "d_w2all": w2all,
            "d_sw1": sw1, "d_sw3": sw3, "d_sw2": sw2, "d_selcol": sel,
        })
    res = run_bass_kernel_spmd(ncF, in_maps, core_ids=list(range(NCORE)))
    outs = [res.results[c]["d_out"] for c in range(NCORE)]
    return np.concatenate(outs, 0).reshape(B, S, D)



_P_CACHE = {}


def _prog(name, builder):
    if name not in _P_CACHE:
        _P_CACHE[name] = builder()
    return _P_CACHE[name]


def p16w(wT):
    F_, O_ = wT.shape
    return np.ascontiguousarray(
        wT.reshape(F_ // 128, 128, O_ // 128, 128).transpose(1, 0, 2, 3).reshape(128, -1))


def run_A0(x):
    ncA0 = _prog("A0", build_A0)
    con = host_tables()
    in_maps = [{"xin": x[c * TPC:(c + 1) * TPC], "identin": con["ident_bf"]}
               for c in range(NCORE)]
    res = run_bass_kernel_spmd(ncA0, in_maps, core_ids=list(range(NCORE)))
    return [res.results[c]["xn0T"] for c in range(NCORE)]


def _xn0T_full(xn0T, b):
    e, o = xn0T[2 * b], xn0T[2 * b + 1]
    out = np.empty((128, 16, S), dtype=e.dtype)
    out[:, :, :TPC] = e.reshape(128, 16, TPC)
    out[:, :, TPC:] = o.reshape(128, 16, TPC)
    return np.ascontiguousarray(out.reshape(128, 16 * S))


def run_A1(xn0T, inputs):
    con = host_tables()
    ncA1 = _prog("A1", build_A1)
    qkv_w = inputs["L0_attn_qkv_w"]
    in_maps = []
    for c in range(NCORE):
        b, half = c // 2, c % 2
        hh = 16 * half
        wq = []
        for grp in range(2):
            rows = np.concatenate(
                [qkv_w[(hh + grp * 8 + i) * 192 + part * 64:
                       (hh + grp * 8 + i) * 192 + (part + 1) * 64]
                 for part in range(3) for i in range(8)], axis=0)
            wq.append(p16w(np.ascontiguousarray(rows.T)))
        bdcol = np.repeat(con["bd"][hh:hh + 16].reshape(1, 16), 64, 0).astype(np.float32)
        in_maps.append({
            "xn0T": _xn0T_full(xn0T, b),
            "wqkv0": wq[0], "wqkv1": wq[1],
            "qdrep": np.ascontiguousarray(con["qd_rep"][:, hh * BS0:(hh + 16) * BS0]),
            "kdrep": np.ascontiguousarray(con["kd_rep"][:, hh * BS0:(hh + 16) * BS0]),
            "ddT": np.ascontiguousarray(con["ddT"][:, hh * 2 * BS0:(hh + 16) * 2 * BS0]),
            "bdcol": bdcol,
            "identin": con["ident_bf"],
        })
    res = run_bass_kernel_spmd(ncA1, in_maps, core_ids=list(range(NCORE)))
    return [res.results[c]["attnT"] for c in range(NCORE)]


def _u8view(w_int32):
    return w_int32.view(np.uint8).reshape(*w_int32.shape, 4)[..., 0]


def run_B(x, xn0T, attnT, inputs):
    con = host_tables()
    ncB = _prog("B", build_B)
    gate_w = inputs["L0_attn_gate_w"]; out_w = inputs["L0_attn_out_w"]
    norm_w = np.asarray(inputs["L0_attn_norm_w"])
    wg8 = _u8view(np.ascontiguousarray(inputs["L0_ffn_gate_w"]))
    wu8 = _u8view(np.ascontiguousarray(inputs["L0_ffn_up_w"]))
    wd8 = _u8view(np.ascontiguousarray(inputs["L0_ffn_down_w"]))
    gatewT = p16w(np.ascontiguousarray(gate_w.T))
    outwT = p16w(np.ascontiguousarray(out_w.T))
    wg8T = p16w(np.ascontiguousarray(wg8.T))
    wu8T = p16w(np.ascontiguousarray(wu8.T))
    wd8T = p16w(np.ascontiguousarray(wd8.T))
    gscol = np.ascontiguousarray(inputs["L0_ffn_gate_scale"].reshape(64, 128).T)
    uscol = np.ascontiguousarray(inputs["L0_ffn_up_scale"].reshape(64, 128).T)
    dsrow = np.ascontiguousarray(inputs["L0_ffn_down_scale"].reshape(1, D))
    normcol = np.ascontiguousarray(norm_w.reshape(16, 128).T)
    gwT = np.ascontiguousarray(
        inputs["L1_ffn_gate_w"].T.reshape(16, 128, 8).transpose(1, 0, 2).reshape(128, 16 * 8))
    gbrep = np.ascontiguousarray(np.repeat(np.asarray(inputs["L1_ffn_gate_b"])[None, :], 128, 0))
    wdkvT = p16w(np.ascontiguousarray(inputs["L1_attn_W_DKV_w"].T))
    wdqT = p16w(np.ascontiguousarray(inputs["L1_attn_W_DQ_w"].T))
    wkrT = p16w(np.ascontiguousarray(inputs["L1_attn_W_KR_w"].T))
    bkvcol = np.concatenate([
        np.asarray(inputs["L1_attn_W_DKV_b"]).reshape(4, 128).T,
        np.asarray(inputs["L1_attn_W_DQ_b"]).reshape(8, 128).T], axis=1)
    bkvcol = np.ascontiguousarray(bkvcol.astype(np.float32))
    bkrrep = np.ascontiguousarray(
        np.repeat(np.asarray(inputs["L1_attn_W_KR_b"])[None, :], 128, 0))
    in_maps = []
    for c in range(NCORE):
        b, half = c // 2, c % 2
        t0 = half * TPC
        at = np.empty((128, 16, TPC), dtype=attnT[0].dtype)
        for fc in range(8):
            at[:, fc, :] = attnT[2 * b].reshape(128, 8, S)[:, fc, t0:t0 + TPC]
            at[:, 8 + fc, :] = attnT[2 * b + 1].reshape(128, 8, S)[:, fc, t0:t0 + TPC]
        in_maps.append({
            "x": x[c * TPC:(c + 1) * TPC],
            "xn0T": xn0T[c],
            "attnT": np.ascontiguousarray(at.reshape(128, 16 * TPC)),
            "normcol": normcol, "gatewT": gatewT, "outwT": outwT,
            "wg8": wg8T, "wu8": wu8T, "wd8": wd8T,
            "gscol": gscol, "uscol": uscol, "dsrow": dsrow,
            "identin": con["ident_bf"],
            "ropeC": np.ascontiguousarray(con["ropeC"][t0:t0 + TPC]),
            "ropeS": np.ascontiguousarray(con["ropeS"][t0:t0 + TPC]),
            "gwT": gwT, "gbrep": gbrep,
            "wdkvT": wdkvT, "wdqT": wdqT, "wkrT": wkrT,
            "bkvcol": bkvcol, "bkrrep": bkrrep,
        })
    res = run_bass_kernel_spmd(ncB, in_maps, core_ids=list(range(NCORE)))
    keys = ["x1", "xn1T", "ckvT", "cqT", "krT", "cwrows"] + (["hdbg", "zdbg"] if DEBUG_B else [])
    return {k: [res.results[c][k] for c in range(NCORE)] for k in keys}


def _pair_full(slices, b, nch):
    e = np.asarray(slices[2 * b]).reshape(128, nch, TPC)
    o = np.asarray(slices[2 * b + 1]).reshape(128, nch, TPC)
    out = np.empty((128, nch, S), dtype=e.dtype)
    out[:, :, :TPC] = e
    out[:, :, TPC:] = o
    return np.ascontiguousarray(out.reshape(128, nch * S))


def run_C(rb, inputs):
    con = host_tables()
    ncC = _prog("C", build_C)
    in_maps = []
    for c in range(NCORE):
        b, half = c // 2, c % 2
        hh8 = 8 * half
        rows = slice(hh8 * DH, (hh8 + 8) * DH)
        wuk = p16w(np.ascontiguousarray(inputs["L1_attn_W_UK_w"][rows].T))
        wuv = p16w(np.ascontiguousarray(inputs["L1_attn_W_UV_w"][rows].T))
        wuq = p16w(np.ascontiguousarray(inputs["L1_attn_W_UQ_w"][rows].T))
        buk = np.ascontiguousarray(
            np.asarray(inputs["L1_attn_W_UK_b"])[rows].reshape(8, 128).T.astype(np.float32))
        buq = np.ascontiguousarray(
            np.asarray(inputs["L1_attn_W_UQ_b"])[rows].reshape(8, 128).T.astype(np.float32))
        buv = np.ascontiguousarray(np.repeat(
            np.asarray(inputs["L1_attn_W_UV_b"])[rows].reshape(1, 8 * 128), 128, 0))
        krT_full = _pair_full(rb["krT"], b, 16).reshape(128, 16, S)
        krT_slice = np.ascontiguousarray(
            krT_full[:, hh8:hh8 + 8, :].reshape(128, 8 * S))
        in_maps.append({
            "ckvT": _pair_full(rb["ckvT"], b, 4),
            "cqT": _pair_full(rb["cqT"], b, 8),
            "krT": krT_slice,
            "wukT": wuk, "wuvT": wuv, "wuqT": wuq,
            "bukcol": buk, "buvrep": buv, "buqcol": buq,
            "ropeCF": con["ropeC"], "ropeSF": con["ropeS"],
            "dmask": con["diagmask"], "identin": con["ident_bf"],
        })
    res = run_bass_kernel_spmd(ncC, in_maps, core_ids=list(range(NCORE)))
    return [res.results[c]["oT"] for c in range(NCORE)]


def run_D(rb, oT, inputs):
    ncD = _prog("D", build_D)
    wOT = p16w(np.ascontiguousarray(inputs["L1_attn_W_O_w"].T))
    bOrow = np.ascontiguousarray(np.asarray(inputs["L1_attn_W_O_b"]).reshape(1, D))
    w1 = np.asarray(inputs["L1_ffn_w1"]); w3 = np.asarray(inputs["L1_ffn_w3"])
    w2 = np.asarray(inputs["L1_ffn_w2"])
    w1all = np.concatenate([p16w(np.ascontiguousarray(w1[e].T)) for e in range(NE)], 1)
    w3all = np.concatenate([p16w(np.ascontiguousarray(w3[e].T)) for e in range(NE)], 1)
    w2all = np.concatenate([p16w(np.ascontiguousarray(w2[e].T)) for e in range(NE)], 1)
    sw1 = p16w(np.ascontiguousarray(np.asarray(inputs["L1_ffn_sw1"]).T))
    sw3 = p16w(np.ascontiguousarray(np.asarray(inputs["L1_ffn_sw3"]).T))
    sw2 = p16w(np.ascontiguousarray(np.asarray(inputs["L1_ffn_sw2"]).T))
    in_maps = []
    for c in range(NCORE):
        b, half = c // 2, c % 2
        t0 = half * TPC
        ot_own = np.empty((128, 16, TPC), dtype=oT[0].dtype)
        for fc in range(8):
            ot_own[:, fc, :] = np.asarray(oT[2 * b]).reshape(128, 8, S)[:, fc, t0:t0 + TPC]
            ot_own[:, 8 + fc, :] = np.asarray(oT[2 * b + 1]).reshape(128, 8, S)[:, fc, t0:t0 + TPC]
        in_maps.append({
            "x1": rb["x1"][c], "xn1T": rb["xn1T"][c],
            "oT": np.ascontiguousarray(ot_own.reshape(128, 16 * TPC)),
            "cwrows": rb["cwrows"][c],
            "wOT": wOT, "bOrow": bOrow,
            "w1all": w1all, "w3all": w3all, "w2all": w2all,
            "sw1": sw1, "sw3": sw3, "sw2": sw2,
        })
    res = run_bass_kernel_spmd(ncD, in_maps, core_ids=list(range(NCORE)))
    return [res.results[c]["out"] for c in range(NCORE)]


def kernel(**inputs):
    return run_fused(inputs)


def kernel_unfused(**inputs):
    x = np.ascontiguousarray(
        np.asarray(inputs["L0_attn_hidden_states"]).reshape(B * S, D))
    xn0T = run_A0(x)
    attnT = run_A1(xn0T, inputs)
    rb = run_B(x, xn0T, attnT, inputs)
    oT = run_C(rb, inputs)
    outs = run_D(rb, oT, inputs)
    return np.concatenate(outs, 0).reshape(B, S, D)


# version 03
# speedup vs baseline: 1.2563x; 1.2563x over previous
import numpy as np
import ml_dtypes
import concourse.bass as bass
import concourse.mybir as mybir
from concourse.bass_utils import run_bass_kernel_spmd

F32 = mybir.dt.float32
BF16 = mybir.dt.bfloat16
U8 = mybir.dt.uint8
AF = mybir.ActivationFunctionType
ALU = mybir.AluOpType
AX = mybir.AxisListType.X
BF = ml_dtypes.bfloat16

NCORE = 8
B, S, D = 4, 4096, 2048
TPC = 2048
H0, D0, BS0 = 32, 64, 256
NB0 = S // BS0
H1, DH, DC, DQC = 16, 128, 512, 1024
FF0 = 8192
FFE = 1408
NE = 8
EPS = 1e-5
DEBUG_B = False



class _Rec:
    __slots__ = ("key", "meth", "args", "kwargs", "sem_incs")

    def __init__(self, key, meth, args, kwargs):
        self.key, self.meth, self.args, self.kwargs = key, meth, args, kwargs
        self.sem_incs = []

    def then_inc(self, sem, amt):
        self.sem_incs.append((sem, amt))
        return self


class _EngRec:
    def __init__(self, ctx, key):
        self._ctx, self._key = ctx, key

    def __getattr__(self, meth):
        ctx, key = self._ctx, self._key

        def call(*a, **k):
            r = _Rec(key, meth, a, k)
            ctx.ops.append(r)
            return r

        return call


_ENG_NAMES = {"t": "tensor", "v": "vector", "s": "scalar", "g": "gpsimd", "y": "sync"}


class Ctx:

    def __init__(self, name):
        self.nc = bass.Bass("TRN2", target_bir_lowering=False, debug=False)
        self.name = name
        self._exit = []
        self.cnt = {}
        self.sem = {}
        self.ops = []
        self.prefix = ""
        self.io = {}
        self.t = _EngRec(self, "t")
        self.v = _EngRec(self, "v")
        self.s = _EngRec(self, "s")
        self.g = _EngRec(self, "g")
        self.y = _EngRec(self, "y")

    def __enter__(self):
        return self

    def __exit__(self, *a):
        return False

    def alloc_sems(self, names):
        for n in names:
            if n in self.sem:
                continue
            cm = self.nc.semaphore(f"sem_{n}")
            self.sem[n] = cm.__enter__()
            self._exit.append(cm)
            self.cnt[n] = 0

    def mark(self):
        return len(self._exit)

    def release(self, m):
        for cm in reversed(self._exit[m:]):
            cm.__exit__(None, None, None)
        del self._exit[m:]

    def inc(self, rec, key, amt=1):
        rec.then_inc(self.sem[key], amt)
        self.cnt[key] += amt
        return self.cnt[key]

    def tick(self, key):
        return self.cnt[key]

    def finalize(self):
        ops = self.ops
        with self.nc.Block() as block:
            for key, name in _ENG_NAMES.items():
                def body(eng, key=key):
                    for r in ops:
                        if r.key != key:
                            continue
                        instr = getattr(eng, r.meth)(*r.args, **r.kwargs)
                        for sem, amt in r.sem_incs:
                            instr = instr.then_inc(sem, amt)
                getattr(block, name)(body)
        for cm in reversed(self._exit):
            cm.__exit__(None, None, None)
        self._exit = []
        return self.nc

    def sb(self, name, shape, dtype):
        cm = self.nc.sbuf_tensor(self.prefix + name, shape, dtype)
        t = cm.__enter__()
        self._exit.append(cm)
        return t

    def ps(self, name, shape, dtype=F32):
        cm = self.nc.psum_tensor(self.prefix + name, shape, dtype)
        t = cm.__enter__()
        self._exit.append(cm)
        return t

    def din(self, name, shape, dtype):
        if name in self.io:
            return self.io[name]
        return self.nc.dram_tensor(self.prefix + name, list(shape), dtype,
                                   kind="ExternalInput").ap()

    def dout(self, name, shape, dtype):
        if name in self.io:
            return self.io[name]
        return self.nc.dram_tensor(self.prefix + name, list(shape), dtype,
                                   kind="ExternalOutput").ap()


def rsqrt_cols(ctx, out_col, var_col, eps, eng_wait_key=None):
    s = ctx.s
    s.activation(out_col, var_col, AF.Ln, bias=float(eps), scale=1.0)
    n = ctx.inc(s.activation(out_col, out_col, AF.Exp, scale=-0.5), "s")
    return n



def _host_const():
    c = {}
    c["ident_bf"] = np.eye(128, dtype=np.float32)
    f32 = np.float32
    base = f32(1.0) / f32(2.0) ** f32(8.0 / H0)
    rate = base ** (np.arange(H0, dtype=f32) + 1.0)
    factor = f32(1.0 + 1e-05)
    slope = (rate * factor).astype(f32)
    r = np.arange(BS0, dtype=f32) + 1.0
    qd = np.exp(-slope[:, None] * r[None, :]).astype(f32)
    kd = np.exp(-slope[:, None] * (BS0 - r[None, :])).astype(f32)
    dd = slope[:, None, None] * (r[:, None] - r[None, :])[None]
    dd = np.where(dd >= 0, np.exp(-dd), 0.0).astype(f32)
    bd = np.exp(-slope * f32(BS0)).astype(f32)
    c["qd_rep"] = np.repeat(qd.reshape(1, H0 * BS0), 128, 0).astype(BF)
    c["kd_rep"] = np.repeat(kd.reshape(1, H0 * BS0), 128, 0).astype(BF)
    ddT = np.transpose(dd, (0, 2, 1))
    ddT = ddT.reshape(H0, 2, 128, BS0)
    c["ddT"] = np.transpose(ddT, (2, 0, 1, 3)).reshape(128, H0 * 2 * BS0).astype(BF)
    c["bd"] = bd
    pos = np.arange(S, dtype=np.float64)
    idx = np.arange(DH // 2, dtype=np.float64)
    theta = (pos[:, None] * (10000.0 ** (-2.0 * idx / DH))[None, :])
    cos = np.cos(theta).astype(f32)
    sin = np.sin(theta).astype(f32)
    C2 = np.empty((S, DH), f32)
    S2 = np.empty((S, DH), f32)
    C2[:, 0::2] = cos; C2[:, 1::2] = cos
    S2[:, 0::2] = sin; S2[:, 1::2] = sin
    c["ropeC"] = C2
    c["ropeS"] = S2
    m = np.zeros((4, 128, 512), f32)
    for o in range(4):
        for k in range(128):
            m[o, k, :] = (np.arange(512) >= (o * 128 + k)).astype(f32)
    c["diagmask"] = m.reshape(4 * 128, 512)
    return c


_CONST = None


def host_tables():
    global _CONST
    if _CONST is None:
        _CONST = _host_const()
    return _CONST



def build_A0(ctx=None):
    own = ctx is None
    with (Ctx("A0") if own else ctx) as ctx:
        nc = ctx.nc
        xin = ctx.din("xin", [TPC, D], F32)
        identin = ctx.din("identin", [128, 128], F32)
        xn0T_d = ctx.dout("xn0T", [128, 16 * TPC], BF16)

        ctx.alloc_sems(["t", "v", "s", "dy", "dg"])
        t, v, s, g, y = ctx.t, ctx.v, ctx.s, ctx.g, ctx.y

        xt = [ctx.sb(f"xt{i}", [128, D], F32) for i in range(2)]
        sq = ctx.sb("sq", [128, D], F32)
        xnb = ctx.sb("xnb", [128, D], BF16)
        st_ = ctx.sb("stats", [128, 8], F32)
        ident = ctx.sb("ident", [128, 128], BF16)
        outT = ctx.sb("outT", [128, 16 * TPC], BF16)
        ptr = [ctx.ps(f"ptr{i}", [128, 128], BF16) for i in range(2)]

        ident_done = ctx.inc(g.dma_start(out=ident[:], in_=identin[:, :]), "dg", 16)

        ctx.inc(y.dma_start(out=xt[0][:], in_=xin[0:128, :]), "dy", 16)

        evac_done = [0] * 2
        tile_consumed = [0, 0]

        for k in range(16):
            buf = xt[k % 2]
            if k + 1 < 16:
                nbuf = (k + 1) % 2
                if k >= 1:
                    y.wait_ge(ctx.sem["v"], tile_consumed[nbuf])
                ctx.inc(y.dma_start(out=xt[nbuf][:],
                                    in_=xin[(k + 1) * 128:(k + 2) * 128, :]), "dy", 16)
            v.wait_ge(ctx.sem["dy"], 16 * (k + 1))
            v.reduce_sum(st_[:, 0:1], buf[:], axis=AX)
            v.tensor_mul(sq[:], buf[:], buf[:])
            n_v = ctx.inc(v.reduce_sum(st_[:, 1:2], sq[:], axis=AX), "v")
            v.wait_ge(ctx.sem["v"], n_v)
            v.tensor_scalar(st_[:, 2:3], st_[:, 0:1], 1.0 / D, None, ALU.mult)
            n_v = ctx.inc(v.tensor_scalar(st_[:, 3:4], st_[:, 1:2], 1.0 / D, None,
                                          ALU.mult), "v")
            v.wait_ge(ctx.sem["v"], n_v)
            n_v = ctx.inc(v.tensor_mul(st_[:, 4:5], st_[:, 2:3], st_[:, 2:3]), "v")
            v.wait_ge(ctx.sem["v"], n_v)
            n_v = ctx.inc(v.scalar_tensor_tensor(st_[:, 4:5], st_[:, 3:4], float(EPS),
                                                 st_[:, 4:5], ALU.add, ALU.subtract), "v")
            s.wait_ge(ctx.sem["v"], n_v)
            s.activation(st_[:, 5:6], st_[:, 4:5], AF.Ln, scale=1.0)
            n_s = ctx.inc(s.activation(st_[:, 5:6], st_[:, 5:6], AF.Exp, scale=-0.5), "s")
            v.wait_ge(ctx.sem["s"], n_s)
            n_v = ctx.inc(v.scalar_tensor_tensor(st_[:, 6:7], st_[:, 2:3], -1.0,
                                                 st_[:, 5:6], ALU.mult, ALU.mult), "v")
            v.wait_ge(ctx.sem["v"], n_v)
            n_s = ctx.inc(
                v.tensor_scalar(xnb[:], buf[:], st_[:, 5:6], st_[:, 6:7],
                                ALU.mult, ALU.add), "v")
            tile_consumed[k % 2] = n_s
            t.wait_ge(ctx.sem["v"], n_s)
            if k == 0:
                t.wait_ge(ctx.sem["dg"], ident_done)
            for f in range(16):
                pb = ptr[f % 2]
                t.wait_ge(ctx.sem["v"], evac_done[f % 2])
                n_t = ctx.inc(t.transpose(pb[:], xnb[:, f * 128:(f + 1) * 128],
                                          ident[:]), "t")
                v.wait_ge(ctx.sem["t"], n_t)
                evac_done[f % 2] = ctx.inc(
                    v.tensor_copy(outT[:, f * TPC + k * 128: f * TPC + (k + 1) * 128],
                                  pb[:]), "v")

        y.wait_ge(ctx.sem["v"], ctx.tick("v"))
        ctx.inc(y.dma_start(out=xn0T_d[:, :], in_=outT[:]), "dy", 16)
        y.wait_ge(ctx.sem["dy"], ctx.tick("dy"))

        return ctx.finalize() if own else None



def build_A1(ctx=None):
    NT = 8
    own = ctx is None
    with (Ctx("A1") if own else ctx) as ctx:
        xn0T_d = ctx.din("xn0T", [128, 16 * S], BF16)
        wqkv_d = [ctx.din(f"wqkv{gidx}", [128, 16 * 12 * 128], F32) for gidx in range(2)]
        qdrep_d = ctx.din("qdrep", [128, 16 * BS0], BF16)
        kdrep_d = ctx.din("kdrep", [128, 16 * BS0], BF16)
        ddT_d = ctx.din("ddT", [128, 16 * 2 * BS0], BF16)
        identin = ctx.din("identin", [128, 128], F32)
        bdcol_d = ctx.din("bdcol", [64, 16], F32)
        attnT_d = ctx.dout("attnT", [128, 8 * S], BF16)

        ctx.alloc_sems(["t", "v", "s", "dy", "dg"])
        t, v, s, g, y = ctx.t, ctx.v, ctx.s, ctx.g, ctx.y
        sem, inc, tick = ctx.sem, ctx.inc, ctx.tick

        wsb = ctx.sb("wsb", [128, 16 * 12 * 128], BF16)
        xtile = [ctx.sb(f"xtile{i}", [128, 16 * 512], BF16) for i in range(2)]
        qkvT = ctx.sb("qkvT", [128, 12 * 512], BF16)
        attnT = ctx.sb("attnT_sb", [128, 4 * S], BF16)
        qd_sb = ctx.sb("qd_sb", [128, 16 * BS0], BF16)
        kd_sb = ctx.sb("kd_sb", [128, 16 * BS0], BF16)
        dd_sb = ctx.sb("dd_sb", [128, 16 * 2 * BS0], BF16)
        bd_sb = ctx.sb("bd_sb", [64, 16], F32)
        ident = ctx.sb("ident", [128, 128], BF16)
        qq_sb = ctx.sb("qq_sb", [64, BS0], BF16)
        kk_sb = ctx.sb("kk_sb", [64, BS0], BF16)
        kt_sb = ctx.sb("kt_sb", [128, 2 * 64], BF16)
        vt_sb = ctx.sb("vt_sb", [128, 2 * 64], BF16)
        awdd = ctx.sb("awdd", [128, 2 * BS0], BF16)
        st_f = ctx.sb("st_f", [64, 8 * 64], F32)
        st_b = ctx.sb("st_b", [64, 64], BF16)

        ps_mm = [ctx.ps(f"psmm{i}", [128, 512], F32) for i in range(2)]
        ps_aw = ctx.ps("psaw", [128, 512], F32)
        ps_tr = ctx.ps("pstr", [128, 256], BF16)
        ps_ys = ctx.ps("psys", [64, 256], F32)
        ps_kv = ctx.ps("pskv", [64, 64], F32)

        inc(g.dma_start(out=ident[:], in_=identin[:, :]), "dg", 16)
        inc(g.dma_start(out=qd_sb[:], in_=qdrep_d[:, :]), "dg", 16)
        inc(g.dma_start(out=kd_sb[:], in_=kdrep_d[:, :]), "dg", 16)
        inc(g.dma_start(out=dd_sb[:], in_=ddT_d[:, :]), "dg", 16)
        inc(g.dma_start(out=bd_sb[:], in_=bdcol_d[:, :]), "dg", 16)
        const_ok = tick("dg")

        v.wait_ge(sem["dg"], const_ok)
        inc(v.memset(st_f[:], 0.0), "v")

        evac_tick = [0, 0]
        last_attn_store = 0
        for grp in range(2):
            g.wait_ge(sem["t"], tick("t"))
            wg_done = inc(g.dma_start(out=wsb[:], in_=wqkv_d[grp][:, :]), "dg", 16)

            if grp > 0:
                s.wait_ge(sem["dy"], tick("dy"))
                v.wait_ge(sem["t"], tick("t"))
                inc(v.memset(st_f[:], 0.0), "v")

            for tt in range(NT):
                xb = xtile[tt % 2]
                y.wait_ge(sem["t"], tick("t"))
                if xn0T_d.shape[0] == 256:
                    rb = 128 * (tt // 4)
                    t4 = (tt % 4) * 512
                    xload = inc(y.dma_start(
                        out=xb[:].rearrange("p (c t) -> p c t", c=16),
                        in_=xn0T_d[rb:rb + 128, :].rearrange("p (c t) -> p c t", c=16)[:, :, t4:t4 + 512]), "dy", 16)
                else:
                    xload = inc(y.dma_start(
                        out=xb[:].rearrange("p (c t) -> p c t", c=16),
                        in_=xn0T_d.rearrange("p (c t) -> p c t", c=16)[:, :, tt * 512:(tt + 1) * 512]), "dy", 16)
                qkvT_free_v = tick("v")
                for oc in range(12):
                    pb = ps_mm[oc % 2]
                    t.wait_ge(sem["dy"], xload)
                    t.wait_ge(sem["dg"], wg_done)
                    t.wait_ge(sem["s"], evac_tick[oc % 2])
                    mm = None
                    for fc in range(16):
                        mm = t.matmul(pb[:], wsb[:, (fc * 12 + oc) * 128:(fc * 12 + oc + 1) * 128],
                                      xb[:, fc * 512:(fc + 1) * 512],
                                      start=(fc == 0), stop=(fc == 15))
                    n_t = inc(mm, "t")
                    s.wait_ge(sem["t"], n_t)
                    s.wait_ge(sem["v"], qkvT_free_v)
                    evac_tick[oc % 2] = inc(
                        s.activation(qkvT[:, oc * 512:(oc + 1) * 512], pb[:], AF.Silu), "s")
                qkv_ready = tick("s")
                for half in range(2):
                    for hl in range(8):
                        h16 = grp * 8 + hl
                        ro0 = (hl % 2) * 64
                        qch, kch, vch = hl // 2, 4 + hl // 2, 8 + hl // 2
                        qT = qkvT[ro0:ro0 + 64, qch * 512 + half * 256: qch * 512 + half * 256 + 256]
                        kT = qkvT[ro0:ro0 + 64, kch * 512 + half * 256: kch * 512 + half * 256 + 256]
                        vT = qkvT[ro0:ro0 + 64, vch * 512 + half * 256: vch * 512 + half * 256 + 256]
                        idq = ident[ro0:ro0 + 64, ro0:ro0 + 64]

                        v.wait_ge(sem["s"], qkv_ready)
                        v.wait_ge(sem["t"], tick("t"))
                        v.tensor_tensor(qq_sb[:], qT, qd_sb[ro0:ro0 + 64, h16 * BS0:(h16 + 1) * BS0], ALU.mult)
                        nv_qq = inc(v.tensor_tensor(kk_sb[:], kT, kd_sb[ro0:ro0 + 64, h16 * BS0:(h16 + 1) * BS0], ALU.mult), "v")

                        t.wait_ge(sem["s"], qkv_ready)
                        t.wait_ge(sem["v"], tick("v"))
                        mm = None
                        for jc in range(2):
                            mm = t.matmul(ps_aw[:, jc * 256:(jc + 1) * 256],
                                          kT[:, jc * 128:(jc + 1) * 128], qT,
                                          start=True, stop=True)
                        n_aw = inc(mm, "t")
                        mm = None
                        for jc in range(2):
                            mm = t.transpose(ps_tr[:, jc * 64:(jc + 1) * 64],
                                             kk_sb[:, jc * 128:(jc + 1) * 128],
                                             ident[0:64, 0:64])
                        for jc in range(2):
                            mm = t.transpose(ps_tr[:, 128 + jc * 64:128 + (jc + 1) * 64],
                                             vT[:, jc * 128:(jc + 1) * 128], idq)
                        n_tr = inc(mm, "t")

                        v.wait_ge(sem["t"], n_tr)
                        for jc in range(2):
                            v.tensor_tensor(
                                awdd[:, jc * 256:(jc + 1) * 256],
                                ps_aw[:, jc * 256:(jc + 1) * 256],
                                dd_sb[:, (h16 * 2 + jc) * 256:(h16 * 2 + jc + 1) * 256],
                                ALU.mult)
                        v.tensor_copy(kt_sb[:], ps_tr[:, 0:128])
                        nv_aw = inc(v.tensor_copy(vt_sb[:], ps_tr[:, 128:256]), "v")

                        nv_st = inc(v.tensor_copy(st_b[:], st_f[:, hl * 64:(hl + 1) * 64]), "v")

                        t.wait_ge(sem["v"], nv_st)
                        t.wait_ge(sem["s"], tick("s"))
                        for jc in range(2):
                            t.matmul(ps_ys[:], vt_sb[:, jc * 64:(jc + 1) * 64],
                                     awdd[:, jc * 256:(jc + 1) * 256],
                                     start=(jc == 0), stop=False)
                        mm = t.matmul(ps_ys[:], st_b[:], qq_sb[:], start=False, stop=True)
                        n_ys = inc(mm, "t")
                        mm = None
                        for jc in range(2):
                            mm = t.matmul(ps_kv[:], kt_sb[:, jc * 64:(jc + 1) * 64],
                                          vt_sb[:, jc * 64:(jc + 1) * 64],
                                          start=(jc == 0), stop=(jc == 1))
                        n_kv = inc(mm, "t")

                        s.wait_ge(sem["t"], n_ys)
                        fch = hl // 2
                        ro = (hl % 2) * 64
                        blk0 = tt * 512 + half * 256
                        last_attn_store = inc(
                            s.copy(attnT[ro:ro + 64, fch * S + blk0: fch * S + blk0 + 256],
                                   ps_ys[:]), "s")

                        v.wait_ge(sem["t"], n_kv)
                        nv_up = inc(v.scalar_tensor_tensor(
                            st_f[:, hl * 64:(hl + 1) * 64],
                            st_f[:, hl * 64:(hl + 1) * 64],
                            bd_sb[0:64, h16:h16 + 1],
                            ps_kv[:], ALU.mult, ALU.add), "v")
                        v.wait_ge(sem["v"], nv_up)

            y.wait_ge(sem["s"], last_attn_store)
            if attnT_d.shape[0] == 256:
                inc(y.dma_start(
                    out=attnT_d[0:128, :].rearrange("p (c t) -> p c t", t=TPC)[:, grp * 4:(grp + 1) * 4, :],
                    in_=attnT[:].rearrange("p (c t) -> p c t", t=S)[:, :, 0:TPC]), "dy", 16)
                inc(y.dma_start(
                    out=attnT_d[128:256, :].rearrange("p (c t) -> p c t", t=TPC)[:, grp * 4:(grp + 1) * 4, :],
                    in_=attnT[:].rearrange("p (c t) -> p c t", t=S)[:, :, TPC:S]), "dy", 16)
            else:
                inc(y.dma_start(out=attnT_d[:, grp * 4 * S:(grp + 1) * 4 * S], in_=attnT[:]),
                    "dy", 16)
        y.wait_ge(sem["dy"], tick("dy"))
        return ctx.finalize() if own else None



def build_B(ctx=None, upto=99):
    NT4 = 4
    own = ctx is None
    with (Ctx("B") if own else ctx) as ctx:
        x_d = ctx.din("x", [TPC, D], F32)
        xn0T_d = ctx.din("xn0T", [128, 16 * TPC], BF16)
        attnT_d = ctx.din("attnT", [128, 16 * TPC], BF16)
        normcol_d = ctx.din("normcol", [128, 16], F32)
        gatewT_d = ctx.din("gatewT", [128, 16 * 16 * 128], F32)
        outwT_d = ctx.din("outwT", [128, 16 * 16 * 128], F32)
        wg8_d = ctx.din("wg8", [128, 16 * 64 * 128], U8)
        wu8_d = ctx.din("wu8", [128, 16 * 64 * 128], U8)
        wd8_d = ctx.din("wd8", [128, 64 * 16 * 128], U8)
        gscol_d = ctx.din("gscol", [128, 64], F32)
        uscol_d = ctx.din("uscol", [128, 64], F32)
        dsrow_d = ctx.din("dsrow", [1, D], F32)
        identin = ctx.din("identin", [128, 128], F32)
        ropeC_d = ctx.din("ropeC", [TPC, DH], F32)
        ropeS_d = ctx.din("ropeS", [TPC, DH], F32)
        gwT_d = ctx.din("gwT", [128, 16 * 8], F32)
        gbrep_d = ctx.din("gbrep", [128, 8], F32)
        wdkvT_d = ctx.din("wdkvT", [128, 16 * 4 * 128], F32)
        wdqT_d = ctx.din("wdqT", [128, 16 * 8 * 128], F32)
        wkrT_d = ctx.din("wkrT", [128, 16 * 16 * 128], F32)
        bkv_col_d = ctx.din("bkvcol", [128, 12], F32)
        bkr_rep_d = ctx.din("bkrrep", [128, D], F32)
        selcol_d = ctx.din("selcol", [128, 2], F32)

        x1_d = ctx.dout("x1", [TPC, D], BF16)
        xn1T_d = ctx.dout("xn1T", [128, 16 * TPC], BF16)
        ckvT_d = ctx.dout("ckvT", [128, 4 * TPC], BF16)
        cqT_d = ctx.dout("cqT", [128, 8 * TPC], BF16)
        krT_d = ctx.dout("krT", [128, 16 * TPC], BF16)
        cwrows_d = ctx.dout("cwrows", [8, TPC], F32)

        hw_scr = ctx.nc.dram_tensor("hw_scr", [128, 64 * TPC], BF16).ap()
        hdbg_d = ctx.dout("hdbg", [128, 64 * TPC], BF16) if DEBUG_B else None
        zdbg_d = ctx.dout("zdbg", [128, 16 * TPC], BF16) if DEBUG_B else None
        outw_scr = ctx.nc.dram_tensor("outw_scr", [128, 16 * 16 * 128], BF16).ap()

        ctx.alloc_sems(["t", "v", "s", "dy", "dg", "dz"])
        t, v, s, g, y = ctx.t, ctx.v, ctx.s, ctx.g, ctx.y
        sem, inc, tick = ctx.sem, ctx.inc, ctx.tick

        bufA = ctx.sb("bufA", [128, 16 * TPC], BF16)
        bufB = ctx.sb("bufB", [128, 16 * TPC], BF16)
        hsb = ctx.sb("hsb", [128, 64 * 128], BF16)
        scratch16 = ctx.sb("scratch16", [128, 16384], U8)
        sc_bf = scratch16[:].bitcast(BF16)
        sc_f = scratch16[:].bitcast(F32)
        gwtile = [sc_bf[:, i * 2048:(i + 1) * 2048] for i in range(2)]
        wgu = [sc_bf[:, i * 4096:(i + 1) * 4096] for i in range(2)]
        xtile = sc_f[:, 0:2048]
        scrA = ctx.sb("scrA", [128, D], F32)
        scrB = ctx.sb("scrB", [128, D], F32)
        scrD = ctx.sb("scrD", [128, D], BF16)
        owdm = ctx.sb("owdm", [128, D], BF16)
        rowA = ctx.sb("rowA", [1, D], F32)
        rowB = ctx.sb("rowB", [1, D], F32)
        rsrep = sc_f[:, 2048:4096]
        ident = ctx.sb("ident", [128, 128], F32)
        ncol = ctx.sb("ncol", [128, 16], F32)
        gscol = ctx.sb("gscolsb", [128, 64], F32)
        uscol = ctx.sb("uscolsb", [128, 64], F32)
        onesb = ctx.sb("onesb", [128, 1], BF16)
        ones1f = ctx.sb("ones1f", [1, 512], F32)
        st_ = ctx.sb("stats", [128, 8], F32)
        gwsb = ctx.sb("gwsb", [128, 16 * 8], BF16)
        gbrep = ctx.sb("gbrepsb", [128, 8], F32)
        sctok = ctx.sb("sctok", [128, 8], F32)
        cwtok = ctx.sb("cwtok", [128, 8], F32)
        r8a = ctx.sb("r8a", [128, 8], F32)
        r8b = ctx.sb("r8b", [128, 8], F32)
        r8c = ctx.sb("r8c", [128, 8], F32)
        bkvcol = ctx.sb("bkvcolsb", [128, 12], F32)
        selcol = ctx.sb("selcolsb", [128, 2], F32)

        psA = ctx.ps("psA", [128, 1024], F32)
        psB = ctx.ps("psB", [128, 1024], F32)
        psC = ctx.ps("psC", [128, 2048], F32)

        for ap_sb, ap_d in ((ident[:], identin), (ncol[:], normcol_d),
                            (gscol[:], gscol_d), (uscol[:], uscol_d),
                            (gwsb[:], gwT_d), (gbrep[:], gbrep_d),
                            (bkvcol[:], bkv_col_d)):
            inc(g.dma_start(out=ap_sb, in_=ap_d[:, :]), "dg", 16)
        inc(g.dma_start(out=selcol[:], in_=selcol_d[:, :]), "dg", 16)
        n_c0 = tick("dg")
        inc(y.dma_start(out=bufA[:], in_=xn0T_d[:, :]), "dy", 16)
        if attnT_d.shape[0] == 512:
            inc(y.dma_start(out=bufB[:, 0:8 * TPC], in_=attnT_d[0:128, :]), "dy", 16)
            inc(y.dma_start(out=bufB[:, 8 * TPC:16 * TPC], in_=attnT_d[256:384, :]),
                "dy", 16)
            n_res = tick("dy")
            scab = scrA[:].bitcast(BF16)
            ev_bl = 0
            for fcb in range(16):
                rb2 = 128 if fcb < 8 else 384
                fcl = fcb % 8
                y.wait_ge(sem["v"], ev_bl)
                lhi = inc(y.dma_start(
                    out=scab[:, 0:TPC],
                    in_=attnT_d[rb2:rb2 + 128, fcl * TPC:(fcl + 1) * TPC]), "dy", 16)
                v.wait_ge(sem["dy"], lhi)
                if fcb == 0:
                    v.wait_ge(sem["dg"], n_c0)
                v.tensor_scalar(bufB[:, fcb * TPC:(fcb + 1) * TPC],
                                bufB[:, fcb * TPC:(fcb + 1) * TPC],
                                selcol[:, 0:1], None, ALU.mult)
                ev_bl = inc(v.scalar_tensor_tensor(
                    bufB[:, fcb * TPC:(fcb + 1) * TPC], scab[:, 0:TPC],
                    selcol[:, 1:2], bufB[:, fcb * TPC:(fcb + 1) * TPC],
                    ALU.mult, ALU.add), "v")
            blend_done = ev_bl
        else:
            inc(y.dma_start(out=bufB[:], in_=attnT_d[:, :]), "dy", 16)
            n_res = tick("dy")
            blend_done = 0
        v.wait_ge(sem["dg"], n_c0)
        v.memset(onesb[:], 1.0)
        nv0 = inc(v.memset(ones1f[:], 1.0), "v")
        for fc in range(16):
            dst = gwtile[fc % 2]
            g.wait_ge(sem["dz"], max(0, (fc - 1)) * 16)
            ld = inc(g.dma_start(out=dst, in_=outwT_d[:, fc * 2048:(fc + 1) * 2048]),
                     "dg", 16)
            y.wait_ge(sem["dg"], ld)
            inc(y.dma_start(out=outw_scr[:, fc * 2048:(fc + 1) * 2048], in_=dst),
                "dz", 16)
        outw_ready = tick("dz")

        if upto <= 0:
            y.wait_ge(sem["dz"], tick("dz")); y.wait_ge(sem["dy"], tick("dy"))
            return ctx.finalize() if own else None if own else None
        t.wait_ge(sem["dy"], n_res)
        t.wait_ge(sem["v"], nv0)
        for qt in range(4):
            t.wait_ge(sem["v"], tick("v"))
            mm = None
            for fc in range(16):
                mm = t.matmul(psA[0:1, 0:512], onesb[:],
                              bufA[:, fc * TPC + qt * 512: fc * TPC + (qt + 1) * 512],
                              start=(fc == 0), stop=(fc == 15))
            n_t = inc(mm, "t")
            v.wait_ge(sem["t"], n_t)
            inc(v.tensor_scalar(rowB[:, qt * 512:(qt + 1) * 512], psA[0:1, 0:512],
                                -128.0, None, ALU.mult), "v")
        sq = scrD[:, 0:512]
        s.wait_ge(sem["dy"], n_res)
        if blend_done:
            s.wait_ge(sem["v"], blend_done)
        for qt in range(4):
            for fc in range(16):
                ns = inc(s.activation(
                    sq, bufB[:, fc * TPC + qt * 512: fc * TPC + (qt + 1) * 512],
                    AF.Square), "s")
                t.wait_ge(sem["s"], ns)
                if fc == 0:
                    t.wait_ge(sem["v"], tick("v"))
                n_t = inc(t.matmul(psA[0:1, 0:512], onesb[:], sq,
                                   start=(fc == 0), stop=(fc == 15)), "t")
                if fc < 15:
                    s.wait_ge(sem["t"], n_t)
            v.wait_ge(sem["t"], tick("t"))
            n_v = inc(v.tensor_scalar(rowA[:, 0:512], psA[0:1, 0:512],
                                      1.0 / D, float(EPS), ALU.mult, ALU.add), "v")
            s.wait_ge(sem["v"], n_v)
            s.activation(rowA[:, 0:512], rowA[:, 0:512], AF.Ln, scale=1.0)
            n_s = inc(s.activation(rowA[:, 0:512], rowA[:, 0:512], AF.Exp, scale=-0.5),
                      "s")
            t.wait_ge(sem["s"], n_s)
            t.wait_ge(sem["v"], tick("v"))
            nb = inc(t.matmul(psB[:, 0:512], ones1f[:, 0:128], rowA[:, 0:512],
                              start=True, stop=True), "t")
            v.wait_ge(sem["t"], nb)
            inc(v.tensor_copy(rsrep[:, qt * 512:(qt + 1) * 512], psB[:, 0:512]), "v")
        rs_done_v = tick("v")

        if upto <= 1:
            y.wait_ge(sem["dz"], tick("dz")); y.wait_ge(sem["dy"], tick("dy"))
            return ctx.finalize() if own else None if own else None
        g.wait_ge(sem["dz"], outw_ready)
        evac_sg = [0, 0]
        zt1 = scrA
        gsig = scrD
        for oc in range(16):
            wt = gwtile[oc % 2]
            g.wait_ge(sem["t"], tick("t"))
            g.wait_ge(sem["s"], tick("s"))
            ldg = inc(g.dma_start(
                out=wt.rearrange("p (f o) -> p f o", o=128),
                in_=gatewT_d.rearrange("p (f o) -> p f o", o=128)[:, oc::16, :]), "dg", 16)
            t.wait_ge(sem["dg"], ldg)
            for tt in range(NT4):
                pb = (psA if tt % 2 == 0 else psB)[:, 0:512]
                t.wait_ge(sem["s"], evac_sg[tt % 2])
                mm = None
                for fc in range(16):
                    mm = t.matmul(pb, wt[:, fc * 128:(fc + 1) * 128],
                                  bufA[:, fc * TPC + tt * 512: fc * TPC + (tt + 1) * 512],
                                  start=(fc == 0), stop=(fc == 15))
                n_g = inc(mm, "t")
                s.wait_ge(sem["t"], n_g)
                if oc == 0 and tt == 0:
                    s.wait_ge(sem["v"], tick("v"))
                evac_sg[tt % 2] = inc(
                    s.activation(gsig[:, tt * 512:(tt + 1) * 512], pb, AF.Sigmoid), "s")
            n_gs = tick("s")
            v.wait_ge(sem["s"], n_gs)
            v.tensor_tensor(zt1[:], bufB[:, oc * TPC:(oc + 1) * TPC], rsrep, ALU.mult)
            v.tensor_scalar(zt1[:], zt1[:], ncol[:, oc:oc + 1], None, ALU.mult)
            inc(v.tensor_tensor(bufB[:, oc * TPC:(oc + 1) * TPC], zt1[:], gsig[:],
                                ALU.mult), "v")
        z_done_v = tick("v")
        if DEBUG_B:
            y.wait_ge(sem["v"], z_done_v)
            inc(y.dma_start(out=zdbg_d[:, :], in_=bufB[:]), "dz", 16)

        if upto <= 2:
            y.wait_ge(sem["dz"], tick("dz")); y.wait_ge(sem["dy"], tick("dy"))
            return ctx.finalize() if own else None if own else None
        silu_g = scrA
        hrow = scrD
        g.wait_ge(sem["v"], z_done_v)
        wgu_last = [0, 0]
        rq = [psB[0:1, 0:512], psB[0:1, 512:1024], psC[0:1, 0:512], psC[0:1, 512:1024]]
        n_rsum = 0
        for ic in range(64):
            wb = wgu[ic % 2]
            g.wait_ge(sem["t"], wgu_last[ic % 2])
            l1 = inc(g.dma_start(
                out=wb[:, 0:2048].rearrange("p (f o) -> p f o", o=128),
                in_=wg8_d.rearrange("p (f i) -> p f i", i=64 * 128)[:, :, ic * 128:(ic + 1) * 128]),
                "dg", 16)
            l2 = inc(g.dma_start(
                out=wb[:, 2048:4096].rearrange("p (f o) -> p f o", o=128),
                in_=wu8_d.rearrange("p (f i) -> p f i", i=64 * 128)[:, :, ic * 128:(ic + 1) * 128]),
                "dg", 16)
            t.wait_ge(sem["dg"], l2)
            for tt in range(NT4):
                pg = psA[:, 0:512]
                pu = psA[:, 512:1024]
                t.wait_ge(sem["s"], tick("s"))
                t.wait_ge(sem["v"], tick("v"))
                for fc in range(16):
                    t.matmul(pg, wb[:, fc * 128:(fc + 1) * 128],
                             bufA[:, fc * TPC + tt * 512: fc * TPC + (tt + 1) * 512],
                             start=(fc == 0), stop=False)
                n_gm = inc(t.matmul(pg, ones1f[:, 0:128],
                                    rowB[:, tt * 512:(tt + 1) * 512],
                                    start=False, stop=True), "t")
                for fc in range(16):
                    t.matmul(pu, wb[:, 2048 + fc * 128:2048 + (fc + 1) * 128],
                             bufA[:, fc * TPC + tt * 512: fc * TPC + (tt + 1) * 512],
                             start=(fc == 0), stop=False)
                n_um = inc(t.matmul(pu, ones1f[:, 0:128],
                                    rowB[:, tt * 512:(tt + 1) * 512],
                                    start=False, stop=True), "t")
                s.wait_ge(sem["t"], n_gm)
                n_sg = inc(s.activation(silu_g[:, tt * 512:(tt + 1) * 512], pg,
                                        AF.Silu, scale=gscol[:, ic:ic + 1]), "s")
                v.wait_ge(sem["t"], n_um)
                v.wait_ge(sem["s"], n_sg)
                inc(v.scalar_tensor_tensor(hrow[:, tt * 512:(tt + 1) * 512], pu,
                                           uscol[:, ic:ic + 1],
                                           silu_g[:, tt * 512:(tt + 1) * 512],
                                           ALU.mult, ALU.mult), "v")
            wgu_last[ic % 2] = tick("t")
            nv_h = tick("v")
            t.wait_ge(sem["v"], nv_h)
            mm = None
            for tt in range(NT4):
                mm = t.matmul(rq[tt], onesb[:], hrow[:, tt * 512:(tt + 1) * 512],
                              start=(ic == 0), stop=(ic == 63))
            n_rsum = inc(mm, "t")
            y.wait_ge(sem["v"], nv_h)
            inc(y.dma_start(out=hw_scr[:, ic * TPC:(ic + 1) * TPC], in_=hrow[:]), "dz", 16)
        hw_done = tick("dz")
        if DEBUG_B:
            y.wait_ge(sem["dz"], hw_done)
            inc(y.dma_start(out=hdbg_d[:, :], in_=hw_scr[:, :]), "dz", 16)
        v.wait_ge(sem["t"], n_rsum)
        for tt in range(NT4):
            inc(v.tensor_scalar(rowA[:, tt * 512:(tt + 1) * 512], rq[tt],
                                -128.0, None, ALU.mult), "v")
        nv_rh = tick("v")
        g.wait_ge(sem["t"], tick("t"))
        lds = inc(g.dma_start(out=rowB[:], in_=dsrow_d[:, :]), "dg", 16)
        t.wait_ge(sem["dg"], lds)
        t.wait_ge(sem["v"], nv_rh)
        for qt in range(4):
            t.wait_ge(sem["v"], tick("v"))
            nb = inc(t.matmul(psB[:, 0:512], ones1f[:, 0:128],
                              rowB[:, qt * 512:(qt + 1) * 512], start=True, stop=True),
                     "t")
            v.wait_ge(sem["t"], nb)
            inc(v.tensor_copy(rsrep[:, qt * 512:(qt + 1) * 512], psB[:, 0:512]), "v")
        ds_done_v = tick("v")
        dsrep = rsrep

        if upto <= 3:
            y.wait_ge(sem["dz"], tick("dz")); y.wait_ge(sem["dy"], tick("dy"))
            return ctx.finalize() if own else None if own else None
        x1f = scrA
        xn1tok = scrB
        x1b = scrD
        psum_free_v = ds_done_v
        hsb_last_t = 0
        owdm_last = 0
        for tc in range(16):
            trow = tc * 128
            y.wait_ge(sem["dz"], hw_done)
            y.wait_ge(sem["t"], hsb_last_t)
            hld = inc(y.dma_start(
                out=hsb[:].rearrange("p (i c) -> p i c", c=128),
                in_=hw_scr.rearrange("p (i c) -> p i c", c=TPC)[:, :, trow:trow + 128]),
                "dy", 16)
            y.wait_ge(sem["v"], tick("v"))
            xld = inc(y.dma_start(out=xtile, in_=x_d[trow:trow + 128, :]), "dy", 16)
            for fc in range(16):
                g.wait_ge(sem["t"], owdm_last)
                lo = inc(g.dma_start(
                    out=owdm[:], in_=outw_scr[:, fc * 2048:(fc + 1) * 2048]), "dg", 16)
                t.wait_ge(sem["dg"], lo)
                if fc == 0:
                    t.wait_ge(sem["v"], psum_free_v)
                    if tc == 0:
                        t.wait_ge(sem["s"], tick("s"))
                mm = None
                for ot in range(4):
                    pa = (psA if ot < 2 else psB)[:, (ot % 2) * 512:(ot % 2) * 512 + 512]
                    mm = t.matmul(pa,
                                  bufB[:, fc * TPC + trow: fc * TPC + trow + 128],
                                  owdm[:, ot * 512:(ot + 1) * 512],
                                  start=(fc == 0), stop=(fc == 15))
                owdm_last = inc(mm, "t")
            for ic in range(64):
                g.wait_ge(sem["t"], owdm_last)
                lw = inc(g.dma_start(
                    out=owdm[:], in_=wd8_d[:, ic * 2048:(ic + 1) * 2048]), "dg", 16)
                t.wait_ge(sem["dg"], lw)
                if ic == 0:
                    t.wait_ge(sem["dy"], hld)
                mm = None
                for ot in range(4):
                    mm = t.matmul(psC[:, ot * 512:(ot + 1) * 512],
                                  hsb[:, ic * 128:(ic + 1) * 128],
                                  owdm[:, ot * 512:(ot + 1) * 512],
                                  start=(ic == 0), stop=False)
                owdm_last = inc(mm, "t")
            mm = None
            for ot in range(4):
                mm = t.matmul(psC[:, ot * 512:(ot + 1) * 512],
                              rowA[:, trow:trow + 128],
                              ones1f[:, 0:512],
                              start=False, stop=True)
            inc(mm, "t")
            n_beta = tick("t")
            hsb_last_t = n_beta
            v.wait_ge(sem["t"], n_beta)
            v.wait_ge(sem["dy"], xld)
            v.tensor_tensor(x1f[:], psC[:], dsrep, ALU.mult)
            v.tensor_tensor(x1f[:, 0:1024], x1f[:, 0:1024], psA[:], ALU.add)
            v.tensor_tensor(x1f[:, 1024:2048], x1f[:, 1024:2048], psB[:], ALU.add)
            nv_x1 = inc(v.tensor_tensor(x1f[:], x1f[:], xtile, ALU.add), "v")
            v.wait_ge(sem["v"], nv_x1)
            nv_x1b = inc(v.tensor_copy(x1b[:], x1f[:]), "v")
            y.wait_ge(sem["v"], nv_x1b)
            inc(y.dma_start(out=x1_d[trow:trow + 128, :], in_=x1b[:]), "dz", 16)
            v.reduce_sum(st_[:, 0:1], x1f[:], axis=AX)
            v.tensor_mul(xn1tok[:], x1f[:], x1f[:])
            n_v = inc(v.reduce_sum(st_[:, 1:2], xn1tok[:], axis=AX), "v")
            v.wait_ge(sem["v"], n_v)
            v.tensor_scalar(st_[:, 2:3], st_[:, 0:1], 1.0 / D, None, ALU.mult)
            n_v = inc(v.tensor_scalar(st_[:, 3:4], st_[:, 1:2], 1.0 / D, None,
                                      ALU.mult), "v")
            v.wait_ge(sem["v"], n_v)
            n_v = inc(v.tensor_mul(st_[:, 4:5], st_[:, 2:3], st_[:, 2:3]), "v")
            v.wait_ge(sem["v"], n_v)
            n_v = inc(v.scalar_tensor_tensor(st_[:, 4:5], st_[:, 3:4], float(EPS),
                                             st_[:, 4:5], ALU.add, ALU.subtract), "v")
            s.wait_ge(sem["v"], n_v)
            s.activation(st_[:, 5:6], st_[:, 4:5], AF.Ln, scale=1.0)
            n_s = inc(s.activation(st_[:, 5:6], st_[:, 5:6], AF.Exp, scale=-0.5), "s")
            v.wait_ge(sem["s"], n_s)
            n_v = inc(v.scalar_tensor_tensor(st_[:, 6:7], st_[:, 2:3], -1.0,
                                             st_[:, 5:6], ALU.mult, ALU.mult), "v")
            v.wait_ge(sem["v"], n_v)
            nv_xn1 = inc(v.tensor_scalar(xn1tok[:], x1f[:], st_[:, 5:6], st_[:, 6:7],
                                         ALU.mult, ALU.add), "v")
            t.wait_ge(sem["v"], nv_xn1)
            evts = [nv_xn1, nv_xn1]
            for fc in range(16):
                pb = (psA if fc % 2 == 0 else psB)[:, 0:128]
                t.wait_ge(sem["v"], evts[fc % 2])
                n_t = inc(t.transpose(pb, xn1tok[:, fc * 128:(fc + 1) * 128],
                                      ident[:]), "t")
                v.wait_ge(sem["t"], n_t)
                evts[fc % 2] = inc(v.tensor_copy(
                    bufA[:, fc * TPC + trow: fc * TPC + trow + 128], pb), "v")
            nv_strip = tick("v")
            t.wait_ge(sem["v"], nv_strip)
            mm = None
            for fc in range(16):
                mm = t.matmul(psC[0:8, 0:128], gwsb[:, fc * 8:(fc + 1) * 8],
                              bufA[:, fc * TPC + trow: fc * TPC + trow + 128],
                              start=(fc == 0), stop=(fc == 15))
            n_sc = inc(mm, "t")
            v.wait_ge(sem["t"], n_sc)
            sc_row = xn1tok[0:8, 128:256]
            n_v = inc(v.tensor_copy(sc_row, psC[0:8, 0:128]), "v")
            t.wait_ge(sem["v"], n_v)
            n_t = inc(t.transpose(psA[0:128, 0:8], sc_row, ident[0:8, 0:8]), "t")
            v.wait_ge(sem["t"], n_t)
            n_v = inc(v.tensor_copy(sctok[:], psA[0:128, 0:8]), "v")
            v.wait_ge(sem["v"], n_v)
            n_v = inc(v.tensor_scalar(r8c[:], sctok[:], -1.0, None, ALU.mult), "v")
            v.wait_ge(sem["v"], n_v)
            n_v = inc(v.tensor_tensor(r8a[:], sctok[:], r8c[:], ALU.max), "v")
            s.wait_ge(sem["v"], n_v)
            s.activation(r8a[:], r8a[:], AF.Exp, scale=-1.0)
            n_s = inc(s.activation(r8a[:], r8a[:], AF.Ln, bias=1.0, scale=1.0), "s")
            v.wait_ge(sem["s"], n_s)
            n_v = inc(v.tensor_scalar(r8b[:], sctok[:], 0.0, None, ALU.max), "v")
            v.wait_ge(sem["v"], n_v)
            n_v = inc(v.tensor_add(r8a[:], r8a[:], r8b[:]), "v")
            s.wait_ge(sem["v"], n_v)
            s.activation(r8a[:], r8a[:], AF.Ln, scale=1.0)
            n_s = inc(s.activation(r8a[:], r8a[:], AF.Exp, scale=0.5), "s")
            v.wait_ge(sem["s"], n_s)
            n_v = inc(v.tensor_add(r8b[:], r8a[:], gbrep[:]), "v")
            v.wait_ge(sem["v"], n_v)
            n_v = inc(v.reduce_max(st_[:, 0:1], r8b[:], axis=AX), "v")
            v.wait_ge(sem["v"], n_v)
            n_v = inc(v.tensor_scalar(r8c[:], r8b[:], st_[:, 0:1], None, ALU.is_equal),
                      "v")
            v.wait_ge(sem["v"], n_v)
            n_v = inc(v.scalar_tensor_tensor(r8b[:], r8c[:], -1e30, r8b[:],
                                             ALU.mult, ALU.add), "v")
            v.wait_ge(sem["v"], n_v)
            n_v = inc(v.reduce_max(st_[:, 1:2], r8b[:], axis=AX), "v")
            v.wait_ge(sem["v"], n_v)
            n_v = inc(v.tensor_scalar(r8b[:], r8b[:], st_[:, 1:2], None, ALU.is_equal),
                      "v")
            v.wait_ge(sem["v"], n_v)
            n_v = inc(v.tensor_add(r8c[:], r8c[:], r8b[:]), "v")
            v.wait_ge(sem["v"], n_v)
            n_v = inc(v.tensor_mul(cwtok[:], r8c[:], r8a[:]), "v")
            t.wait_ge(sem["v"], n_v)
            n_t = inc(t.transpose(psA[0:8, 0:128], cwtok[:], ident[:]), "t")
            v.wait_ge(sem["t"], n_t)
            v.wait_ge(sem["dz"], tick("dz"))
            cw_row = xn1tok[0:8, 0:128]
            psum_free_v = inc(v.tensor_copy(cw_row, psA[0:8, 0:128]), "v")
            y.wait_ge(sem["v"], psum_free_v)
            inc(y.dma_start(out=cwrows_d[:, trow:trow + 128], in_=cw_row), "dz", 16)

        y.wait_ge(sem["v"], tick("v"))
        inc(y.dma_start(out=xn1T_d[:, :], in_=bufA[:]), "dz", 16)

        if upto <= 4:
            y.wait_ge(sem["dz"], tick("dz")); y.wait_ge(sem["dy"], tick("dy"))
            return ctx.finalize() if own else None if own else None
        cvq = scrA
        for dcc in range(12):
            wsrc = wdkvT_d if dcc < 4 else wdqT_d
            nch = 4 if dcc < 4 else 8
            wofs = dcc if dcc < 4 else dcc - 4
            wt = gwtile[dcc % 2]
            g.wait_ge(sem["t"], tick("t"))
            g.wait_ge(sem["dz"], tick("dz"))
            ldw = inc(g.dma_start(
                out=wt[:, 0:16 * 128].rearrange("p (f o) -> p f o", o=128),
                in_=wsrc.rearrange("p (f o) -> p f o", o=128)[:, wofs::nch, :]), "dg", 16)
            t.wait_ge(sem["dg"], ldw)
            v.wait_ge(sem["dz"], tick("dz"))
            for tt in range(NT4):
                pb = (psA if tt % 2 == 0 else psB)[:, 0:512]
                t.wait_ge(sem["v"], tick("v"))
                mm = None
                for fc in range(16):
                    mm = t.matmul(pb, wt[:, fc * 128:(fc + 1) * 128],
                                  bufA[:, fc * TPC + tt * 512: fc * TPC + (tt + 1) * 512],
                                  start=(fc == 0), stop=(fc == 15))
                n_t = inc(mm, "t")
                v.wait_ge(sem["t"], n_t)
                inc(v.tensor_scalar(cvq[:, tt * 512:(tt + 1) * 512], pb,
                                    bkvcol[:, dcc:dcc + 1], None, ALU.add), "v")
            nvp = tick("v")
            g.wait_ge(sem["v"], nvp)
            if dcc < 4:
                inc(g.dma_start(out=ckvT_d[:, dcc * TPC:(dcc + 1) * TPC], in_=cvq[:]),
                    "dz", 16)
            else:
                inc(g.dma_start(out=cqT_d[:, (dcc - 4) * TPC:(dcc - 3) * TPC],
                                in_=cvq[:]), "dz", 16)
        if upto <= 5:
            y.wait_ge(sem["dz"], tick("dz")); y.wait_ge(sem["dy"], tick("dy"))
            return ctx.finalize() if own else None if own else None
        g.wait_ge(sem["t"], tick("t"))
        g.wait_ge(sem["v"], tick("v"))
        nwkr = inc(g.dma_start(out=bufB[:], in_=wkrT_d[:, :]), "dg", 16)
        lbk = inc(g.dma_start(out=rsrep, in_=bkr_rep_d[:, :]), "dg", 16)
        ropeC = owdm[:].bitcast(F32)[:, 0:128]
        ropeS = owdm[:].bitcast(F32)[:, 128:256]
        bkrrep = rsrep
        krtok = scrA
        krsw = scrB
        krTb = scrD
        kr_ev = tick("v")
        t.wait_ge(sem["dg"], lbk)
        for tc in range(16):
            trow = tc * 128
            g.wait_ge(sem["v"], tick("v"))
            lr1 = inc(g.dma_start(out=ropeC, in_=ropeC_d[trow:trow + 128, :]), "dg", 16)
            lr2 = inc(g.dma_start(out=ropeS, in_=ropeS_d[trow:trow + 128, :]), "dg", 16)
            t.wait_ge(sem["dg"], lr2)
            t.wait_ge(sem["v"], kr_ev)
            mm = None
            for ot in range(4):
                for fc in range(16):
                    mm = t.matmul(psC[:, ot * 512:(ot + 1) * 512],
                                  bufA[:, fc * TPC + trow: fc * TPC + trow + 128],
                                  bufB[:, fc * TPC + ot * 512: fc * TPC + (ot + 1) * 512],
                                  start=(fc == 0), stop=(fc == 15))
            n_kr = inc(mm, "t")
            v.wait_ge(sem["t"], n_kr)
            v.wait_ge(sem["dz"], tick("dz"))
            kr_ev = inc(v.tensor_tensor(krtok[:], psC[:], bkrrep, ALU.add), "v")
            v.wait_ge(sem["v"], kr_ev)
            v.tensor_scalar(krsw[:, 0:D:2], krtok[:, 1:D:2], -1.0, None, ALU.mult)
            n_v = inc(v.tensor_copy(krsw[:, 1:D:2], krtok[:, 0:D:2]), "v")
            v.wait_ge(sem["v"], n_v)
            last = None
            for hh in range(16):
                sl = slice(hh * 128, (hh + 1) * 128)
                v.tensor_tensor(krtok[:, sl], krtok[:, sl], ropeC, ALU.mult)
                last = v.tensor_tensor(krsw[:, sl], krsw[:, sl], ropeS, ALU.mult)
            n_v = inc(last, "v")
            v.wait_ge(sem["v"], n_v)
            nv_rope = inc(v.tensor_tensor(krtok[:], krtok[:], krsw[:], ALU.add), "v")
            t.wait_ge(sem["v"], nv_rope)
            evts = [nv_rope, nv_rope]
            for fc in range(16):
                pb = (psA if fc % 2 == 0 else psB)[:, 0:128]
                t.wait_ge(sem["v"], evts[fc % 2])
                n_t = inc(t.transpose(pb, krtok[:, fc * 128:(fc + 1) * 128], ident[:]),
                          "t")
                v.wait_ge(sem["t"], n_t)
                evts[fc % 2] = inc(
                    v.tensor_copy(krTb[:, fc * 128:(fc + 1) * 128], pb), "v")
            nv_kr = tick("v")
            y.wait_ge(sem["v"], nv_kr)
            inc(y.dma_start(
                out=krT_d.rearrange("p (f t) -> p f t", t=TPC)[:, :, trow:trow + 128],
                in_=krTb[:].rearrange("p (f t) -> p f t", t=128)), "dz", 16)
        y.wait_ge(sem["dz"], tick("dz"))
        y.wait_ge(sem["dy"], tick("dy"))
        return ctx.finalize() if own else None



def build_C(ctx=None):
    own = ctx is None
    with (Ctx("C") if own else ctx) as ctx:
        ckvT_d = ctx.din("ckvT", [128, 4 * S], BF16)
        cqT_d = ctx.din("cqT", [128, 8 * S], BF16)
        krT_d = ctx.din("krT", [128, 8 * S], BF16)
        wukT_d = ctx.din("wukT", [128, 4 * 8 * 128], F32)
        wuvT_d = ctx.din("wuvT", [128, 4 * 8 * 128], F32)
        wuqT_d = ctx.din("wuqT", [128, 8 * 8 * 128], F32)
        buk_d = ctx.din("bukcol", [128, 8], F32)
        buvrep_d = ctx.din("buvrep", [128, 8 * 128], F32)
        buq_d = ctx.din("buqcol", [128, 8], F32)
        ropeCF_d = ctx.din("ropeCF", [S, DH], F32)
        ropeSF_d = ctx.din("ropeSF", [S, DH], F32)
        dmask_d = ctx.din("dmask", [4 * 128, 512], F32)
        identin = ctx.din("identin", [128, 128], F32)
        selcol_d = ctx.din("selcol", [128, 2], F32)
        oT_d = ctx.dout("oT", [128, 8 * S], BF16)

        ctx.alloc_sems(["t", "v", "s", "dy", "dg", "dz"])
        t, v, s, g, y = ctx.t, ctx.v, ctx.s, ctx.g, ctx.y
        sem, inc, tick = ctx.sem, ctx.inc, ctx.tick

        ckv = ctx.sb("ckv", [128, 4 * S], BF16)
        cq = ctx.sb("cq", [128, 8 * S], BF16)
        kC = ctx.sb("kC", [128, S], BF16)
        kR = ctx.sb("kR", [128, S], BF16)
        qC = ctx.sb("qC", [128, S], BF16)
        qR = ctx.sb("qR", [128, S], BF16)
        vsb = ctx.sb("vsb", [128, S], BF16)
        oT = ctx.sb("oT_sb", [128, S], BF16)
        wuk = ctx.sb("wuk", [128, 4 * 8 * 128], BF16)
        wuv = ctx.sb("wuv", [128, 4 * 8 * 128], BF16)
        wuq = ctx.sb("wuq", [128, 8 * 8 * 128], BF16)
        bukc = ctx.sb("bukc", [128, 8], F32)
        buvrep = ctx.sb("buvrep_sb", [128, 8 * 128], F32)
        buqc = ctx.sb("buqc", [128, 8], F32)
        dmask = ctx.sb("dmask_sb", [128, 4 * 512], BF16)
        ident = ctx.sb("identsb", [128, 128], F32)
        identb = ctx.sb("identbsb", [128, 128], BF16)
        ropeC = ctx.sb("ropeCsb", [128, DH], F32)
        ropeS = ctx.sb("ropeSsb", [128, DH], F32)
        qtok = ctx.sb("qtok", [128, 128], BF16)
        qsw = ctx.sb("qsw", [128, 128], F32)
        qrt = ctx.sb("qrt", [128, 128], BF16)
        ptile = [ctx.sb(f"ptile{i}", [128, 512], BF16) for i in range(3)]
        onesb = ctx.sb("onesbc", [128, 1], BF16)
        ones1f = ctx.sb("ones1fc", [1, 128], F32)
        sqrow = ctx.sb("sqrow", [128, 512], BF16)
        nrow = ctx.sb("nrow", [1, 512], F32)
        sc1 = ctx.sb("sc1", [1, 8], F32)
        selcolC = ctx.sb("selcolC", [128, 2], F32)
        mcol = ctx.sb("mcol", [128, 1], F32)
        zrow = ctx.sb("zrow", [1, 512], F32)
        zrep = ctx.sb("zrep", [128, 512], F32)

        psS = ctx.ps("psS", [128, 1024], F32)
        psO = ctx.ps("psO", [128, 512], F32)
        psK = ctx.ps("psK", [128, 512], F32)
        psZ = ctx.ps("psZ", [1, 512], F32)
        psD = ctx.ps("psD", [128, 256], BF16)

        SCALE = float(1.0 / np.sqrt(np.float32(2 * DH)).astype(np.float32))

        FUSED_C = ckvT_d.shape[0] == 256
        if FUSED_C:
            for nch, dst, srcd in ((4, ckv, ckvT_d), (8, cq, cqT_d)):
                inc(y.dma_start(
                    out=dst[:].rearrange("p (c t) -> p c t", t=S)[:, :, 0:TPC],
                    in_=srcd[0:128, :].rearrange("p (c t) -> p c t", t=TPC)), "dy", 16)
                inc(y.dma_start(
                    out=dst[:].rearrange("p (c t) -> p c t", t=S)[:, :, TPC:S],
                    in_=srcd[128:256, :].rearrange("p (c t) -> p c t", t=TPC)), "dy", 16)
        else:
            inc(y.dma_start(out=ckv[:], in_=ckvT_d[:, :]), "dy", 16)
            inc(y.dma_start(out=cq[:], in_=cqT_d[:, :]), "dy", 16)
        inc(g.dma_start(out=selcolC[:], in_=selcol_d[:, :]), "dg", 16)
        for ap_sb, ap_d in ((wuk, wukT_d), (wuv, wuvT_d), (wuq, wuqT_d),
                            (ident, identin)):
            inc(g.dma_start(out=ap_sb[:], in_=ap_d[:, :]), "dg", 16)
        inc(g.dma_start(out=dmask[:].rearrange("k (o q) -> k o q", q=512),
                        in_=dmask_d.rearrange("(o k) q -> k o q", k=128)), "dg", 16)
        for ap_sb, ap_d in ((bukc, buk_d), (buvrep, buvrep_d), (buqc, buq_d)):
            inc(g.dma_start(out=ap_sb[:], in_=ap_d[:, :]), "dg", 16)
        ld0 = tick("dg")
        ldy0 = tick("dy")
        v.wait_ge(sem["dg"], ld0)
        v.wait_ge(sem["dy"], ldy0)
        v.memset(onesb[:], 1.0)
        v.memset(ones1f[:], 1.0)
        nv0 = inc(v.tensor_copy(identb[:], ident[:]), "v")

        for hl in range(8):
            t_prev = tick("t")
            g.wait_ge(sem["t"], t_prev)
            if FUSED_C:
                g.wait_ge(sem["v"], tick("v"))
                for cand, dst in ((hl, kR), (8 + hl, qR)):
                    inc(g.dma_start(out=dst[:, 0:TPC],
                                    in_=krT_d[0:128, cand * TPC:(cand + 1) * TPC]),
                        "dg", 16)
                    inc(g.dma_start(out=dst[:, TPC:S],
                                    in_=krT_d[128:256, cand * TPC:(cand + 1) * TPC]),
                        "dg", 16)
                krl0 = tick("dg")
                v.wait_ge(sem["dg"], krl0)
                v.tensor_scalar(kR[:], kR[:], selcolC[:, 0:1], None, ALU.mult)
                nbl = inc(v.scalar_tensor_tensor(kR[:], qR[:], selcolC[:, 1:2], kR[:],
                                                 ALU.mult, ALU.add), "v")
                t.wait_ge(sem["v"], nbl)
                krl = 0
            else:
                krl = inc(g.dma_start(out=kR[:], in_=krT_d[:, hl * S:(hl + 1) * S]), "dg", 16)
            evk = [tick("v")] * 2
            for tt in range(8):
                pb = psS[:, (tt % 2) * 512:(tt % 2) * 512 + 512]
                t.wait_ge(sem["v"], evk[tt % 2])
                if hl == 0 and tt == 0:
                    t.wait_ge(sem["v"], nv0)
                mm = None
                for dcc in range(4):
                    mm = t.matmul(pb, wuk[:, (dcc * 8 + hl) * 128:(dcc * 8 + hl + 1) * 128],
                                  ckv[:, dcc * S + tt * 512: dcc * S + (tt + 1) * 512],
                                  start=(dcc == 0), stop=(dcc == 3))
                n_t = inc(mm, "t")
                v.wait_ge(sem["t"], n_t)
                evk[tt % 2] = inc(v.tensor_scalar(kC[:, tt * 512:(tt + 1) * 512], pb,
                                                  bukc[:, hl:hl + 1], None, ALU.add), "v")
            for tt in range(8):
                pb = psS[:, (tt % 2) * 512:(tt % 2) * 512 + 512]
                t.wait_ge(sem["v"], evk[tt % 2])
                mm = None
                for dcc in range(8):
                    mm = t.matmul(pb, wuq[:, (dcc * 8 + hl) * 128:(dcc * 8 + hl + 1) * 128],
                                  cq[:, dcc * S + tt * 512: dcc * S + (tt + 1) * 512],
                                  start=(dcc == 0), stop=(dcc == 7))
                n_t = inc(mm, "t")
                v.wait_ge(sem["t"], n_t)
                evk[tt % 2] = inc(v.tensor_scalar(qC[:, tt * 512:(tt + 1) * 512], pb,
                                                  buqc[:, hl:hl + 1], None, ALU.add), "v")
            evv = tick("v")
            for tc in range(32):
                t.wait_ge(sem["v"], evv)
                mm = None
                for dcc in range(4):
                    mm = t.matmul(psK[:, 0:128],
                                  ckv[:, dcc * S + tc * 128: dcc * S + (tc + 1) * 128],
                                  wuv[:, (dcc * 8 + hl) * 128:(dcc * 8 + hl + 1) * 128],
                                  start=(dcc == 0), stop=(dcc == 3))
                n_t = inc(mm, "t")
                v.wait_ge(sem["t"], n_t)
                evv = inc(v.tensor_tensor(vsb[:, tc * 128:(tc + 1) * 128], psK[:, 0:128],
                                          buvrep[:, hl * 128:(hl + 1) * 128], ALU.add),
                          "v")
            ev_q = tick("v")
            for tc in range(32):
                g.wait_ge(sem["v"], ev_q)
                lr1 = inc(g.dma_start(out=ropeC[:], in_=ropeCF_d[tc * 128:(tc + 1) * 128, :]), "dg", 16)
                lr2 = inc(g.dma_start(out=ropeS[:], in_=ropeSF_d[tc * 128:(tc + 1) * 128, :]), "dg", 16)
                t.wait_ge(sem["v"], ev_q)
                n_t = inc(t.transpose(psD[:, 0:128], qC[:, tc * 128:(tc + 1) * 128],
                                      identb[:]), "t")
                v.wait_ge(sem["t"], n_t)
                nv = inc(v.tensor_copy(qtok[:], psD[:, 0:128]), "v")
                v.wait_ge(sem["v"], nv)
                v.wait_ge(sem["dg"], lr2)
                v.tensor_scalar(qsw[:, 0:128:2], qtok[:, 1:128:2], -1.0, None, ALU.mult)
                nv = inc(v.tensor_copy(qsw[:, 1:128:2], qtok[:, 0:128:2]), "v")
                v.wait_ge(sem["v"], nv)
                v.tensor_tensor(qsw[:], qsw[:], ropeS[:], ALU.mult)
                nv = inc(v.scalar_tensor_tensor(qrt[:], qtok[:], 1.0, ropeC[:],
                                                ALU.mult, ALU.mult), "v")
                v.wait_ge(sem["v"], nv)
                nv = inc(v.tensor_tensor(qrt[:], qrt[:], qsw[:], ALU.add), "v")
                t.wait_ge(sem["v"], nv)
                n_t = inc(t.transpose(psD[:, 128:256], qrt[:], identb[:]), "t")
                v.wait_ge(sem["t"], n_t)
                ev_q = inc(v.tensor_copy(qR[:, tc * 128:(tc + 1) * 128], psD[:, 128:256]),
                           "v")
            nrm_parts = [(qC, qR, 0), (kC, kR, 1)]
            for (Ca, Ra, idx) in nrm_parts:
                for tt in range(8):
                    sl = slice(tt * 512, (tt + 1) * 512)
                    ns = inc(s.activation(sqrow[:], Ca[:, sl], AF.Square), "s")
                    t.wait_ge(sem["s"], ns)
                    if tt == 0:
                        t.wait_ge(sem["v"], tick("v"))
                    n_t = inc(t.matmul(psZ[:], onesb[:], sqrow[:],
                                       start=True, stop=False), "t")
                    s.wait_ge(sem["t"], n_t)
                    ns = inc(s.activation(sqrow[:], Ra[:, sl], AF.Square), "s")
                    t.wait_ge(sem["s"], ns)
                    n_t = inc(t.matmul(psZ[:], onesb[:], sqrow[:],
                                       start=False, stop=True), "t")
                    s.wait_ge(sem["t"], n_t)
                    v.wait_ge(sem["t"], n_t)
                    nv = inc(v.tensor_copy(nrow[:], psZ[:]), "v")
                    v.wait_ge(sem["v"], nv)
                    nv = inc(v.reduce_max(sc1[:, 4 + idx:5 + idx], nrow[:], axis=AX), "v")
                    v.wait_ge(sem["v"], nv)
                    if tt == 0:
                        nv = inc(v.tensor_copy(sc1[:, idx:idx + 1],
                                               sc1[:, 4 + idx:5 + idx]), "v")
                    else:
                        nv = inc(v.tensor_tensor(sc1[:, idx:idx + 1],
                                                 sc1[:, idx:idx + 1],
                                                 sc1[:, 4 + idx:5 + idx], ALU.max), "v")
                    v.wait_ge(sem["v"], nv)
                    t.wait_ge(sem["v"], nv)
            s.wait_ge(sem["v"], tick("v"))
            s.activation(sc1[:, 0:1], sc1[:, 0:1], AF.Ln, scale=1.0)
            ns = inc(s.activation(sc1[:, 1:2], sc1[:, 1:2], AF.Ln, scale=1.0), "s")
            v.wait_ge(sem["s"], ns)
            nv = inc(v.tensor_add(sc1[:, 2:3], sc1[:, 0:1], sc1[:, 1:2]), "v")
            s.wait_ge(sem["v"], nv)
            ns = inc(s.activation(sc1[:, 2:3], sc1[:, 2:3], AF.Exp, scale=0.5), "s")
            v.wait_ge(sem["s"], ns)
            nv = inc(v.tensor_scalar(sc1[:, 3:4], sc1[:, 2:3], -SCALE, None, ALU.mult),
                     "v")
            t.wait_ge(sem["v"], nv)
            n_t = inc(t.matmul(psK[:, 0:1], ones1f[:, 0:128], sc1[:, 3:4],
                               start=True, stop=True), "t")
            v.wait_ge(sem["t"], n_t)
            nv_m = inc(v.tensor_copy(mcol[:], psK[:, 0:1]), "v")

            t.wait_ge(sem["dg"], krl)
            ev_o = tick("v")
            for qt in range(8):
                nk = 4 * (qt + 1)
                pidx = 0
                pt_free = [tick("t")] * 3
                for kc in range(nk):
                    half = kc % 2
                    pbS = psS[:, half * 512: half * 512 + 512]
                    t.wait_ge(sem["s"], max(0, tick("s") - 1))
                    mm = t.matmul(pbS, kC[:, kc * 128:(kc + 1) * 128],
                                  qC[:, qt * 512:(qt + 1) * 512], start=True, stop=False)
                    mm = t.matmul(pbS, kR[:, kc * 128:(kc + 1) * 128],
                                  qR[:, qt * 512:(qt + 1) * 512], start=False, stop=True)
                    n_S = inc(mm, "t")
                    pt = ptile[pidx]
                    s.wait_ge(sem["t"], n_S)
                    s.wait_ge(sem["t"], pt_free[pidx])
                    if kc == 0 and qt == 0:
                        s.wait_ge(sem["v"], nv_m)
                    n_e = inc(s.activation(pt[:], pbS, AF.Exp, bias=mcol[:],
                                           scale=SCALE), "s")
                    off = kc - 4 * qt
                    if off >= 0:
                        v.wait_ge(sem["s"], n_e)
                        n_e_v = inc(v.tensor_tensor(pt[:], pt[:],
                                                    dmask[:, off * 512:(off + 1) * 512],
                                                    ALU.mult), "v")
                        t.wait_ge(sem["v"], n_e_v)
                    else:
                        t.wait_ge(sem["s"], n_e)
                    mm = t.matmul(psZ[:], onesb[:], pt[:],
                                  start=(kc == 0), stop=(kc == nk - 1))
                    mm = t.matmul(psO[:], vsb[:, kc * 128:(kc + 1) * 128], pt[:],
                                  start=(kc == 0), stop=(kc == nk - 1))
                    pt_free[pidx] = inc(mm, "t")
                    pidx = (pidx + 1) % 3
                n_fl = tick("t")
                v.wait_ge(sem["t"], n_fl)
                nv = inc(v.tensor_copy(zrow[:], psZ[:]), "v")
                v.wait_ge(sem["v"], nv)
                nv = inc(v.reciprocal(zrow[:], zrow[:]), "v")
                t.wait_ge(sem["v"], nv)
                n_t = inc(t.matmul(psK[:, 0:512], ones1f[:], zrow[:],
                                   start=True, stop=True), "t")
                v.wait_ge(sem["t"], n_t)
                nv = inc(v.tensor_copy(zrep[:], psK[:, 0:512]), "v")
                v.wait_ge(sem["v"], nv)
                ev_o = inc(v.tensor_tensor(oT[:, qt * 512:(qt + 1) * 512], psO[:],
                                           zrep[:], ALU.mult), "v")
                t.wait_ge(sem["v"], ev_o)
            y.wait_ge(sem["v"], ev_o)
            if oT_d.shape[0] == 256:
                inc(y.dma_start(out=oT_d[0:128, hl * TPC:(hl + 1) * TPC],
                                in_=oT[:, 0:TPC]), "dz", 16)
                inc(y.dma_start(out=oT_d[128:256, hl * TPC:(hl + 1) * TPC],
                                in_=oT[:, TPC:S]), "dz", 16)
            else:
                inc(y.dma_start(out=oT_d[:, hl * S:(hl + 1) * S], in_=oT[:]), "dz", 16)
            v.wait_ge(sem["dz"], tick("dz"))
        y.wait_ge(sem["dz"], tick("dz"))
        y.wait_ge(sem["dy"], tick("dy"))
        return ctx.finalize() if own else None



def build_D(ctx=None):
    NT4 = 4
    NIC = 11
    own = ctx is None
    with (Ctx("D") if own else ctx) as ctx:
        x1_d = ctx.din("x1", [TPC, D], BF16)
        xn1T_d = ctx.din("xn1T", [128, 16 * TPC], BF16)
        oT_d = ctx.din("oT", [128, 16 * TPC], BF16)
        cwrows_d = ctx.din("cwrows", [8, TPC], F32)
        wOT_d = ctx.din("wOT", [128, 16 * 16 * 128], F32)
        bOrow_d = ctx.din("bOrow", [1, D], F32)
        w1all_d = ctx.din("w1all", [128, NE * 16 * NIC * 128], BF16)
        w3all_d = ctx.din("w3all", [128, NE * 16 * NIC * 128], BF16)
        w2all_d = ctx.din("w2all", [128, NE * NIC * 16 * 128], BF16)
        sw1_d = ctx.din("sw1", [128, 16 * NIC * 128], BF16)
        sw3_d = ctx.din("sw3", [128, 16 * NIC * 128], BF16)
        sw2_d = ctx.din("sw2", [128, NIC * 16 * 128], BF16)
        selcol_d = ctx.din("selcol", [128, 2], F32)
        identin_d = ctx.din("identin", [128, 128], F32)
        out_d = ctx.dout("out", [TPC, D], F32)

        wO_scr = ctx.nc.dram_tensor("wO_scr", [128, 16 * 16 * 128], BF16).ap()

        ctx.alloc_sems(["t", "v", "s", "dy", "dg", "dz"])
        t, v, s, g, y = ctx.t, ctx.v, ctx.s, ctx.g, ctx.y
        sem, inc, tick = ctx.sem, ctx.inc, ctx.tick

        xn1T = ctx.sb("xn1Tsb", [128, 16 * TPC], BF16)
        hsb_e = ctx.sb("hsb_e", [128, NIC * TPC], BF16)
        a1_scr = ctx.nc.dram_tensor(ctx.prefix + "a1_scr", [TPC, D], BF16).ap()
        wst = [ctx.sb("wstD0", [128, 2 * 16 * 128], BF16)]
        otile = hsb_e[:, 0:2048]
        x1t = hsb_e[:, 0:2048]
        a1t = hsb_e[:, 2048:4096]
        outf = hsb_e[:].bitcast(F32)[:, 2048:4096]
        cwrep = ctx.sb("cwrepD", [128, TPC], BF16)
        ysbT = ctx.sb("ysbT", [128, 16 * TPC], BF16)
        w2q = ctx.sb("w2q", [128, NIC * 4 * 128], BF16)
        wmov = w2q[:, 0:2048]
        silu_g = ctx.sb("silugD", [128, 512], F32)
        tf32 = silu_g[:, 0:128]
        rowD = ctx.sb("rowD", [1, D], F32)
        bOrow = rowD
        cw1 = rowD
        ones1f = ctx.sb("ones1fD", [1, 128], F32)
        selcolD = ctx.sb("selcolD", [128, 2], F32)
        ident_f = ctx.sb("identfD", [128, 128], F32)

        psA = ctx.ps("psAD", [128, 1024], F32)
        psB = ctx.ps("psBD", [128, 1024], F32)
        psC = ctx.ps("psCD", [128, 2048], F32)

        FUSED_D = oT_d.shape[0] == 512
        nv_yz = inc(v.memset(ysbT[:], 0.0), "v")
        inc(y.dma_start(out=xn1T[:], in_=xn1T_d[:, :]), "dy", 16)
        inc(g.dma_start(out=bOrow[:], in_=bOrow_d[:, :]), "dg", 16)
        inc(g.dma_start(out=selcolD[:], in_=selcol_d[:, :]), "dg", 16)
        inc(g.dma_start(out=ident_f[:], in_=identin_d[:, :]), "dg", 16)
        ldc = tick("dg")
        v.wait_ge(sem["dg"], ldc)
        nv0 = inc(v.memset(ones1f[:], 1.0), "v")
        for fc in range(16):
            dst = wst[0][:, 0:2048]
            g.wait_ge(sem["dz"], max(0, fc) * 16)
            ld = inc(g.dma_start(out=dst, in_=wOT_d[:, fc * 2048:(fc + 1) * 2048]),
                     "dg", 16)
            y.wait_ge(sem["dg"], ld)
            inc(y.dma_start(out=wO_scr[:, fc * 2048:(fc + 1) * 2048], in_=dst), "dz", 16)
        wo_ready = tick("dz")
        xn1_ready = 16

        ev_y = nv0
        for tc in range(16):
            trow = tc * 128
            y.wait_ge(sem["t"], tick("t"))
            if FUSED_D:
                osc = hsb_e[:, 4096:6144]
                y.wait_ge(sem["v"], tick("v"))
                inc(y.dma_start(
                    out=otile[:].rearrange("p (f c) -> p f c", c=128)[:, 0:8, :],
                    in_=oT_d[0:128, :].rearrange("p (f t) -> p f t", t=TPC)[:, :, trow:trow + 128]), "dy", 16)
                inc(y.dma_start(
                    out=otile[:].rearrange("p (f c) -> p f c", c=128)[:, 8:16, :],
                    in_=oT_d[256:384, :].rearrange("p (f t) -> p f t", t=TPC)[:, :, trow:trow + 128]), "dy", 16)
                inc(y.dma_start(
                    out=osc.rearrange("p (f c) -> p f c", c=128)[:, 0:8, :],
                    in_=oT_d[128:256, :].rearrange("p (f t) -> p f t", t=TPC)[:, :, trow:trow + 128]), "dy", 16)
                lo4 = inc(y.dma_start(
                    out=osc.rearrange("p (f c) -> p f c", c=128)[:, 8:16, :],
                    in_=oT_d[384:512, :].rearrange("p (f t) -> p f t", t=TPC)[:, :, trow:trow + 128]),
                    "dy", 16)
                lo = lo4
                v.wait_ge(sem["dy"], lo4)
                v.tensor_scalar(otile, otile, selcolD[:, 0:1], None, ALU.mult)
                nbl = inc(v.scalar_tensor_tensor(otile, osc, selcolD[:, 1:2], otile,
                                                 ALU.mult, ALU.add), "v")
                t.wait_ge(sem["v"], nbl)
            else:
                lo = inc(y.dma_start(
                    out=otile[:].rearrange("p (f c) -> p f c", c=128),
                    in_=oT_d.rearrange("p (f t) -> p f t", t=TPC)[:, :, trow:trow + 128]),
                    "dy", 16)
            mm = None
            for fc in range(16):
                g.wait_ge(sem["t"], tick("t"))
                if fc == 0:
                    g.wait_ge(sem["dz"], wo_ready)
                lw = inc(g.dma_start(out=wmov, in_=wO_scr[:, fc * 2048:(fc + 1) * 2048]),
                         "dg", 16)
                t.wait_ge(sem["dg"], lw)
                t.wait_ge(sem["dy"], lo)
                if fc == 0:
                    t.wait_ge(sem["v"], ev_y)
                for ot in range(4):
                    mm = t.matmul(psC[:, ot * 512:(ot + 1) * 512],
                                  otile[:, fc * 128:(fc + 1) * 128],
                                  wmov[:, ot * 512:(ot + 1) * 512],
                                  start=(fc == 0), stop=False)
                inc(mm, "t")
            for ot in range(4):
                mm = t.matmul(psC[:, ot * 512:(ot + 1) * 512], ones1f[:],
                              bOrow[:, ot * 512:(ot + 1) * 512],
                              start=False, stop=True)
            n_t = inc(mm, "t")
            v.wait_ge(sem["t"], n_t)
            v.wait_ge(sem["dz"], tick("dz"))
            ev_y = inc(v.tensor_copy(a1t, psC[:]), "v")
            y.wait_ge(sem["v"], ev_y)
            inc(y.dma_start(out=a1_scr[trow:trow + 128, :], in_=a1t), "dz", 16)

        for e in range(NE + 1):
            shared = (e == NE)
            if shared:
                w1src, w3src, w2src = sw1_d, sw3_d, sw2_d
                eoff1 = 0
                eoff2 = 0
            else:
                w1src, w3src, w2src = w1all_d, w3all_d, w2all_d
                eoff1 = e * 16 * NIC * 128
                eoff2 = e * NIC * 16 * 128
            if not shared:
                g.wait_ge(sem["v"], tick("v"))
                g.wait_ge(sem["t"], tick("t"))
                lcw = inc(g.dma_start(out=cw1[:], in_=cwrows_d[e:e + 1, :]), "dg", 16)
                t.wait_ge(sem["dg"], lcw)
                for qt in range(4):
                    t.wait_ge(sem["v"], tick("v"))
                    n_t = inc(t.matmul(psB[:, 0:512], ones1f[:],
                                       cw1[:, qt * 512:(qt + 1) * 512],
                                       start=True, stop=True), "t")
                    v.wait_ge(sem["t"], n_t)
                    inc(v.tensor_copy(cwrep[:, qt * 512:(qt + 1) * 512], psB[:, 0:512]),
                        "v")
            cw_ready = tick("v")
            wst_last = [0]
            for ic in range(NIC):
                wb = wst[0]
                g.wait_ge(sem["t"], wst_last[0])
                l1 = inc(g.dma_start(
                    out=wb[:, 0:2048].rearrange("p (f o) -> p f o", o=128),
                    in_=w1src.rearrange("p (f o) -> p f o", o=128)[:, eoff1 // 128 + ic::NIC, :]
                    if False else
                    w1src.rearrange("p (f o) -> p f o", o=128)[:, eoff1 // 128 + ic: eoff1 // 128 + 16 * NIC: NIC, :]),
                    "dg", 16)
                l2 = inc(g.dma_start(
                    out=wb[:, 2048:4096].rearrange("p (f o) -> p f o", o=128),
                    in_=w3src.rearrange("p (f o) -> p f o", o=128)[:, eoff1 // 128 + ic: eoff1 // 128 + 16 * NIC: NIC, :]),
                    "dg", 16)
                t.wait_ge(sem["dg"], l2)
                for tt in range(NT4):
                    pg = psA[:, 0:512]
                    pu = psA[:, 512:1024]
                    t.wait_ge(sem["s"], tick("s"))
                    t.wait_ge(sem["v"], tick("v"))
                    if e == 0 and ic == 0 and tt == 0:
                        t.wait_ge(sem["dy"], 16)
                    mm = None
                    for fc in range(16):
                        mm = t.matmul(pg, wb[:, fc * 128:(fc + 1) * 128],
                                      xn1T[:, fc * TPC + tt * 512: fc * TPC + (tt + 1) * 512],
                                      start=(fc == 0), stop=(fc == 15))
                    n_gm = inc(mm, "t")
                    mm = None
                    for fc in range(16):
                        mm = t.matmul(pu, wb[:, 2048 + fc * 128:2048 + (fc + 1) * 128],
                                      xn1T[:, fc * TPC + tt * 512: fc * TPC + (tt + 1) * 512],
                                      start=(fc == 0), stop=(fc == 15))
                    n_um = inc(mm, "t")
                    s.wait_ge(sem["t"], n_gm)
                    n_sg = inc(s.activation(silu_g[:], pg, AF.Silu), "s")
                    v.wait_ge(sem["t"], n_um)
                    v.wait_ge(sem["s"], n_sg)
                    if not shared:
                        nv = inc(v.tensor_tensor(silu_g[:], silu_g[:],
                                                 cwrep[:, tt * 512:(tt + 1) * 512],
                                                 ALU.mult), "v")
                        v.wait_ge(sem["v"], nv)
                    inc(v.tensor_tensor(
                        hsb_e[:, ic * TPC + tt * 512: ic * TPC + (tt + 1) * 512],
                        pu, silu_g[:], ALU.mult), "v")
                wst_last[0] = tick("t")
            nv_h = tick("v")
            ev_yb = nv_yz
            for ocq in range(4):
                g.wait_ge(sem["t"], tick("t"))
                lwq = inc(g.dma_start(
                    out=w2q[:].rearrange("p (i o) -> p i o", o=512),
                    in_=w2src[:, eoff2:eoff2 + NIC * 16 * 128]
                    .rearrange("p (i o) -> p i o", o=16 * 128)[:, :, ocq * 512:(ocq + 1) * 512]),
                    "dg", 16)
                t.wait_ge(sem["dg"], lwq)
                for tt in range(NT4):
                    t.wait_ge(sem["v"], max(nv_h, ev_yb))
                    mm = None
                    for ic in range(NIC):
                        for j in range(4):
                            mm = t.matmul(psC[:, j * 512:(j + 1) * 512],
                                          w2q[:, (ic * 4 + j) * 128:(ic * 4 + j + 1) * 128],
                                          hsb_e[:, ic * TPC + tt * 512: ic * TPC + (tt + 1) * 512],
                                          start=(ic == 0), stop=(ic == NIC - 1))
                    n_t = inc(mm, "t")
                    v.wait_ge(sem["t"], n_t)
                    last = None
                    for j in range(4):
                        oc = ocq * 4 + j
                        last = v.tensor_tensor(
                            ysbT[:, oc * TPC + tt * 512: oc * TPC + (tt + 1) * 512],
                            ysbT[:, oc * TPC + tt * 512: oc * TPC + (tt + 1) * 512],
                            psC[:, j * 512:(j + 1) * 512], ALU.add)
                    ev_yb = inc(last, "v")

        t.wait_ge(sem["s"], tick("s"))
        ev_fin = tick("v")
        ev_tr = [tick("v")] * 2
        for tc in range(16):
            trow = tc * 128
            y.wait_ge(sem["v"], ev_fin)
            lx = inc(y.dma_start(out=x1t, in_=x1_d[trow:trow + 128, :]), "dy", 16)
            lx = inc(y.dma_start(out=a1t, in_=a1_scr[trow:trow + 128, :]), "dy", 16)
            v.wait_ge(sem["dy"], lx)
            v.wait_ge(sem["dz"], tick("dz"))
            nv = inc(v.tensor_tensor(outf[:], a1t, x1t, ALU.add), "v")
            for fc in range(16):
                v.wait_ge(sem["v"], nv if fc == 0 else 0)
                nvc = inc(v.tensor_copy(tf32, ysbT[:, fc * TPC + trow: fc * TPC + trow + 128]), "v")
                pb = (psA if fc % 2 == 0 else psB)[:, 0:128]
                t.wait_ge(sem["v"], max(nvc, ev_tr[fc % 2]))
                n_t = inc(t.transpose(pb, tf32, ident_f[:]), "t")
                v.wait_ge(sem["t"], n_t)
                nv = inc(v.tensor_tensor(outf[:, fc * 128:(fc + 1) * 128],
                                         outf[:, fc * 128:(fc + 1) * 128], pb, ALU.add),
                         "v")
                ev_tr[fc % 2] = nv
            y.wait_ge(sem["v"], nv)
            inc(y.dma_start(out=out_d[trow:trow + 128, :], in_=outf), "dz", 16)
            ev_fin = nv
        y.wait_ge(sem["dz"], tick("dz"))
        y.wait_ge(sem["dy"], tick("dy"))
        return ctx.finalize() if own else None



def _barrier_all(ctx):
    for eng in (ctx.t, ctx.v, ctx.s, ctx.g, ctx.y):
        for k in list(ctx.sem):
            vv = ctx.tick(k)
            if vv:
                eng.wait_ge(ctx.sem[k], vv)


def _wait_all(ctx, key, val):
    for eng in (ctx.t, ctx.v, ctx.s, ctx.g, ctx.y):
        eng.wait_ge(ctx.sem[key], val)


def build_fused():
    ctx = Ctx("FUSED")
    nc = ctx.nc
    ctx.alloc_sems(["t", "v", "s", "dy", "dg", "dz", "c1", "c2", "c3", "c4"])
    rg = [[0, 1], [2, 3], [4, 5], [6, 7]]
    BF_ = BF16
    ag1i = nc.dram_tensor("ag1i", [128, 16 * TPC], BF_).ap()
    ag1o = nc.dram_tensor("ag1o", [256, 16 * TPC], BF_).ap()
    ag2i = nc.dram_tensor("ag2i", [256, 8 * TPC], BF_).ap()
    ag2o = nc.dram_tensor("ag2o", [512, 8 * TPC], BF_).ap()
    ag3i = nc.dram_tensor("ag3i", [128, 28 * TPC], BF_).ap()
    ag3o = nc.dram_tensor("ag3o", [256, 28 * TPC], BF_).ap()
    ag4i = nc.dram_tensor("ag4i", [256, 8 * TPC], BF_).ap()
    ag4o = nc.dram_tensor("ag4o", [512, 8 * TPC], BF_).ap()
    x1_i = nc.dram_tensor("x1_i", [TPC, D], BF_).ap()
    xn1T_i = nc.dram_tensor("xn1T_i", [128, 16 * TPC], BF_).ap()
    cw_i = nc.dram_tensor("cw_i", [8, TPC], F32).ap()

    def agcall(key, ain, aout):
        ctx.g.wait_ge(ctx.sem["dz"], ctx.tick("dz"))
        ctx.g.wait_ge(ctx.sem["dy"], ctx.tick("dy"))
        ctx.inc(ctx.g.collective_compute(
            "AllGather", ALU.bypass, replica_groups=rg,
            ins=[ain.opt()], outs=[aout.opt()]), key, 1)
        _wait_all(ctx, key, ctx.tick(key))

    ctx.prefix = "a0_"; ctx.io = {"xn0T": ag1i}
    m = ctx.mark(); build_A0(ctx); ctx.release(m); _barrier_all(ctx)
    agcall("c1", ag1i, ag1o)
    ctx.prefix = "a1_"; ctx.io = {"xn0T": ag1o, "attnT": ag2i}
    m = ctx.mark(); build_A1(ctx); ctx.release(m); _barrier_all(ctx)
    agcall("c2", ag2i, ag2o)
    ctx.prefix = "b_"
    ctx.io = {"attnT": ag2o, "xn0T": ag1i, "x1": x1_i, "xn1T": xn1T_i, "cwrows": cw_i,
              "ckvT": ag3i[:, 0:4 * TPC], "cqT": ag3i[:, 4 * TPC:12 * TPC],
              "krT": ag3i[:, 12 * TPC:28 * TPC]}
    m = ctx.mark(); build_B(ctx); ctx.release(m); _barrier_all(ctx)
    agcall("c3", ag3i, ag3o)
    ctx.prefix = "c_"
    ctx.io = {"ckvT": ag3o[:, 0:4 * TPC], "cqT": ag3o[:, 4 * TPC:12 * TPC],
              "krT": ag3o[:, 12 * TPC:28 * TPC], "oT": ag4i}
    m = ctx.mark(); build_C(ctx); ctx.release(m); _barrier_all(ctx)
    agcall("c4", ag4i, ag4o)
    ctx.prefix = "d_"
    ctx.io = {"oT": ag4o, "x1": x1_i, "xn1T": xn1T_i, "cwrows": cw_i}
    m = ctx.mark(); build_D(ctx); ctx.release(m)
    ctx.prefix = ""
    ctx.io = {}
    return ctx.finalize()


def run_fused(inputs):
    con = host_tables()
    ncF = _prog("FUSED", build_fused)
    x = np.ascontiguousarray(
        np.asarray(inputs["L0_attn_hidden_states"]).reshape(B * S, D))
    qkv_w = inputs["L0_attn_qkv_w"]
    norm_w = np.asarray(inputs["L0_attn_norm_w"])
    wg8 = _u8view(np.ascontiguousarray(inputs["L0_ffn_gate_w"]))
    wu8 = _u8view(np.ascontiguousarray(inputs["L0_ffn_up_w"]))
    wd8 = _u8view(np.ascontiguousarray(inputs["L0_ffn_down_w"]))
    gatewT = p16w(np.ascontiguousarray(inputs["L0_attn_gate_w"].T))
    outwT = p16w(np.ascontiguousarray(inputs["L0_attn_out_w"].T))
    wg8T = p16w(np.ascontiguousarray(wg8.T))
    wu8T = p16w(np.ascontiguousarray(wu8.T))
    wd8T = p16w(np.ascontiguousarray(wd8.T))
    gscol = np.ascontiguousarray(inputs["L0_ffn_gate_scale"].reshape(64, 128).T)
    uscol = np.ascontiguousarray(inputs["L0_ffn_up_scale"].reshape(64, 128).T)
    dsrow = np.ascontiguousarray(inputs["L0_ffn_down_scale"].reshape(1, D))
    normcol = np.ascontiguousarray(norm_w.reshape(16, 128).T)
    gwT = np.ascontiguousarray(
        inputs["L1_ffn_gate_w"].T.reshape(16, 128, 8).transpose(1, 0, 2).reshape(128, 16 * 8))
    gbrep = np.ascontiguousarray(np.repeat(np.asarray(inputs["L1_ffn_gate_b"])[None, :], 128, 0))
    wdkvT = p16w(np.ascontiguousarray(inputs["L1_attn_W_DKV_w"].T))
    wdqT = p16w(np.ascontiguousarray(inputs["L1_attn_W_DQ_w"].T))
    wkrT = p16w(np.ascontiguousarray(inputs["L1_attn_W_KR_w"].T))
    bkvcol = np.ascontiguousarray(np.concatenate([
        np.asarray(inputs["L1_attn_W_DKV_b"]).reshape(4, 128).T,
        np.asarray(inputs["L1_attn_W_DQ_b"]).reshape(8, 128).T], axis=1).astype(np.float32))
    bkrrep = np.ascontiguousarray(
        np.repeat(np.asarray(inputs["L1_attn_W_KR_b"])[None, :], 128, 0))
    wOT = p16w(np.ascontiguousarray(inputs["L1_attn_W_O_w"].T))
    bOrow = np.ascontiguousarray(np.asarray(inputs["L1_attn_W_O_b"]).reshape(1, D))
    w1 = np.asarray(inputs["L1_ffn_w1"]); w3 = np.asarray(inputs["L1_ffn_w3"])
    w2 = np.asarray(inputs["L1_ffn_w2"])
    w1all = np.concatenate([p16w(np.ascontiguousarray(w1[e].T)) for e in range(NE)], 1)
    w3all = np.concatenate([p16w(np.ascontiguousarray(w3[e].T)) for e in range(NE)], 1)
    w2all = np.concatenate([p16w(np.ascontiguousarray(w2[e].T)) for e in range(NE)], 1)
    sw1 = p16w(np.ascontiguousarray(np.asarray(inputs["L1_ffn_sw1"]).T))
    sw3 = p16w(np.ascontiguousarray(np.asarray(inputs["L1_ffn_sw3"]).T))
    sw2 = p16w(np.ascontiguousarray(np.asarray(inputs["L1_ffn_sw2"]).T))
    in_maps = []
    for c in range(NCORE):
        b, half = c // 2, c % 2
        hh = 16 * half
        t0 = half * TPC
        wq = []
        for grp in range(2):
            rows = np.concatenate(
                [qkv_w[(hh + grp * 8 + i) * 192 + part * 64:
                       (hh + grp * 8 + i) * 192 + (part + 1) * 64]
                 for part in range(3) for i in range(8)], axis=0)
            wq.append(p16w(np.ascontiguousarray(rows.T)))
        bdcol = np.repeat(con["bd"][hh:hh + 16].reshape(1, 16), 64, 0).astype(np.float32)
        hh8 = 8 * half
        hrows = slice(hh8 * DH, (hh8 + 8) * DH)
        wuk = p16w(np.ascontiguousarray(inputs["L1_attn_W_UK_w"][hrows].T))
        wuv = p16w(np.ascontiguousarray(inputs["L1_attn_W_UV_w"][hrows].T))
        wuq = p16w(np.ascontiguousarray(inputs["L1_attn_W_UQ_w"][hrows].T))
        buk = np.ascontiguousarray(
            np.asarray(inputs["L1_attn_W_UK_b"])[hrows].reshape(8, 128).T.astype(np.float32))
        buq = np.ascontiguousarray(
            np.asarray(inputs["L1_attn_W_UQ_b"])[hrows].reshape(8, 128).T.astype(np.float32))
        buv = np.ascontiguousarray(np.repeat(
            np.asarray(inputs["L1_attn_W_UV_b"])[hrows].reshape(1, 8 * 128), 128, 0))
        sel = np.zeros((128, 2), np.float32)
        sel[:, half] = 1.0
        in_maps.append({
            "a0_xin": x[c * TPC:(c + 1) * TPC], "a0_identin": con["ident_bf"],
            "a1_wqkv0": wq[0], "a1_wqkv1": wq[1],
            "a1_qdrep": np.ascontiguousarray(con["qd_rep"][:, hh * BS0:(hh + 16) * BS0]),
            "a1_kdrep": np.ascontiguousarray(con["kd_rep"][:, hh * BS0:(hh + 16) * BS0]),
            "a1_ddT": np.ascontiguousarray(con["ddT"][:, hh * 2 * BS0:(hh + 16) * 2 * BS0]),
            "a1_bdcol": bdcol, "a1_identin": con["ident_bf"],
            "b_x": x[c * TPC:(c + 1) * TPC],
            "b_normcol": normcol, "b_gatewT": gatewT, "b_outwT": outwT,
            "b_wg8": wg8T, "b_wu8": wu8T, "b_wd8": wd8T,
            "b_gscol": gscol, "b_uscol": uscol, "b_dsrow": dsrow,
            "b_identin": con["ident_bf"],
            "b_ropeC": np.ascontiguousarray(con["ropeC"][t0:t0 + TPC]),
            "b_ropeS": np.ascontiguousarray(con["ropeS"][t0:t0 + TPC]),
            "b_gwT": gwT, "b_gbrep": gbrep,
            "b_wdkvT": wdkvT, "b_wdqT": wdqT, "b_wkrT": wkrT,
            "b_bkvcol": bkvcol, "b_bkrrep": bkrrep, "b_selcol": sel,
            "c_wukT": wuk, "c_wuvT": wuv, "c_wuqT": wuq,
            "c_bukcol": buk, "c_buvrep": buv, "c_buqcol": buq,
            "c_ropeCF": con["ropeC"], "c_ropeSF": con["ropeS"],
            "c_dmask": con["diagmask"], "c_identin": con["ident_bf"],
            "c_selcol": sel,
            "d_wOT": wOT, "d_bOrow": bOrow,
            "d_w1all": w1all, "d_w3all": w3all, "d_w2all": w2all,
            "d_sw1": sw1, "d_sw3": sw3, "d_sw2": sw2, "d_selcol": sel,
            "d_identin": con["ident_bf"],
        })
    res = run_bass_kernel_spmd(ncF, in_maps, core_ids=list(range(NCORE)))
    outs = [res.results[c]["d_out"] for c in range(NCORE)]
    return np.concatenate(outs, 0).reshape(B, S, D)



_P_CACHE = {}


def _prog(name, builder):
    if name not in _P_CACHE:
        _P_CACHE[name] = builder()
    return _P_CACHE[name]


def p16w(wT):
    F_, O_ = wT.shape
    return np.ascontiguousarray(
        wT.reshape(F_ // 128, 128, O_ // 128, 128).transpose(1, 0, 2, 3).reshape(128, -1))


def run_A0(x):
    ncA0 = _prog("A0", build_A0)
    con = host_tables()
    in_maps = [{"xin": x[c * TPC:(c + 1) * TPC], "identin": con["ident_bf"]}
               for c in range(NCORE)]
    res = run_bass_kernel_spmd(ncA0, in_maps, core_ids=list(range(NCORE)))
    return [res.results[c]["xn0T"] for c in range(NCORE)]


def _xn0T_full(xn0T, b):
    e, o = xn0T[2 * b], xn0T[2 * b + 1]
    out = np.empty((128, 16, S), dtype=e.dtype)
    out[:, :, :TPC] = e.reshape(128, 16, TPC)
    out[:, :, TPC:] = o.reshape(128, 16, TPC)
    return np.ascontiguousarray(out.reshape(128, 16 * S))


def run_A1(xn0T, inputs):
    con = host_tables()
    ncA1 = _prog("A1", build_A1)
    qkv_w = inputs["L0_attn_qkv_w"]
    in_maps = []
    for c in range(NCORE):
        b, half = c // 2, c % 2
        hh = 16 * half
        wq = []
        for grp in range(2):
            rows = np.concatenate(
                [qkv_w[(hh + grp * 8 + i) * 192 + part * 64:
                       (hh + grp * 8 + i) * 192 + (part + 1) * 64]
                 for part in range(3) for i in range(8)], axis=0)
            wq.append(p16w(np.ascontiguousarray(rows.T)))
        bdcol = np.repeat(con["bd"][hh:hh + 16].reshape(1, 16), 64, 0).astype(np.float32)
        in_maps.append({
            "xn0T": _xn0T_full(xn0T, b),
            "wqkv0": wq[0], "wqkv1": wq[1],
            "qdrep": np.ascontiguousarray(con["qd_rep"][:, hh * BS0:(hh + 16) * BS0]),
            "kdrep": np.ascontiguousarray(con["kd_rep"][:, hh * BS0:(hh + 16) * BS0]),
            "ddT": np.ascontiguousarray(con["ddT"][:, hh * 2 * BS0:(hh + 16) * 2 * BS0]),
            "bdcol": bdcol,
            "identin": con["ident_bf"],
        })
    res = run_bass_kernel_spmd(ncA1, in_maps, core_ids=list(range(NCORE)))
    return [res.results[c]["attnT"] for c in range(NCORE)]


def _u8view(w_int32):
    return w_int32.view(np.uint8).reshape(*w_int32.shape, 4)[..., 0]


def run_B(x, xn0T, attnT, inputs):
    con = host_tables()
    ncB = _prog("B", build_B)
    gate_w = inputs["L0_attn_gate_w"]; out_w = inputs["L0_attn_out_w"]
    norm_w = np.asarray(inputs["L0_attn_norm_w"])
    wg8 = _u8view(np.ascontiguousarray(inputs["L0_ffn_gate_w"]))
    wu8 = _u8view(np.ascontiguousarray(inputs["L0_ffn_up_w"]))
    wd8 = _u8view(np.ascontiguousarray(inputs["L0_ffn_down_w"]))
    gatewT = p16w(np.ascontiguousarray(gate_w.T))
    outwT = p16w(np.ascontiguousarray(out_w.T))
    wg8T = p16w(np.ascontiguousarray(wg8.T))
    wu8T = p16w(np.ascontiguousarray(wu8.T))
    wd8T = p16w(np.ascontiguousarray(wd8.T))
    gscol = np.ascontiguousarray(inputs["L0_ffn_gate_scale"].reshape(64, 128).T)
    uscol = np.ascontiguousarray(inputs["L0_ffn_up_scale"].reshape(64, 128).T)
    dsrow = np.ascontiguousarray(inputs["L0_ffn_down_scale"].reshape(1, D))
    normcol = np.ascontiguousarray(norm_w.reshape(16, 128).T)
    gwT = np.ascontiguousarray(
        inputs["L1_ffn_gate_w"].T.reshape(16, 128, 8).transpose(1, 0, 2).reshape(128, 16 * 8))
    gbrep = np.ascontiguousarray(np.repeat(np.asarray(inputs["L1_ffn_gate_b"])[None, :], 128, 0))
    wdkvT = p16w(np.ascontiguousarray(inputs["L1_attn_W_DKV_w"].T))
    wdqT = p16w(np.ascontiguousarray(inputs["L1_attn_W_DQ_w"].T))
    wkrT = p16w(np.ascontiguousarray(inputs["L1_attn_W_KR_w"].T))
    bkvcol = np.concatenate([
        np.asarray(inputs["L1_attn_W_DKV_b"]).reshape(4, 128).T,
        np.asarray(inputs["L1_attn_W_DQ_b"]).reshape(8, 128).T], axis=1)
    bkvcol = np.ascontiguousarray(bkvcol.astype(np.float32))
    bkrrep = np.ascontiguousarray(
        np.repeat(np.asarray(inputs["L1_attn_W_KR_b"])[None, :], 128, 0))
    in_maps = []
    for c in range(NCORE):
        b, half = c // 2, c % 2
        t0 = half * TPC
        at = np.empty((128, 16, TPC), dtype=attnT[0].dtype)
        for fc in range(8):
            at[:, fc, :] = attnT[2 * b].reshape(128, 8, S)[:, fc, t0:t0 + TPC]
            at[:, 8 + fc, :] = attnT[2 * b + 1].reshape(128, 8, S)[:, fc, t0:t0 + TPC]
        sel = np.zeros((128, 2), np.float32)
        sel[:, c % 2] = 1.0
        in_maps.append({
            "selcol": sel,
            "x": x[c * TPC:(c + 1) * TPC],
            "xn0T": xn0T[c],
            "attnT": np.ascontiguousarray(at.reshape(128, 16 * TPC)),
            "normcol": normcol, "gatewT": gatewT, "outwT": outwT,
            "wg8": wg8T, "wu8": wu8T, "wd8": wd8T,
            "gscol": gscol, "uscol": uscol, "dsrow": dsrow,
            "identin": con["ident_bf"],
            "ropeC": np.ascontiguousarray(con["ropeC"][t0:t0 + TPC]),
            "ropeS": np.ascontiguousarray(con["ropeS"][t0:t0 + TPC]),
            "gwT": gwT, "gbrep": gbrep,
            "wdkvT": wdkvT, "wdqT": wdqT, "wkrT": wkrT,
            "bkvcol": bkvcol, "bkrrep": bkrrep,
        })
    res = run_bass_kernel_spmd(ncB, in_maps, core_ids=list(range(NCORE)))
    keys = ["x1", "xn1T", "ckvT", "cqT", "krT", "cwrows"] + (["hdbg", "zdbg"] if DEBUG_B else [])
    return {k: [res.results[c][k] for c in range(NCORE)] for k in keys}


def _pair_full(slices, b, nch):
    e = np.asarray(slices[2 * b]).reshape(128, nch, TPC)
    o = np.asarray(slices[2 * b + 1]).reshape(128, nch, TPC)
    out = np.empty((128, nch, S), dtype=e.dtype)
    out[:, :, :TPC] = e
    out[:, :, TPC:] = o
    return np.ascontiguousarray(out.reshape(128, nch * S))


def run_C(rb, inputs):
    con = host_tables()
    ncC = _prog("C", build_C)
    in_maps = []
    for c in range(NCORE):
        b, half = c // 2, c % 2
        hh8 = 8 * half
        rows = slice(hh8 * DH, (hh8 + 8) * DH)
        wuk = p16w(np.ascontiguousarray(inputs["L1_attn_W_UK_w"][rows].T))
        wuv = p16w(np.ascontiguousarray(inputs["L1_attn_W_UV_w"][rows].T))
        wuq = p16w(np.ascontiguousarray(inputs["L1_attn_W_UQ_w"][rows].T))
        buk = np.ascontiguousarray(
            np.asarray(inputs["L1_attn_W_UK_b"])[rows].reshape(8, 128).T.astype(np.float32))
        buq = np.ascontiguousarray(
            np.asarray(inputs["L1_attn_W_UQ_b"])[rows].reshape(8, 128).T.astype(np.float32))
        buv = np.ascontiguousarray(np.repeat(
            np.asarray(inputs["L1_attn_W_UV_b"])[rows].reshape(1, 8 * 128), 128, 0))
        krT_full = _pair_full(rb["krT"], b, 16).reshape(128, 16, S)
        krT_slice = np.ascontiguousarray(
            krT_full[:, hh8:hh8 + 8, :].reshape(128, 8 * S))
        sel = np.zeros((128, 2), np.float32)
        sel[:, c % 2] = 1.0
        in_maps.append({
            "selcol": sel,
            "ckvT": _pair_full(rb["ckvT"], b, 4),
            "cqT": _pair_full(rb["cqT"], b, 8),
            "krT": krT_slice,
            "wukT": wuk, "wuvT": wuv, "wuqT": wuq,
            "bukcol": buk, "buvrep": buv, "buqcol": buq,
            "ropeCF": con["ropeC"], "ropeSF": con["ropeS"],
            "dmask": con["diagmask"], "identin": con["ident_bf"],
        })
    res = run_bass_kernel_spmd(ncC, in_maps, core_ids=list(range(NCORE)))
    return [res.results[c]["oT"] for c in range(NCORE)]


def run_D(rb, oT, inputs):
    ncD = _prog("D", build_D)
    wOT = p16w(np.ascontiguousarray(inputs["L1_attn_W_O_w"].T))
    bOrow = np.ascontiguousarray(np.asarray(inputs["L1_attn_W_O_b"]).reshape(1, D))
    w1 = np.asarray(inputs["L1_ffn_w1"]); w3 = np.asarray(inputs["L1_ffn_w3"])
    w2 = np.asarray(inputs["L1_ffn_w2"])
    w1all = np.concatenate([p16w(np.ascontiguousarray(w1[e].T)) for e in range(NE)], 1)
    w3all = np.concatenate([p16w(np.ascontiguousarray(w3[e].T)) for e in range(NE)], 1)
    w2all = np.concatenate([p16w(np.ascontiguousarray(w2[e].T)) for e in range(NE)], 1)
    sw1 = p16w(np.ascontiguousarray(np.asarray(inputs["L1_ffn_sw1"]).T))
    sw3 = p16w(np.ascontiguousarray(np.asarray(inputs["L1_ffn_sw3"]).T))
    sw2 = p16w(np.ascontiguousarray(np.asarray(inputs["L1_ffn_sw2"]).T))
    in_maps = []
    for c in range(NCORE):
        b, half = c // 2, c % 2
        t0 = half * TPC
        ot_own = np.empty((128, 16, TPC), dtype=oT[0].dtype)
        for fc in range(8):
            ot_own[:, fc, :] = np.asarray(oT[2 * b]).reshape(128, 8, S)[:, fc, t0:t0 + TPC]
            ot_own[:, 8 + fc, :] = np.asarray(oT[2 * b + 1]).reshape(128, 8, S)[:, fc, t0:t0 + TPC]
        sel = np.zeros((128, 2), np.float32)
        sel[:, c % 2] = 1.0
        in_maps.append({
            "selcol": sel,
            "x1": rb["x1"][c], "xn1T": rb["xn1T"][c],
            "oT": np.ascontiguousarray(ot_own.reshape(128, 16 * TPC)),
            "cwrows": rb["cwrows"][c],
            "wOT": wOT, "bOrow": bOrow,
            "w1all": w1all, "w3all": w3all, "w2all": w2all,
            "sw1": sw1, "sw3": sw3, "sw2": sw2,
            "identin": host_tables()["ident_bf"],
        })
    res = run_bass_kernel_spmd(ncD, in_maps, core_ids=list(range(NCORE)))
    return [res.results[c]["out"] for c in range(NCORE)]


def kernel(**inputs):
    return run_fused(inputs)


def kernel_unfused(**inputs):
    x = np.ascontiguousarray(
        np.asarray(inputs["L0_attn_hidden_states"]).reshape(B * S, D))
    xn0T = run_A0(x)
    attnT = run_A1(xn0T, inputs)
    rb = run_B(x, xn0T, attnT, inputs)
    oT = run_C(rb, inputs)
    outs = run_D(rb, oT, inputs)
    return np.concatenate(outs, 0).reshape(B, S, D)


# version 04 (final)
# speedup vs baseline: 1.3316x; 1.0599x over previous
import numpy as np
import ml_dtypes
import concourse.bass as bass
import concourse.mybir as mybir
from concourse.bass_utils import run_bass_kernel_spmd

F32 = mybir.dt.float32
BF16 = mybir.dt.bfloat16
U8 = mybir.dt.uint8
AF = mybir.ActivationFunctionType
ALU = mybir.AluOpType
AX = mybir.AxisListType.X
BF = ml_dtypes.bfloat16

NCORE = 8
B, S, D = 4, 4096, 2048
TPC = 2048
H0, D0, BS0 = 32, 64, 256
NB0 = S // BS0
H1, DH, DC, DQC = 16, 128, 512, 1024
FF0 = 8192
FFE = 1408
NE = 8
EPS = 1e-5
DEBUG_B = False



class _Rec:
    __slots__ = ("key", "meth", "args", "kwargs", "sem_incs")

    def __init__(self, key, meth, args, kwargs):
        self.key, self.meth, self.args, self.kwargs = key, meth, args, kwargs
        self.sem_incs = []

    def then_inc(self, sem, amt):
        self.sem_incs.append((sem, amt))
        return self


class _EngRec:
    def __init__(self, ctx, key):
        self._ctx, self._key = ctx, key

    def __getattr__(self, meth):
        ctx, key = self._ctx, self._key

        def call(*a, **k):
            r = _Rec(key, meth, a, k)
            ctx.ops.append(r)
            return r

        return call


_ENG_NAMES = {"t": "tensor", "v": "vector", "s": "scalar", "g": "gpsimd", "y": "sync"}


class Ctx:

    def __init__(self, name):
        self.nc = bass.Bass("TRN2", target_bir_lowering=False, debug=False)
        self.name = name
        self._exit = []
        self.cnt = {}
        self.sem = {}
        self.ops = []
        self.prefix = ""
        self.io = {}
        self.t = _EngRec(self, "t")
        self.v = _EngRec(self, "v")
        self.s = _EngRec(self, "s")
        self.g = _EngRec(self, "g")
        self.y = _EngRec(self, "y")

    def __enter__(self):
        return self

    def __exit__(self, *a):
        return False

    def alloc_sems(self, names):
        for n in names:
            if n in self.sem:
                continue
            cm = self.nc.semaphore(f"sem_{n}")
            self.sem[n] = cm.__enter__()
            self._exit.append(cm)
            self.cnt[n] = 0

    def mark(self):
        return len(self._exit)

    def release(self, m):
        for cm in reversed(self._exit[m:]):
            cm.__exit__(None, None, None)
        del self._exit[m:]

    def inc(self, rec, key, amt=1):
        rec.then_inc(self.sem[key], amt)
        self.cnt[key] += amt
        return self.cnt[key]

    def tick(self, key):
        return self.cnt[key]

    def finalize(self):
        ops = self.ops
        with self.nc.Block() as block:
            for key, name in _ENG_NAMES.items():
                def body(eng, key=key):
                    for r in ops:
                        if r.key != key:
                            continue
                        instr = getattr(eng, r.meth)(*r.args, **r.kwargs)
                        for sem, amt in r.sem_incs:
                            instr = instr.then_inc(sem, amt)
                getattr(block, name)(body)
        for cm in reversed(self._exit):
            cm.__exit__(None, None, None)
        self._exit = []
        return self.nc

    def sb(self, name, shape, dtype):
        cm = self.nc.sbuf_tensor(self.prefix + name, shape, dtype)
        t = cm.__enter__()
        self._exit.append(cm)
        return t

    def ps(self, name, shape, dtype=F32):
        cm = self.nc.psum_tensor(self.prefix + name, shape, dtype)
        t = cm.__enter__()
        self._exit.append(cm)
        return t

    def din(self, name, shape, dtype):
        if name in self.io:
            return self.io[name]
        return self.nc.dram_tensor(self.prefix + name, list(shape), dtype,
                                   kind="ExternalInput").ap()

    def dout(self, name, shape, dtype):
        if name in self.io:
            return self.io[name]
        return self.nc.dram_tensor(self.prefix + name, list(shape), dtype,
                                   kind="ExternalOutput").ap()


def rsqrt_cols(ctx, out_col, var_col, eps, eng_wait_key=None):
    s = ctx.s
    s.activation(out_col, var_col, AF.Ln, bias=float(eps), scale=1.0)
    n = ctx.inc(s.activation(out_col, out_col, AF.Exp, scale=-0.5), "s")
    return n



def _host_const():
    c = {}
    c["ident_bf"] = np.eye(128, dtype=np.float32)
    f32 = np.float32
    base = f32(1.0) / f32(2.0) ** f32(8.0 / H0)
    rate = base ** (np.arange(H0, dtype=f32) + 1.0)
    factor = f32(1.0 + 1e-05)
    slope = (rate * factor).astype(f32)
    r = np.arange(BS0, dtype=f32) + 1.0
    qd = np.exp(-slope[:, None] * r[None, :]).astype(f32)
    kd = np.exp(-slope[:, None] * (BS0 - r[None, :])).astype(f32)
    dd = slope[:, None, None] * (r[:, None] - r[None, :])[None]
    dd = np.where(dd >= 0, np.exp(-dd), 0.0).astype(f32)
    bd = np.exp(-slope * f32(BS0)).astype(f32)
    c["qd_rep"] = np.repeat(qd.reshape(1, H0 * BS0), 128, 0).astype(BF)
    c["kd_rep"] = np.repeat(kd.reshape(1, H0 * BS0), 128, 0).astype(BF)
    ddT = np.transpose(dd, (0, 2, 1))
    ddT = ddT.reshape(H0, 2, 128, BS0)
    c["ddT"] = np.transpose(ddT, (2, 0, 1, 3)).reshape(128, H0 * 2 * BS0).astype(BF)
    c["bd"] = bd
    pos = np.arange(S, dtype=np.float64)
    idx = np.arange(DH // 2, dtype=np.float64)
    theta = (pos[:, None] * (10000.0 ** (-2.0 * idx / DH))[None, :])
    cos = np.cos(theta).astype(f32)
    sin = np.sin(theta).astype(f32)
    C2 = np.empty((S, DH), f32)
    S2 = np.empty((S, DH), f32)
    C2[:, 0::2] = cos; C2[:, 1::2] = cos
    S2[:, 0::2] = sin; S2[:, 1::2] = sin
    c["ropeC"] = C2
    c["ropeS"] = S2
    m = np.zeros((4, 128, 512), f32)
    for o in range(4):
        for k in range(128):
            m[o, k, :] = (np.arange(512) >= (o * 128 + k)).astype(f32)
    c["diagmask"] = m.reshape(4 * 128, 512)
    return c


_CONST = None


def host_tables():
    global _CONST
    if _CONST is None:
        _CONST = _host_const()
    return _CONST



def build_A0(ctx=None):
    own = ctx is None
    with (Ctx("A0") if own else ctx) as ctx:
        nc = ctx.nc
        xin = ctx.din("xin", [TPC, D], F32)
        identin = ctx.din("identin", [128, 128], F32)
        xn0T_d = ctx.dout("xn0T", [128, 16 * TPC], BF16)

        ctx.alloc_sems(["t", "v", "s", "dy", "dg"])
        t, v, s, g, y = ctx.t, ctx.v, ctx.s, ctx.g, ctx.y

        xt = [ctx.sb(f"xt{i}", [128, D], F32) for i in range(2)]
        sq = ctx.sb("sq", [128, D], F32)
        xnb = ctx.sb("xnb", [128, D], BF16)
        st_ = ctx.sb("stats", [128, 8], F32)
        ident = ctx.sb("ident", [128, 128], BF16)
        outT = ctx.sb("outT", [128, 16 * TPC], BF16)
        ptr = [ctx.ps(f"ptr{i}", [128, 128], BF16) for i in range(2)]

        ident_done = ctx.inc(g.dma_start(out=ident[:], in_=identin[:, :]), "dg", 16)

        ctx.inc(y.dma_start(out=xt[0][:], in_=xin[0:128, :]), "dy", 16)

        evac_done = [0] * 2
        tile_consumed = [0, 0]

        for k in range(16):
            buf = xt[k % 2]
            if k + 1 < 16:
                nbuf = (k + 1) % 2
                if k >= 1:
                    y.wait_ge(ctx.sem["v"], tile_consumed[nbuf])
                ctx.inc(y.dma_start(out=xt[nbuf][:],
                                    in_=xin[(k + 1) * 128:(k + 2) * 128, :]), "dy", 16)
            v.wait_ge(ctx.sem["dy"], 16 * (k + 1))
            v.reduce_sum(st_[:, 0:1], buf[:], axis=AX)
            v.tensor_mul(sq[:], buf[:], buf[:])
            n_v = ctx.inc(v.reduce_sum(st_[:, 1:2], sq[:], axis=AX), "v")
            v.wait_ge(ctx.sem["v"], n_v)
            v.tensor_scalar(st_[:, 2:3], st_[:, 0:1], 1.0 / D, None, ALU.mult)
            n_v = ctx.inc(v.tensor_scalar(st_[:, 3:4], st_[:, 1:2], 1.0 / D, None,
                                          ALU.mult), "v")
            v.wait_ge(ctx.sem["v"], n_v)
            n_v = ctx.inc(v.tensor_mul(st_[:, 4:5], st_[:, 2:3], st_[:, 2:3]), "v")
            v.wait_ge(ctx.sem["v"], n_v)
            n_v = ctx.inc(v.scalar_tensor_tensor(st_[:, 4:5], st_[:, 3:4], float(EPS),
                                                 st_[:, 4:5], ALU.add, ALU.subtract), "v")
            s.wait_ge(ctx.sem["v"], n_v)
            s.activation(st_[:, 5:6], st_[:, 4:5], AF.Ln, scale=1.0)
            n_s = ctx.inc(s.activation(st_[:, 5:6], st_[:, 5:6], AF.Exp, scale=-0.5), "s")
            v.wait_ge(ctx.sem["s"], n_s)
            n_v = ctx.inc(v.scalar_tensor_tensor(st_[:, 6:7], st_[:, 2:3], -1.0,
                                                 st_[:, 5:6], ALU.mult, ALU.mult), "v")
            v.wait_ge(ctx.sem["v"], n_v)
            n_s = ctx.inc(
                v.tensor_scalar(xnb[:], buf[:], st_[:, 5:6], st_[:, 6:7],
                                ALU.mult, ALU.add), "v")
            tile_consumed[k % 2] = n_s
            t.wait_ge(ctx.sem["v"], n_s)
            if k == 0:
                t.wait_ge(ctx.sem["dg"], ident_done)
            for f in range(16):
                pb = ptr[f % 2]
                t.wait_ge(ctx.sem["v"], evac_done[f % 2])
                n_t = ctx.inc(t.transpose(pb[:], xnb[:, f * 128:(f + 1) * 128],
                                          ident[:]), "t")
                v.wait_ge(ctx.sem["t"], n_t)
                evac_done[f % 2] = ctx.inc(
                    v.tensor_copy(outT[:, f * TPC + k * 128: f * TPC + (k + 1) * 128],
                                  pb[:]), "v")

        y.wait_ge(ctx.sem["v"], ctx.tick("v"))
        ctx.inc(y.dma_start(out=xn0T_d[:, :], in_=outT[:]), "dy", 16)
        y.wait_ge(ctx.sem["dy"], ctx.tick("dy"))

        return ctx.finalize() if own else None



def build_A1(ctx=None):
    NT = 8
    own = ctx is None
    with (Ctx("A1") if own else ctx) as ctx:
        xn0T_d = ctx.din("xn0T", [128, 16 * S], BF16)
        wqkv_d = [ctx.din(f"wqkv{gidx}", [128, 16 * 12 * 128], F32) for gidx in range(2)]
        qdrep_d = ctx.din("qdrep", [128, 16 * BS0], BF16)
        kdrep_d = ctx.din("kdrep", [128, 16 * BS0], BF16)
        ddT_d = ctx.din("ddT", [128, 16 * 2 * BS0], BF16)
        identin = ctx.din("identin", [128, 128], F32)
        bdcol_d = ctx.din("bdcol", [64, 16], F32)
        attnT_d = ctx.dout("attnT", [128, 8 * S], BF16)

        ctx.alloc_sems(["t", "v", "s", "dy", "dg"])
        t, v, s, g, y = ctx.t, ctx.v, ctx.s, ctx.g, ctx.y
        sem, inc, tick = ctx.sem, ctx.inc, ctx.tick

        wsb = ctx.sb("wsb", [128, 16 * 12 * 128], BF16)
        xtile = [ctx.sb(f"xtile{i}", [128, 16 * 512], BF16) for i in range(2)]
        qkvT = ctx.sb("qkvT", [128, 12 * 512], BF16)
        attnT = ctx.sb("attnT_sb", [128, 4 * S], BF16)
        qd_sb = ctx.sb("qd_sb", [128, 16 * BS0], BF16)
        kd_sb = ctx.sb("kd_sb", [128, 16 * BS0], BF16)
        dd_sb = ctx.sb("dd_sb", [128, 16 * 2 * BS0], BF16)
        bd_sb = ctx.sb("bd_sb", [64, 16], F32)
        ident = ctx.sb("ident", [128, 128], BF16)
        qq_sb = ctx.sb("qq_sb", [64, BS0], BF16)
        kk_sb = ctx.sb("kk_sb", [64, BS0], BF16)
        kt_sb = ctx.sb("kt_sb", [128, 2 * 64], BF16)
        vt_sb = ctx.sb("vt_sb", [128, 2 * 64], BF16)
        awdd = ctx.sb("awdd", [128, 2 * BS0], BF16)
        st_f = ctx.sb("st_f", [64, 8 * 64], F32)
        st_b = ctx.sb("st_b", [64, 64], BF16)

        ps_mm = [ctx.ps(f"psmm{i}", [128, 512], F32) for i in range(2)]
        ps_aw = ctx.ps("psaw", [128, 512], F32)
        ps_tr = ctx.ps("pstr", [128, 256], BF16)
        ps_ys = ctx.ps("psys", [64, 256], F32)
        ps_kv = ctx.ps("pskv", [64, 64], F32)

        inc(g.dma_start(out=ident[:], in_=identin[:, :]), "dg", 16)
        inc(g.dma_start(out=qd_sb[:], in_=qdrep_d[:, :]), "dg", 16)
        inc(g.dma_start(out=kd_sb[:], in_=kdrep_d[:, :]), "dg", 16)
        inc(g.dma_start(out=dd_sb[:], in_=ddT_d[:, :]), "dg", 16)
        inc(g.dma_start(out=bd_sb[:], in_=bdcol_d[:, :]), "dg", 16)
        const_ok = tick("dg")

        v.wait_ge(sem["dg"], const_ok)
        inc(v.memset(st_f[:], 0.0), "v")

        evac_tick = [0, 0]
        last_attn_store = 0
        for grp in range(2):
            g.wait_ge(sem["t"], tick("t"))
            wg_done = inc(g.dma_start(out=wsb[:], in_=wqkv_d[grp][:, :]), "dg", 16)

            if grp > 0:
                s.wait_ge(sem["dy"], tick("dy"))
                v.wait_ge(sem["t"], tick("t"))
                inc(v.memset(st_f[:], 0.0), "v")

            for tt in range(NT):
                xb = xtile[tt % 2]
                y.wait_ge(sem["t"], tick("t"))
                if xn0T_d.shape[0] == 256:
                    rb = 128 * (tt // 4)
                    t4 = (tt % 4) * 512
                    xload = inc(y.dma_start(
                        out=xb[:].rearrange("p (c t) -> p c t", c=16),
                        in_=xn0T_d[rb:rb + 128, :].rearrange("p (c t) -> p c t", c=16)[:, :, t4:t4 + 512]), "dy", 16)
                else:
                    xload = inc(y.dma_start(
                        out=xb[:].rearrange("p (c t) -> p c t", c=16),
                        in_=xn0T_d.rearrange("p (c t) -> p c t", c=16)[:, :, tt * 512:(tt + 1) * 512]), "dy", 16)
                qkvT_free_v = tick("v")
                for oc in range(12):
                    pb = ps_mm[oc % 2]
                    t.wait_ge(sem["dy"], xload)
                    t.wait_ge(sem["dg"], wg_done)
                    t.wait_ge(sem["s"], evac_tick[oc % 2])
                    mm = None
                    for fc in range(16):
                        mm = t.matmul(pb[:], wsb[:, (fc * 12 + oc) * 128:(fc * 12 + oc + 1) * 128],
                                      xb[:, fc * 512:(fc + 1) * 512],
                                      start=(fc == 0), stop=(fc == 15))
                    n_t = inc(mm, "t")
                    s.wait_ge(sem["t"], n_t)
                    s.wait_ge(sem["v"], qkvT_free_v)
                    evac_tick[oc % 2] = inc(
                        s.activation(qkvT[:, oc * 512:(oc + 1) * 512], pb[:], AF.Silu), "s")
                qkv_ready = tick("s")
                for half in range(2):
                    for hl in range(8):
                        h16 = grp * 8 + hl
                        ro0 = (hl % 2) * 64
                        qch, kch, vch = hl // 2, 4 + hl // 2, 8 + hl // 2
                        qT = qkvT[ro0:ro0 + 64, qch * 512 + half * 256: qch * 512 + half * 256 + 256]
                        kT = qkvT[ro0:ro0 + 64, kch * 512 + half * 256: kch * 512 + half * 256 + 256]
                        vT = qkvT[ro0:ro0 + 64, vch * 512 + half * 256: vch * 512 + half * 256 + 256]
                        idq = ident[ro0:ro0 + 64, ro0:ro0 + 64]

                        v.wait_ge(sem["s"], qkv_ready)
                        v.wait_ge(sem["t"], tick("t"))
                        v.tensor_tensor(qq_sb[:], qT, qd_sb[ro0:ro0 + 64, h16 * BS0:(h16 + 1) * BS0], ALU.mult)
                        nv_qq = inc(v.tensor_tensor(kk_sb[:], kT, kd_sb[ro0:ro0 + 64, h16 * BS0:(h16 + 1) * BS0], ALU.mult), "v")

                        t.wait_ge(sem["s"], qkv_ready)
                        t.wait_ge(sem["v"], tick("v"))
                        mm = None
                        for jc in range(2):
                            mm = t.matmul(ps_aw[:, jc * 256:(jc + 1) * 256],
                                          kT[:, jc * 128:(jc + 1) * 128], qT,
                                          start=True, stop=True)
                        n_aw = inc(mm, "t")
                        mm = None
                        for jc in range(2):
                            mm = t.transpose(ps_tr[:, jc * 64:(jc + 1) * 64],
                                             kk_sb[:, jc * 128:(jc + 1) * 128],
                                             ident[0:64, 0:64])
                        for jc in range(2):
                            mm = t.transpose(ps_tr[:, 128 + jc * 64:128 + (jc + 1) * 64],
                                             vT[:, jc * 128:(jc + 1) * 128], idq)
                        n_tr = inc(mm, "t")

                        v.wait_ge(sem["t"], n_tr)
                        for jc in range(2):
                            v.tensor_tensor(
                                awdd[:, jc * 256:(jc + 1) * 256],
                                ps_aw[:, jc * 256:(jc + 1) * 256],
                                dd_sb[:, (h16 * 2 + jc) * 256:(h16 * 2 + jc + 1) * 256],
                                ALU.mult)
                        v.tensor_copy(kt_sb[:], ps_tr[:, 0:128])
                        nv_aw = inc(v.tensor_copy(vt_sb[:], ps_tr[:, 128:256]), "v")

                        nv_st = inc(v.tensor_copy(st_b[:], st_f[:, hl * 64:(hl + 1) * 64]), "v")

                        t.wait_ge(sem["v"], nv_st)
                        t.wait_ge(sem["s"], tick("s"))
                        for jc in range(2):
                            t.matmul(ps_ys[:], vt_sb[:, jc * 64:(jc + 1) * 64],
                                     awdd[:, jc * 256:(jc + 1) * 256],
                                     start=(jc == 0), stop=False)
                        mm = t.matmul(ps_ys[:], st_b[:], qq_sb[:], start=False, stop=True)
                        n_ys = inc(mm, "t")
                        mm = None
                        for jc in range(2):
                            mm = t.matmul(ps_kv[:], kt_sb[:, jc * 64:(jc + 1) * 64],
                                          vt_sb[:, jc * 64:(jc + 1) * 64],
                                          start=(jc == 0), stop=(jc == 1))
                        n_kv = inc(mm, "t")

                        s.wait_ge(sem["t"], n_ys)
                        fch = hl // 2
                        ro = (hl % 2) * 64
                        blk0 = tt * 512 + half * 256
                        last_attn_store = inc(
                            s.copy(attnT[ro:ro + 64, fch * S + blk0: fch * S + blk0 + 256],
                                   ps_ys[:]), "s")

                        v.wait_ge(sem["t"], n_kv)
                        nv_up = inc(v.scalar_tensor_tensor(
                            st_f[:, hl * 64:(hl + 1) * 64],
                            st_f[:, hl * 64:(hl + 1) * 64],
                            bd_sb[0:64, h16:h16 + 1],
                            ps_kv[:], ALU.mult, ALU.add), "v")
                        v.wait_ge(sem["v"], nv_up)

            y.wait_ge(sem["s"], last_attn_store)
            if attnT_d.shape[0] == 256:
                inc(y.dma_start(
                    out=attnT_d[0:128, :].rearrange("p (c t) -> p c t", t=TPC)[:, grp * 4:(grp + 1) * 4, :],
                    in_=attnT[:].rearrange("p (c t) -> p c t", t=S)[:, :, 0:TPC]), "dy", 16)
                inc(y.dma_start(
                    out=attnT_d[128:256, :].rearrange("p (c t) -> p c t", t=TPC)[:, grp * 4:(grp + 1) * 4, :],
                    in_=attnT[:].rearrange("p (c t) -> p c t", t=S)[:, :, TPC:S]), "dy", 16)
            else:
                inc(y.dma_start(out=attnT_d[:, grp * 4 * S:(grp + 1) * 4 * S], in_=attnT[:]),
                    "dy", 16)
        y.wait_ge(sem["dy"], tick("dy"))
        return ctx.finalize() if own else None



def build_B(ctx=None, upto=99):
    NT4 = 4
    own = ctx is None
    with (Ctx("B") if own else ctx) as ctx:
        x_d = ctx.din("x", [TPC, D], F32)
        xn0T_d = ctx.din("xn0T", [128, 16 * TPC], BF16)
        attnT_d = ctx.din("attnT", [128, 16 * TPC], BF16)
        normcol_d = ctx.din("normcol", [128, 16], F32)
        gatewT_d = ctx.din("gatewT", [128, 16 * 16 * 128], F32)
        outwT_d = ctx.din("outwT", [128, 16 * 16 * 128], F32)
        wg8_d = ctx.din("wg8", [128, 16 * 64 * 128], U8)
        wu8_d = ctx.din("wu8", [128, 16 * 64 * 128], U8)
        wd8_d = ctx.din("wd8", [128, 64 * 16 * 128], U8)
        gscol_d = ctx.din("gscol", [128, 64], F32)
        uscol_d = ctx.din("uscol", [128, 64], F32)
        dsrow_d = ctx.din("dsrow", [1, D], F32)
        identin = ctx.din("identin", [128, 128], F32)
        ropeC_d = ctx.din("ropeC", [TPC, DH], F32)
        ropeS_d = ctx.din("ropeS", [TPC, DH], F32)
        gwT_d = ctx.din("gwT", [128, 16 * 8], F32)
        gbrep_d = ctx.din("gbrep", [128, 8], F32)
        wdkvT_d = ctx.din("wdkvT", [128, 16 * 4 * 128], F32)
        wdqT_d = ctx.din("wdqT", [128, 16 * 8 * 128], F32)
        wkrT_d = ctx.din("wkrT", [128, 16 * 16 * 128], F32)
        bkv_col_d = ctx.din("bkvcol", [128, 12], F32)
        bkr_rep_d = ctx.din("bkrrep", [128, D], F32)
        selcol_d = ctx.din("selcol", [128, 2], F32)

        x1_d = ctx.dout("x1", [TPC, D], BF16)
        xn1T_d = ctx.dout("xn1T", [128, 16 * TPC], BF16)
        ckvT_d = ctx.dout("ckvT", [128, 4 * TPC], BF16)
        cqT_d = ctx.dout("cqT", [128, 8 * TPC], BF16)
        krT_d = ctx.dout("krT", [128, 16 * TPC], BF16)
        cwrows_d = ctx.dout("cwrows", [8, TPC], F32)

        hw_scr = ctx.nc.dram_tensor("hw_scr", [128, 64 * TPC], BF16).ap()
        hdbg_d = ctx.dout("hdbg", [128, 64 * TPC], BF16) if DEBUG_B else None
        zdbg_d = ctx.dout("zdbg", [128, 16 * TPC], BF16) if DEBUG_B else None
        outw_scr = ctx.nc.dram_tensor("outw_scr", [128, 16 * 16 * 128], BF16).ap()

        ctx.alloc_sems(["t", "v", "s", "dy", "dg", "dz"])
        t, v, s, g, y = ctx.t, ctx.v, ctx.s, ctx.g, ctx.y
        sem, inc, tick = ctx.sem, ctx.inc, ctx.tick

        bufA = ctx.sb("bufA", [128, 16 * TPC], BF16)
        bufB = ctx.sb("bufB", [128, 16 * TPC], BF16)
        hsb = ctx.sb("hsb", [128, 64 * 128], BF16)
        scratch16 = ctx.sb("scratch16", [128, 16384], U8)
        sc_bf = scratch16[:].bitcast(BF16)
        sc_f = scratch16[:].bitcast(F32)
        gwtile = [sc_bf[:, i * 2048:(i + 1) * 2048] for i in range(2)]
        wgu = [sc_bf[:, i * 4096:(i + 1) * 4096] for i in range(2)]
        xtile = sc_f[:, 0:2048]
        scrA = ctx.sb("scrA", [128, D], F32)
        scrB = ctx.sb("scrB", [128, D], F32)
        scrD = ctx.sb("scrD", [128, D], BF16)
        owdm = ctx.sb("owdm", [128, D], BF16)
        rowA = ctx.sb("rowA", [1, D], F32)
        rowB = ctx.sb("rowB", [1, D], F32)
        rsrep = sc_f[:, 2048:4096]
        ident = ctx.sb("ident", [128, 128], F32)
        ncol = ctx.sb("ncol", [128, 16], F32)
        gscol = ctx.sb("gscolsb", [128, 64], F32)
        uscol = ctx.sb("uscolsb", [128, 64], F32)
        onesb = ctx.sb("onesb", [128, 1], BF16)
        ones1f = ctx.sb("ones1f", [1, 512], F32)
        st_ = ctx.sb("stats", [128, 8], F32)
        gwsb = ctx.sb("gwsb", [128, 16 * 8], BF16)
        gbrep = ctx.sb("gbrepsb", [128, 8], F32)
        sctok = ctx.sb("sctok", [128, 8], F32)
        cwtok = ctx.sb("cwtok", [128, 8], F32)
        r8a = ctx.sb("r8a", [128, 8], F32)
        r8b = ctx.sb("r8b", [128, 8], F32)
        r8c = ctx.sb("r8c", [128, 8], F32)
        bkvcol = ctx.sb("bkvcolsb", [128, 12], F32)
        selcol = ctx.sb("selcolsb", [128, 2], F32)

        psA = ctx.ps("psA", [128, 1024], F32)
        psB = ctx.ps("psB", [128, 1024], F32)
        psC = ctx.ps("psC", [128, 2048], F32)

        for ap_sb, ap_d in ((ident[:], identin), (ncol[:], normcol_d),
                            (gscol[:], gscol_d), (uscol[:], uscol_d),
                            (gwsb[:], gwT_d), (gbrep[:], gbrep_d),
                            (bkvcol[:], bkv_col_d)):
            inc(g.dma_start(out=ap_sb, in_=ap_d[:, :]), "dg", 16)
        inc(g.dma_start(out=selcol[:], in_=selcol_d[:, :]), "dg", 16)
        n_c0 = tick("dg")
        inc(y.dma_start(out=bufA[:], in_=xn0T_d[:, :]), "dy", 16)
        if attnT_d.shape[0] == 512:
            inc(y.dma_start(out=bufB[:, 0:8 * TPC], in_=attnT_d[0:128, :]), "dy", 16)
            inc(y.dma_start(out=bufB[:, 8 * TPC:16 * TPC], in_=attnT_d[256:384, :]),
                "dy", 16)
            n_res = tick("dy")
            scab = scrA[:].bitcast(BF16)
            ev_bl = 0
            for fcb in range(16):
                rb2 = 128 if fcb < 8 else 384
                fcl = fcb % 8
                y.wait_ge(sem["v"], ev_bl)
                lhi = inc(y.dma_start(
                    out=scab[:, 0:TPC],
                    in_=attnT_d[rb2:rb2 + 128, fcl * TPC:(fcl + 1) * TPC]), "dy", 16)
                v.wait_ge(sem["dy"], lhi)
                if fcb == 0:
                    v.wait_ge(sem["dg"], n_c0)
                v.tensor_scalar(bufB[:, fcb * TPC:(fcb + 1) * TPC],
                                bufB[:, fcb * TPC:(fcb + 1) * TPC],
                                selcol[:, 0:1], None, ALU.mult)
                ev_bl = inc(v.scalar_tensor_tensor(
                    bufB[:, fcb * TPC:(fcb + 1) * TPC], scab[:, 0:TPC],
                    selcol[:, 1:2], bufB[:, fcb * TPC:(fcb + 1) * TPC],
                    ALU.mult, ALU.add), "v")
            blend_done = ev_bl
        else:
            inc(y.dma_start(out=bufB[:], in_=attnT_d[:, :]), "dy", 16)
            n_res = tick("dy")
            blend_done = 0
        v.wait_ge(sem["dg"], n_c0)
        v.memset(onesb[:], 1.0)
        nv0 = inc(v.memset(ones1f[:], 1.0), "v")
        for fc in range(16):
            dst = gwtile[fc % 2]
            g.wait_ge(sem["dz"], max(0, (fc - 1)) * 16)
            ld = inc(g.dma_start(out=dst, in_=outwT_d[:, fc * 2048:(fc + 1) * 2048]),
                     "dg", 16)
            y.wait_ge(sem["dg"], ld)
            inc(y.dma_start(out=outw_scr[:, fc * 2048:(fc + 1) * 2048], in_=dst),
                "dz", 16)
        outw_ready = tick("dz")

        if upto <= 0:
            y.wait_ge(sem["dz"], tick("dz")); y.wait_ge(sem["dy"], tick("dy"))
            return ctx.finalize() if own else None if own else None
        t.wait_ge(sem["dy"], n_res)
        t.wait_ge(sem["v"], nv0)
        for qt in range(4):
            t.wait_ge(sem["v"], tick("v"))
            mm = None
            for fc in range(16):
                mm = t.matmul(psA[0:1, 0:512], onesb[:],
                              bufA[:, fc * TPC + qt * 512: fc * TPC + (qt + 1) * 512],
                              start=(fc == 0), stop=(fc == 15))
            n_t = inc(mm, "t")
            v.wait_ge(sem["t"], n_t)
            inc(v.tensor_scalar(rowB[:, qt * 512:(qt + 1) * 512], psA[0:1, 0:512],
                                -128.0, None, ALU.mult), "v")
        sq = scrD[:, 0:512]
        s.wait_ge(sem["dy"], n_res)
        if blend_done:
            s.wait_ge(sem["v"], blend_done)
        for qt in range(4):
            for fc in range(16):
                ns = inc(s.activation(
                    sq, bufB[:, fc * TPC + qt * 512: fc * TPC + (qt + 1) * 512],
                    AF.Square), "s")
                t.wait_ge(sem["s"], ns)
                if fc == 0:
                    t.wait_ge(sem["v"], tick("v"))
                n_t = inc(t.matmul(psA[0:1, 0:512], onesb[:], sq,
                                   start=(fc == 0), stop=(fc == 15)), "t")
                if fc < 15:
                    s.wait_ge(sem["t"], n_t)
            v.wait_ge(sem["t"], tick("t"))
            n_v = inc(v.tensor_scalar(rowA[:, 0:512], psA[0:1, 0:512],
                                      1.0 / D, float(EPS), ALU.mult, ALU.add), "v")
            s.wait_ge(sem["v"], n_v)
            s.activation(rowA[:, 0:512], rowA[:, 0:512], AF.Ln, scale=1.0)
            n_s = inc(s.activation(rowA[:, 0:512], rowA[:, 0:512], AF.Exp, scale=-0.5),
                      "s")
            t.wait_ge(sem["s"], n_s)
            t.wait_ge(sem["v"], tick("v"))
            nb = inc(t.matmul(psB[:, 0:512], ones1f[:, 0:128], rowA[:, 0:512],
                              start=True, stop=True), "t")
            v.wait_ge(sem["t"], nb)
            inc(v.tensor_copy(rsrep[:, qt * 512:(qt + 1) * 512], psB[:, 0:512]), "v")
        rs_done_v = tick("v")

        if upto <= 1:
            y.wait_ge(sem["dz"], tick("dz")); y.wait_ge(sem["dy"], tick("dy"))
            return ctx.finalize() if own else None if own else None
        g.wait_ge(sem["dz"], outw_ready)
        evac_sg = [0, 0]
        zt1 = scrA
        gsig = scrD
        for oc in range(16):
            wt = gwtile[oc % 2]
            g.wait_ge(sem["t"], tick("t"))
            g.wait_ge(sem["s"], tick("s"))
            ldg = inc(g.dma_start(
                out=wt.rearrange("p (f o) -> p f o", o=128),
                in_=gatewT_d.rearrange("p (f o) -> p f o", o=128)[:, oc::16, :]), "dg", 16)
            t.wait_ge(sem["dg"], ldg)
            for tt in range(NT4):
                pb = (psA if tt % 2 == 0 else psB)[:, 0:512]
                t.wait_ge(sem["s"], evac_sg[tt % 2])
                mm = None
                for fc in range(16):
                    mm = t.matmul(pb, wt[:, fc * 128:(fc + 1) * 128],
                                  bufA[:, fc * TPC + tt * 512: fc * TPC + (tt + 1) * 512],
                                  start=(fc == 0), stop=(fc == 15))
                n_g = inc(mm, "t")
                s.wait_ge(sem["t"], n_g)
                if oc == 0 and tt == 0:
                    s.wait_ge(sem["v"], tick("v"))
                evac_sg[tt % 2] = inc(
                    s.activation(gsig[:, tt * 512:(tt + 1) * 512], pb, AF.Sigmoid), "s")
            n_gs = tick("s")
            v.wait_ge(sem["s"], n_gs)
            v.tensor_tensor(zt1[:], bufB[:, oc * TPC:(oc + 1) * TPC], rsrep, ALU.mult)
            v.tensor_scalar(zt1[:], zt1[:], ncol[:, oc:oc + 1], None, ALU.mult)
            inc(v.tensor_tensor(bufB[:, oc * TPC:(oc + 1) * TPC], zt1[:], gsig[:],
                                ALU.mult), "v")
        z_done_v = tick("v")
        if DEBUG_B:
            y.wait_ge(sem["v"], z_done_v)
            inc(y.dma_start(out=zdbg_d[:, :], in_=bufB[:]), "dz", 16)

        if upto <= 2:
            y.wait_ge(sem["dz"], tick("dz")); y.wait_ge(sem["dy"], tick("dy"))
            return ctx.finalize() if own else None if own else None
        silu_g = scrA
        hrow = scrD
        g.wait_ge(sem["v"], z_done_v)
        wgu_last = [0, 0]
        rq = [psB[0:1, 0:512], psB[0:1, 512:1024], psC[0:1, 0:512], psC[0:1, 512:1024]]
        n_rsum = 0
        for ic in range(64):
            wb = wgu[ic % 2]
            g.wait_ge(sem["t"], wgu_last[ic % 2])
            l1 = inc(g.dma_start(
                out=wb[:, 0:2048].rearrange("p (f o) -> p f o", o=128),
                in_=wg8_d.rearrange("p (f i) -> p f i", i=64 * 128)[:, :, ic * 128:(ic + 1) * 128]),
                "dg", 16)
            l2 = inc(g.dma_start(
                out=wb[:, 2048:4096].rearrange("p (f o) -> p f o", o=128),
                in_=wu8_d.rearrange("p (f i) -> p f i", i=64 * 128)[:, :, ic * 128:(ic + 1) * 128]),
                "dg", 16)
            t.wait_ge(sem["dg"], l2)
            for tt in range(NT4):
                pg = psA[:, 0:512]
                pu = psA[:, 512:1024]
                t.wait_ge(sem["s"], tick("s"))
                t.wait_ge(sem["v"], tick("v"))
                for fc in range(16):
                    t.matmul(pg, wb[:, fc * 128:(fc + 1) * 128],
                             bufA[:, fc * TPC + tt * 512: fc * TPC + (tt + 1) * 512],
                             start=(fc == 0), stop=False)
                n_gm = inc(t.matmul(pg, ones1f[:, 0:128],
                                    rowB[:, tt * 512:(tt + 1) * 512],
                                    start=False, stop=True), "t")
                for fc in range(16):
                    t.matmul(pu, wb[:, 2048 + fc * 128:2048 + (fc + 1) * 128],
                             bufA[:, fc * TPC + tt * 512: fc * TPC + (tt + 1) * 512],
                             start=(fc == 0), stop=False)
                n_um = inc(t.matmul(pu, ones1f[:, 0:128],
                                    rowB[:, tt * 512:(tt + 1) * 512],
                                    start=False, stop=True), "t")
                s.wait_ge(sem["t"], n_gm)
                n_sg = inc(s.activation(silu_g[:, tt * 512:(tt + 1) * 512], pg,
                                        AF.Silu, scale=gscol[:, ic:ic + 1]), "s")
                v.wait_ge(sem["t"], n_um)
                v.wait_ge(sem["s"], n_sg)
                inc(v.scalar_tensor_tensor(hrow[:, tt * 512:(tt + 1) * 512], pu,
                                           uscol[:, ic:ic + 1],
                                           silu_g[:, tt * 512:(tt + 1) * 512],
                                           ALU.mult, ALU.mult), "v")
            wgu_last[ic % 2] = tick("t")
            nv_h = tick("v")
            t.wait_ge(sem["v"], nv_h)
            mm = None
            for tt in range(NT4):
                mm = t.matmul(rq[tt], onesb[:], hrow[:, tt * 512:(tt + 1) * 512],
                              start=(ic == 0), stop=(ic == 63))
            n_rsum = inc(mm, "t")
            y.wait_ge(sem["v"], nv_h)
            inc(y.dma_start(out=hw_scr[:, ic * TPC:(ic + 1) * TPC], in_=hrow[:]), "dz", 16)
        hw_done = tick("dz")
        if DEBUG_B:
            y.wait_ge(sem["dz"], hw_done)
            inc(y.dma_start(out=hdbg_d[:, :], in_=hw_scr[:, :]), "dz", 16)
        v.wait_ge(sem["t"], n_rsum)
        for tt in range(NT4):
            inc(v.tensor_scalar(rowA[:, tt * 512:(tt + 1) * 512], rq[tt],
                                -128.0, None, ALU.mult), "v")
        nv_rh = tick("v")
        g.wait_ge(sem["t"], tick("t"))
        lds = inc(g.dma_start(out=rowB[:], in_=dsrow_d[:, :]), "dg", 16)
        t.wait_ge(sem["dg"], lds)
        t.wait_ge(sem["v"], nv_rh)
        for qt in range(4):
            t.wait_ge(sem["v"], tick("v"))
            nb = inc(t.matmul(psB[:, 0:512], ones1f[:, 0:128],
                              rowB[:, qt * 512:(qt + 1) * 512], start=True, stop=True),
                     "t")
            v.wait_ge(sem["t"], nb)
            inc(v.tensor_copy(rsrep[:, qt * 512:(qt + 1) * 512], psB[:, 0:512]), "v")
        ds_done_v = tick("v")
        dsrep = rsrep

        if upto <= 3:
            y.wait_ge(sem["dz"], tick("dz")); y.wait_ge(sem["dy"], tick("dy"))
            return ctx.finalize() if own else None if own else None
        x1f = scrA
        xn1tok = scrB
        x1b = scrD
        psum_free_v = ds_done_v
        hsb_last_t = 0
        owdm_last = 0
        for tc in range(16):
            trow = tc * 128
            y.wait_ge(sem["dz"], hw_done)
            y.wait_ge(sem["t"], hsb_last_t)
            hld = inc(y.dma_start(
                out=hsb[:].rearrange("p (i c) -> p i c", c=128),
                in_=hw_scr.rearrange("p (i c) -> p i c", c=TPC)[:, :, trow:trow + 128]),
                "dy", 16)
            y.wait_ge(sem["v"], tick("v"))
            xld = inc(y.dma_start(out=xtile, in_=x_d[trow:trow + 128, :]), "dy", 16)
            for fc in range(16):
                g.wait_ge(sem["t"], owdm_last)
                lo = inc(g.dma_start(
                    out=owdm[:], in_=outw_scr[:, fc * 2048:(fc + 1) * 2048]), "dg", 16)
                t.wait_ge(sem["dg"], lo)
                if fc == 0:
                    t.wait_ge(sem["v"], psum_free_v)
                    if tc == 0:
                        t.wait_ge(sem["s"], tick("s"))
                mm = None
                for ot in range(4):
                    pa = (psA if ot < 2 else psB)[:, (ot % 2) * 512:(ot % 2) * 512 + 512]
                    mm = t.matmul(pa,
                                  bufB[:, fc * TPC + trow: fc * TPC + trow + 128],
                                  owdm[:, ot * 512:(ot + 1) * 512],
                                  start=(fc == 0), stop=(fc == 15))
                owdm_last = inc(mm, "t")
            for ic in range(64):
                g.wait_ge(sem["t"], owdm_last)
                lw = inc(g.dma_start(
                    out=owdm[:], in_=wd8_d[:, ic * 2048:(ic + 1) * 2048]), "dg", 16)
                t.wait_ge(sem["dg"], lw)
                if ic == 0:
                    t.wait_ge(sem["dy"], hld)
                mm = None
                for ot in range(4):
                    mm = t.matmul(psC[:, ot * 512:(ot + 1) * 512],
                                  hsb[:, ic * 128:(ic + 1) * 128],
                                  owdm[:, ot * 512:(ot + 1) * 512],
                                  start=(ic == 0), stop=False)
                owdm_last = inc(mm, "t")
            mm = None
            for ot in range(4):
                mm = t.matmul(psC[:, ot * 512:(ot + 1) * 512],
                              rowA[:, trow:trow + 128],
                              ones1f[:, 0:512],
                              start=False, stop=True)
            inc(mm, "t")
            n_beta = tick("t")
            hsb_last_t = n_beta
            v.wait_ge(sem["t"], n_beta)
            v.wait_ge(sem["dy"], xld)
            v.tensor_tensor(x1f[:], psC[:], dsrep, ALU.mult)
            v.tensor_tensor(x1f[:, 0:1024], x1f[:, 0:1024], psA[:], ALU.add)
            v.tensor_tensor(x1f[:, 1024:2048], x1f[:, 1024:2048], psB[:], ALU.add)
            nv_x1 = inc(v.tensor_tensor(x1f[:], x1f[:], xtile, ALU.add), "v")
            v.wait_ge(sem["v"], nv_x1)
            nv_x1b = inc(v.tensor_copy(x1b[:], x1f[:]), "v")
            y.wait_ge(sem["v"], nv_x1b)
            inc(y.dma_start(out=x1_d[trow:trow + 128, :], in_=x1b[:]), "dz", 16)
            v.reduce_sum(st_[:, 0:1], x1f[:], axis=AX)
            v.tensor_mul(xn1tok[:], x1f[:], x1f[:])
            n_v = inc(v.reduce_sum(st_[:, 1:2], xn1tok[:], axis=AX), "v")
            v.wait_ge(sem["v"], n_v)
            v.tensor_scalar(st_[:, 2:3], st_[:, 0:1], 1.0 / D, None, ALU.mult)
            n_v = inc(v.tensor_scalar(st_[:, 3:4], st_[:, 1:2], 1.0 / D, None,
                                      ALU.mult), "v")
            v.wait_ge(sem["v"], n_v)
            n_v = inc(v.tensor_mul(st_[:, 4:5], st_[:, 2:3], st_[:, 2:3]), "v")
            v.wait_ge(sem["v"], n_v)
            n_v = inc(v.scalar_tensor_tensor(st_[:, 4:5], st_[:, 3:4], float(EPS),
                                             st_[:, 4:5], ALU.add, ALU.subtract), "v")
            s.wait_ge(sem["v"], n_v)
            s.activation(st_[:, 5:6], st_[:, 4:5], AF.Ln, scale=1.0)
            n_s = inc(s.activation(st_[:, 5:6], st_[:, 5:6], AF.Exp, scale=-0.5), "s")
            v.wait_ge(sem["s"], n_s)
            n_v = inc(v.scalar_tensor_tensor(st_[:, 6:7], st_[:, 2:3], -1.0,
                                             st_[:, 5:6], ALU.mult, ALU.mult), "v")
            v.wait_ge(sem["v"], n_v)
            nv_xn1 = inc(v.tensor_scalar(xn1tok[:], x1f[:], st_[:, 5:6], st_[:, 6:7],
                                         ALU.mult, ALU.add), "v")
            t.wait_ge(sem["v"], nv_xn1)
            evts = [nv_xn1, nv_xn1]
            for fc in range(16):
                pb = (psA if fc % 2 == 0 else psB)[:, 0:128]
                t.wait_ge(sem["v"], evts[fc % 2])
                n_t = inc(t.transpose(pb, xn1tok[:, fc * 128:(fc + 1) * 128],
                                      ident[:]), "t")
                v.wait_ge(sem["t"], n_t)
                evts[fc % 2] = inc(v.tensor_copy(
                    bufA[:, fc * TPC + trow: fc * TPC + trow + 128], pb), "v")
            nv_strip = tick("v")
            t.wait_ge(sem["v"], nv_strip)
            mm = None
            for fc in range(16):
                mm = t.matmul(psC[0:8, 0:128], gwsb[:, fc * 8:(fc + 1) * 8],
                              bufA[:, fc * TPC + trow: fc * TPC + trow + 128],
                              start=(fc == 0), stop=(fc == 15))
            n_sc = inc(mm, "t")
            v.wait_ge(sem["t"], n_sc)
            sc_row = xn1tok[0:8, 128:256]
            n_v = inc(v.tensor_copy(sc_row, psC[0:8, 0:128]), "v")
            t.wait_ge(sem["v"], n_v)
            n_t = inc(t.transpose(psA[0:128, 0:8], sc_row, ident[0:8, 0:8]), "t")
            v.wait_ge(sem["t"], n_t)
            n_v = inc(v.tensor_copy(sctok[:], psA[0:128, 0:8]), "v")
            v.wait_ge(sem["v"], n_v)
            n_v = inc(v.tensor_scalar(r8c[:], sctok[:], -1.0, None, ALU.mult), "v")
            v.wait_ge(sem["v"], n_v)
            n_v = inc(v.tensor_tensor(r8a[:], sctok[:], r8c[:], ALU.max), "v")
            s.wait_ge(sem["v"], n_v)
            s.activation(r8a[:], r8a[:], AF.Exp, scale=-1.0)
            n_s = inc(s.activation(r8a[:], r8a[:], AF.Ln, bias=1.0, scale=1.0), "s")
            v.wait_ge(sem["s"], n_s)
            n_v = inc(v.tensor_scalar(r8b[:], sctok[:], 0.0, None, ALU.max), "v")
            v.wait_ge(sem["v"], n_v)
            n_v = inc(v.tensor_add(r8a[:], r8a[:], r8b[:]), "v")
            s.wait_ge(sem["v"], n_v)
            s.activation(r8a[:], r8a[:], AF.Ln, scale=1.0)
            n_s = inc(s.activation(r8a[:], r8a[:], AF.Exp, scale=0.5), "s")
            v.wait_ge(sem["s"], n_s)
            n_v = inc(v.tensor_add(r8b[:], r8a[:], gbrep[:]), "v")
            v.wait_ge(sem["v"], n_v)
            n_v = inc(v.reduce_max(st_[:, 0:1], r8b[:], axis=AX), "v")
            v.wait_ge(sem["v"], n_v)
            n_v = inc(v.tensor_scalar(r8c[:], r8b[:], st_[:, 0:1], None, ALU.is_equal),
                      "v")
            v.wait_ge(sem["v"], n_v)
            n_v = inc(v.scalar_tensor_tensor(r8b[:], r8c[:], -1e30, r8b[:],
                                             ALU.mult, ALU.add), "v")
            v.wait_ge(sem["v"], n_v)
            n_v = inc(v.reduce_max(st_[:, 1:2], r8b[:], axis=AX), "v")
            v.wait_ge(sem["v"], n_v)
            n_v = inc(v.tensor_scalar(r8b[:], r8b[:], st_[:, 1:2], None, ALU.is_equal),
                      "v")
            v.wait_ge(sem["v"], n_v)
            n_v = inc(v.tensor_add(r8c[:], r8c[:], r8b[:]), "v")
            v.wait_ge(sem["v"], n_v)
            n_v = inc(v.tensor_mul(cwtok[:], r8c[:], r8a[:]), "v")
            t.wait_ge(sem["v"], n_v)
            n_t = inc(t.transpose(psA[0:8, 0:128], cwtok[:], ident[:]), "t")
            v.wait_ge(sem["t"], n_t)
            v.wait_ge(sem["dz"], tick("dz"))
            cw_row = xn1tok[0:8, 0:128]
            psum_free_v = inc(v.tensor_copy(cw_row, psA[0:8, 0:128]), "v")
            y.wait_ge(sem["v"], psum_free_v)
            inc(y.dma_start(out=cwrows_d[:, trow:trow + 128], in_=cw_row), "dz", 16)

        y.wait_ge(sem["v"], tick("v"))
        inc(y.dma_start(out=xn1T_d[:, :], in_=bufA[:]), "dz", 16)

        if upto <= 4:
            y.wait_ge(sem["dz"], tick("dz")); y.wait_ge(sem["dy"], tick("dy"))
            return ctx.finalize() if own else None if own else None
        cvq = scrA
        for dcc in range(12):
            wsrc = wdkvT_d if dcc < 4 else wdqT_d
            nch = 4 if dcc < 4 else 8
            wofs = dcc if dcc < 4 else dcc - 4
            wt = gwtile[dcc % 2]
            g.wait_ge(sem["t"], tick("t"))
            g.wait_ge(sem["dz"], tick("dz"))
            ldw = inc(g.dma_start(
                out=wt[:, 0:16 * 128].rearrange("p (f o) -> p f o", o=128),
                in_=wsrc.rearrange("p (f o) -> p f o", o=128)[:, wofs::nch, :]), "dg", 16)
            t.wait_ge(sem["dg"], ldw)
            v.wait_ge(sem["dz"], tick("dz"))
            for tt in range(NT4):
                pb = (psA if tt % 2 == 0 else psB)[:, 0:512]
                t.wait_ge(sem["v"], tick("v"))
                mm = None
                for fc in range(16):
                    mm = t.matmul(pb, wt[:, fc * 128:(fc + 1) * 128],
                                  bufA[:, fc * TPC + tt * 512: fc * TPC + (tt + 1) * 512],
                                  start=(fc == 0), stop=(fc == 15))
                n_t = inc(mm, "t")
                v.wait_ge(sem["t"], n_t)
                inc(v.tensor_scalar(cvq[:, tt * 512:(tt + 1) * 512], pb,
                                    bkvcol[:, dcc:dcc + 1], None, ALU.add), "v")
            nvp = tick("v")
            g.wait_ge(sem["v"], nvp)
            if dcc < 4:
                inc(g.dma_start(out=ckvT_d[:, dcc * TPC:(dcc + 1) * TPC], in_=cvq[:]),
                    "dz", 16)
            else:
                inc(g.dma_start(out=cqT_d[:, (dcc - 4) * TPC:(dcc - 3) * TPC],
                                in_=cvq[:]), "dz", 16)
        if upto <= 5:
            y.wait_ge(sem["dz"], tick("dz")); y.wait_ge(sem["dy"], tick("dy"))
            return ctx.finalize() if own else None if own else None
        g.wait_ge(sem["t"], tick("t"))
        g.wait_ge(sem["v"], tick("v"))
        nwkr = inc(g.dma_start(out=bufB[:], in_=wkrT_d[:, :]), "dg", 16)
        lbk = inc(g.dma_start(out=rsrep, in_=bkr_rep_d[:, :]), "dg", 16)
        ropeC = owdm[:].bitcast(F32)[:, 0:128]
        ropeS = owdm[:].bitcast(F32)[:, 128:256]
        bkrrep = rsrep
        krtok = scrA
        krsw = scrB
        krTb = scrD
        kr_ev = tick("v")
        t.wait_ge(sem["dg"], lbk)
        for tc in range(16):
            trow = tc * 128
            g.wait_ge(sem["v"], tick("v"))
            lr1 = inc(g.dma_start(out=ropeC, in_=ropeC_d[trow:trow + 128, :]), "dg", 16)
            lr2 = inc(g.dma_start(out=ropeS, in_=ropeS_d[trow:trow + 128, :]), "dg", 16)
            t.wait_ge(sem["dg"], lr2)
            t.wait_ge(sem["v"], kr_ev)
            mm = None
            for ot in range(4):
                for fc in range(16):
                    mm = t.matmul(psC[:, ot * 512:(ot + 1) * 512],
                                  bufA[:, fc * TPC + trow: fc * TPC + trow + 128],
                                  bufB[:, fc * TPC + ot * 512: fc * TPC + (ot + 1) * 512],
                                  start=(fc == 0), stop=(fc == 15))
            n_kr = inc(mm, "t")
            v.wait_ge(sem["t"], n_kr)
            v.wait_ge(sem["dz"], tick("dz"))
            kr_ev = inc(v.tensor_tensor(krtok[:], psC[:], bkrrep, ALU.add), "v")
            v.wait_ge(sem["v"], kr_ev)
            v.tensor_scalar(krsw[:, 0:D:2], krtok[:, 1:D:2], -1.0, None, ALU.mult)
            n_v = inc(v.tensor_copy(krsw[:, 1:D:2], krtok[:, 0:D:2]), "v")
            v.wait_ge(sem["v"], n_v)
            last = None
            for hh in range(16):
                sl = slice(hh * 128, (hh + 1) * 128)
                v.tensor_tensor(krtok[:, sl], krtok[:, sl], ropeC, ALU.mult)
                last = v.tensor_tensor(krsw[:, sl], krsw[:, sl], ropeS, ALU.mult)
            n_v = inc(last, "v")
            v.wait_ge(sem["v"], n_v)
            nv_rope = inc(v.tensor_tensor(krtok[:], krtok[:], krsw[:], ALU.add), "v")
            t.wait_ge(sem["v"], nv_rope)
            evts = [nv_rope, nv_rope]
            for fc in range(16):
                pb = (psA if fc % 2 == 0 else psB)[:, 0:128]
                t.wait_ge(sem["v"], evts[fc % 2])
                n_t = inc(t.transpose(pb, krtok[:, fc * 128:(fc + 1) * 128], ident[:]),
                          "t")
                v.wait_ge(sem["t"], n_t)
                evts[fc % 2] = inc(
                    v.tensor_copy(krTb[:, fc * 128:(fc + 1) * 128], pb), "v")
            nv_kr = tick("v")
            y.wait_ge(sem["v"], nv_kr)
            inc(y.dma_start(
                out=krT_d.rearrange("p (f t) -> p f t", t=TPC)[:, :, trow:trow + 128],
                in_=krTb[:].rearrange("p (f t) -> p f t", t=128)), "dz", 16)
        y.wait_ge(sem["dz"], tick("dz"))
        y.wait_ge(sem["dy"], tick("dy"))
        return ctx.finalize() if own else None



def build_C(ctx=None):
    own = ctx is None
    with (Ctx("C") if own else ctx) as ctx:
        ckvT_d = ctx.din("ckvT", [128, 4 * S], BF16)
        cqT_d = ctx.din("cqT", [128, 8 * S], BF16)
        krT_d = ctx.din("krT", [128, 8 * S], BF16)
        wukT_d = ctx.din("wukT", [128, 4 * 8 * 128], F32)
        wuvT_d = ctx.din("wuvT", [128, 4 * 8 * 128], F32)
        wuqT_d = ctx.din("wuqT", [128, 8 * 8 * 128], F32)
        buk_d = ctx.din("bukcol", [128, 8], F32)
        buvrep_d = ctx.din("buvrep", [128, 8 * 128], F32)
        buq_d = ctx.din("buqcol", [128, 8], F32)
        ropeCF_d = ctx.din("ropeCF", [S, DH], F32)
        ropeSF_d = ctx.din("ropeSF", [S, DH], F32)
        dmask_d = ctx.din("dmask", [4 * 128, 512], F32)
        identin = ctx.din("identin", [128, 128], F32)
        selcol_d = ctx.din("selcol", [128, 2], F32)
        oT_d = ctx.dout("oT", [128, 8 * S], BF16)

        ctx.alloc_sems(["t", "v", "s", "dy", "dg", "dz"])
        t, v, s, g, y = ctx.t, ctx.v, ctx.s, ctx.g, ctx.y
        sem, inc, tick = ctx.sem, ctx.inc, ctx.tick

        ckv = ctx.sb("ckv", [128, 4 * S], BF16)
        cq = ctx.sb("cq", [128, 8 * S], BF16)
        kC = ctx.sb("kC", [128, S], BF16)
        kR = ctx.sb("kR", [128, S], BF16)
        qC = ctx.sb("qC", [128, S], BF16)
        qR = ctx.sb("qR", [128, S], BF16)
        vsb = ctx.sb("vsb", [128, S], BF16)
        oT = ctx.sb("oT_sb", [128, S], BF16)
        wuk = ctx.sb("wuk", [128, 4 * 8 * 128], BF16)
        wuv = ctx.sb("wuv", [128, 4 * 8 * 128], BF16)
        wuq = ctx.sb("wuq", [128, 8 * 8 * 128], BF16)
        bukc = ctx.sb("bukc", [128, 8], F32)
        buvrep = ctx.sb("buvrep_sb", [128, 8 * 128], F32)
        buqc = ctx.sb("buqc", [128, 8], F32)
        dmask = ctx.sb("dmask_sb", [128, 4 * 512], BF16)
        ident = ctx.sb("identsb", [128, 128], F32)
        identb = ctx.sb("identbsb", [128, 128], BF16)
        ropeC = ctx.sb("ropeCsb", [128, DH], F32)
        ropeS = ctx.sb("ropeSsb", [128, DH], F32)
        qtok = ctx.sb("qtok", [128, 128], BF16)
        qsw = ctx.sb("qsw", [128, 128], F32)
        qrt = ctx.sb("qrt", [128, 128], BF16)
        ptile = [ctx.sb(f"ptile{i}", [128, 512], BF16) for i in range(3)]
        onesb = ctx.sb("onesbc", [128, 1], BF16)
        ones1f = ctx.sb("ones1fc", [1, 128], F32)
        sqrow = ctx.sb("sqrow", [128, 512], BF16)
        nrow = ctx.sb("nrow", [1, 512], F32)
        sc1 = ctx.sb("sc1", [1, 8], F32)
        selcolC = ctx.sb("selcolC", [128, 2], F32)
        mcol = ctx.sb("mcol", [128, 1], F32)
        zrow = ctx.sb("zrow", [1, 512], F32)
        zrep = ctx.sb("zrep", [128, 512], F32)

        psS = ctx.ps("psS", [128, 1024], F32)
        psO = ctx.ps("psO", [128, 512], F32)
        psK = ctx.ps("psK", [128, 512], F32)
        psZ = ctx.ps("psZ", [1, 512], F32)
        psD = ctx.ps("psD", [128, 256], BF16)

        SCALE = float(1.0 / np.sqrt(np.float32(2 * DH)).astype(np.float32))

        FUSED_C = ckvT_d.shape[0] == 256
        if FUSED_C:
            for nch, dst, srcd in ((4, ckv, ckvT_d), (8, cq, cqT_d)):
                inc(y.dma_start(
                    out=dst[:].rearrange("p (c t) -> p c t", t=S)[:, :, 0:TPC],
                    in_=srcd[0:128, :].rearrange("p (c t) -> p c t", t=TPC)), "dy", 16)
                inc(y.dma_start(
                    out=dst[:].rearrange("p (c t) -> p c t", t=S)[:, :, TPC:S],
                    in_=srcd[128:256, :].rearrange("p (c t) -> p c t", t=TPC)), "dy", 16)
        else:
            inc(y.dma_start(out=ckv[:], in_=ckvT_d[:, :]), "dy", 16)
            inc(y.dma_start(out=cq[:], in_=cqT_d[:, :]), "dy", 16)
        inc(g.dma_start(out=selcolC[:], in_=selcol_d[:, :]), "dg", 16)
        for ap_sb, ap_d in ((wuk, wukT_d), (wuv, wuvT_d), (wuq, wuqT_d),
                            (ident, identin)):
            inc(g.dma_start(out=ap_sb[:], in_=ap_d[:, :]), "dg", 16)
        inc(g.dma_start(out=dmask[:].rearrange("k (o q) -> k o q", q=512),
                        in_=dmask_d.rearrange("(o k) q -> k o q", k=128)), "dg", 16)
        for ap_sb, ap_d in ((bukc, buk_d), (buvrep, buvrep_d), (buqc, buq_d)):
            inc(g.dma_start(out=ap_sb[:], in_=ap_d[:, :]), "dg", 16)
        ld0 = tick("dg")
        ldy0 = tick("dy")
        v.wait_ge(sem["dg"], ld0)
        v.wait_ge(sem["dy"], ldy0)
        v.memset(onesb[:], 1.0)
        v.memset(ones1f[:], 1.0)
        nv0 = inc(v.tensor_copy(identb[:], ident[:]), "v")

        for hl in range(8):
            t_prev = tick("t")
            g.wait_ge(sem["t"], t_prev)
            if FUSED_C:
                g.wait_ge(sem["v"], tick("v"))
                for cand, dst in ((hl, kR), (8 + hl, qR)):
                    inc(g.dma_start(out=dst[:, 0:TPC],
                                    in_=krT_d[0:128, cand * TPC:(cand + 1) * TPC]),
                        "dg", 16)
                    inc(g.dma_start(out=dst[:, TPC:S],
                                    in_=krT_d[128:256, cand * TPC:(cand + 1) * TPC]),
                        "dg", 16)
                krl0 = tick("dg")
                v.wait_ge(sem["dg"], krl0)
                v.tensor_scalar(kR[:], kR[:], selcolC[:, 0:1], None, ALU.mult)
                nbl = inc(v.scalar_tensor_tensor(kR[:], qR[:], selcolC[:, 1:2], kR[:],
                                                 ALU.mult, ALU.add), "v")
                t.wait_ge(sem["v"], nbl)
                krl = 0
            else:
                krl = inc(g.dma_start(out=kR[:], in_=krT_d[:, hl * S:(hl + 1) * S]), "dg", 16)
            evk = [tick("v")] * 2
            for tt in range(8):
                pb = psS[:, (tt % 2) * 512:(tt % 2) * 512 + 512]
                t.wait_ge(sem["v"], evk[tt % 2])
                if hl == 0 and tt == 0:
                    t.wait_ge(sem["v"], nv0)
                mm = None
                for dcc in range(4):
                    mm = t.matmul(pb, wuk[:, (dcc * 8 + hl) * 128:(dcc * 8 + hl + 1) * 128],
                                  ckv[:, dcc * S + tt * 512: dcc * S + (tt + 1) * 512],
                                  start=(dcc == 0), stop=(dcc == 3))
                n_t = inc(mm, "t")
                v.wait_ge(sem["t"], n_t)
                evk[tt % 2] = inc(v.tensor_scalar(kC[:, tt * 512:(tt + 1) * 512], pb,
                                                  bukc[:, hl:hl + 1], None, ALU.add), "v")
            for tt in range(8):
                pb = psS[:, (tt % 2) * 512:(tt % 2) * 512 + 512]
                t.wait_ge(sem["v"], evk[tt % 2])
                mm = None
                for dcc in range(8):
                    mm = t.matmul(pb, wuq[:, (dcc * 8 + hl) * 128:(dcc * 8 + hl + 1) * 128],
                                  cq[:, dcc * S + tt * 512: dcc * S + (tt + 1) * 512],
                                  start=(dcc == 0), stop=(dcc == 7))
                n_t = inc(mm, "t")
                v.wait_ge(sem["t"], n_t)
                evk[tt % 2] = inc(v.tensor_scalar(qC[:, tt * 512:(tt + 1) * 512], pb,
                                                  buqc[:, hl:hl + 1], None, ALU.add), "v")
            evv = tick("v")
            for tc in range(32):
                t.wait_ge(sem["v"], evv)
                mm = None
                for dcc in range(4):
                    mm = t.matmul(psK[:, 0:128],
                                  ckv[:, dcc * S + tc * 128: dcc * S + (tc + 1) * 128],
                                  wuv[:, (dcc * 8 + hl) * 128:(dcc * 8 + hl + 1) * 128],
                                  start=(dcc == 0), stop=(dcc == 3))
                n_t = inc(mm, "t")
                v.wait_ge(sem["t"], n_t)
                evv = inc(v.tensor_tensor(vsb[:, tc * 128:(tc + 1) * 128], psK[:, 0:128],
                                          buvrep[:, hl * 128:(hl + 1) * 128], ALU.add),
                          "v")
            ev_q = tick("v")
            for tc in range(32):
                g.wait_ge(sem["v"], ev_q)
                lr1 = inc(g.dma_start(out=ropeC[:], in_=ropeCF_d[tc * 128:(tc + 1) * 128, :]), "dg", 16)
                lr2 = inc(g.dma_start(out=ropeS[:], in_=ropeSF_d[tc * 128:(tc + 1) * 128, :]), "dg", 16)
                t.wait_ge(sem["v"], ev_q)
                n_t = inc(t.transpose(psD[:, 0:128], qC[:, tc * 128:(tc + 1) * 128],
                                      identb[:]), "t")
                v.wait_ge(sem["t"], n_t)
                nv = inc(v.tensor_copy(qtok[:], psD[:, 0:128]), "v")
                v.wait_ge(sem["v"], nv)
                v.wait_ge(sem["dg"], lr2)
                v.tensor_scalar(qsw[:, 0:128:2], qtok[:, 1:128:2], -1.0, None, ALU.mult)
                nv = inc(v.tensor_copy(qsw[:, 1:128:2], qtok[:, 0:128:2]), "v")
                v.wait_ge(sem["v"], nv)
                v.tensor_tensor(qsw[:], qsw[:], ropeS[:], ALU.mult)
                nv = inc(v.scalar_tensor_tensor(qrt[:], qtok[:], 1.0, ropeC[:],
                                                ALU.mult, ALU.mult), "v")
                v.wait_ge(sem["v"], nv)
                nv = inc(v.tensor_tensor(qrt[:], qrt[:], qsw[:], ALU.add), "v")
                t.wait_ge(sem["v"], nv)
                n_t = inc(t.transpose(psD[:, 128:256], qrt[:], identb[:]), "t")
                v.wait_ge(sem["t"], n_t)
                ev_q = inc(v.tensor_copy(qR[:, tc * 128:(tc + 1) * 128], psD[:, 128:256]),
                           "v")
            nrm_parts = [(qC, qR, 0), (kC, kR, 1)]
            for (Ca, Ra, idx) in nrm_parts:
                for tt in range(8):
                    sl = slice(tt * 512, (tt + 1) * 512)
                    ns = inc(s.activation(sqrow[:], Ca[:, sl], AF.Square), "s")
                    t.wait_ge(sem["s"], ns)
                    if tt == 0:
                        t.wait_ge(sem["v"], tick("v"))
                    n_t = inc(t.matmul(psZ[:], onesb[:], sqrow[:],
                                       start=True, stop=False), "t")
                    s.wait_ge(sem["t"], n_t)
                    ns = inc(s.activation(sqrow[:], Ra[:, sl], AF.Square), "s")
                    t.wait_ge(sem["s"], ns)
                    n_t = inc(t.matmul(psZ[:], onesb[:], sqrow[:],
                                       start=False, stop=True), "t")
                    s.wait_ge(sem["t"], n_t)
                    v.wait_ge(sem["t"], n_t)
                    nv = inc(v.tensor_copy(nrow[:], psZ[:]), "v")
                    v.wait_ge(sem["v"], nv)
                    nv = inc(v.reduce_max(sc1[:, 4 + idx:5 + idx], nrow[:], axis=AX), "v")
                    v.wait_ge(sem["v"], nv)
                    if tt == 0:
                        nv = inc(v.tensor_copy(sc1[:, idx:idx + 1],
                                               sc1[:, 4 + idx:5 + idx]), "v")
                    else:
                        nv = inc(v.tensor_tensor(sc1[:, idx:idx + 1],
                                                 sc1[:, idx:idx + 1],
                                                 sc1[:, 4 + idx:5 + idx], ALU.max), "v")
                    v.wait_ge(sem["v"], nv)
                    t.wait_ge(sem["v"], nv)
            s.wait_ge(sem["v"], tick("v"))
            s.activation(sc1[:, 0:1], sc1[:, 0:1], AF.Ln, scale=1.0)
            ns = inc(s.activation(sc1[:, 1:2], sc1[:, 1:2], AF.Ln, scale=1.0), "s")
            v.wait_ge(sem["s"], ns)
            nv = inc(v.tensor_add(sc1[:, 2:3], sc1[:, 0:1], sc1[:, 1:2]), "v")
            s.wait_ge(sem["v"], nv)
            ns = inc(s.activation(sc1[:, 2:3], sc1[:, 2:3], AF.Exp, scale=0.5), "s")
            v.wait_ge(sem["s"], ns)
            nv = inc(v.tensor_scalar(sc1[:, 3:4], sc1[:, 2:3], -SCALE, None, ALU.mult),
                     "v")
            t.wait_ge(sem["v"], nv)
            n_t = inc(t.matmul(psK[:, 0:1], ones1f[:, 0:128], sc1[:, 3:4],
                               start=True, stop=True), "t")
            v.wait_ge(sem["t"], n_t)
            nv_m = inc(v.tensor_copy(mcol[:], psK[:, 0:1]), "v")

            t.wait_ge(sem["dg"], krl)
            ev_o = tick("v")
            for qt in range(8):
                nk = 4 * (qt + 1)
                pt_free = [tick("t")] * 3
                exp_done = [0] * nk
                exp_eng = [None] * nk
                sS_done = [0] * nk

                def emit_S(kc):
                    pbS = psS[:, (kc % 2) * 512: (kc % 2) * 512 + 512]
                    if kc >= 2:
                        t.wait_ge(sem[exp_eng[kc - 2]], exp_done[kc - 2])
                    mm = t.matmul(pbS, kC[:, kc * 128:(kc + 1) * 128],
                                  qC[:, qt * 512:(qt + 1) * 512], start=True, stop=False)
                    mm = t.matmul(pbS, kR[:, kc * 128:(kc + 1) * 128],
                                  qR[:, qt * 512:(qt + 1) * 512], start=False, stop=True)
                    sS_done[kc] = inc(mm, "t")

                def emit_exp(kc):
                    pbS = psS[:, (kc % 2) * 512: (kc % 2) * 512 + 512]
                    pt = ptile[kc % 3]
                    s.wait_ge(sem["t"], sS_done[kc])
                    s.wait_ge(sem["t"], pt_free[kc % 3])
                    if kc == 0 and qt == 0:
                        s.wait_ge(sem["v"], nv_m)
                    n_e = inc(s.activation(pt[:], pbS, AF.Exp, bias=mcol[:],
                                           scale=SCALE), "s")
                    off = kc - 4 * qt
                    if off >= 0:
                        v.wait_ge(sem["s"], n_e)
                        exp_done[kc] = inc(v.tensor_tensor(
                            pt[:], pt[:], dmask[:, off * 512:(off + 1) * 512],
                            ALU.mult), "v")
                        exp_eng[kc] = "v"
                    else:
                        exp_done[kc] = n_e
                        exp_eng[kc] = "s"

                def emit_ZPV(kc):
                    pt = ptile[kc % 3]
                    t.wait_ge(sem[exp_eng[kc]], exp_done[kc])
                    mm = t.matmul(psZ[:], onesb[:], pt[:],
                                  start=(kc == 0), stop=(kc == nk - 1))
                    mm = t.matmul(psO[:], vsb[:, kc * 128:(kc + 1) * 128], pt[:],
                                  start=(kc == 0), stop=(kc == nk - 1))
                    pt_free[kc % 3] = inc(mm, "t")

                emit_S(0)
                emit_exp(0)
                for kc in range(1, nk):
                    emit_S(kc)
                    emit_exp(kc)
                    emit_ZPV(kc - 1)
                emit_ZPV(nk - 1)
                n_fl = tick("t")
                v.wait_ge(sem["t"], n_fl)
                nv = inc(v.tensor_copy(zrow[:], psZ[:]), "v")
                v.wait_ge(sem["v"], nv)
                nv = inc(v.reciprocal(zrow[:], zrow[:]), "v")
                t.wait_ge(sem["v"], nv)
                n_t = inc(t.matmul(psK[:, 0:512], ones1f[:], zrow[:],
                                   start=True, stop=True), "t")
                v.wait_ge(sem["t"], n_t)
                nv = inc(v.tensor_copy(zrep[:], psK[:, 0:512]), "v")
                v.wait_ge(sem["v"], nv)
                ev_o = inc(v.tensor_tensor(oT[:, qt * 512:(qt + 1) * 512], psO[:],
                                           zrep[:], ALU.mult), "v")
                t.wait_ge(sem["v"], ev_o)
            y.wait_ge(sem["v"], ev_o)
            if oT_d.shape[0] == 256:
                inc(y.dma_start(out=oT_d[0:128, hl * TPC:(hl + 1) * TPC],
                                in_=oT[:, 0:TPC]), "dz", 16)
                inc(y.dma_start(out=oT_d[128:256, hl * TPC:(hl + 1) * TPC],
                                in_=oT[:, TPC:S]), "dz", 16)
            else:
                inc(y.dma_start(out=oT_d[:, hl * S:(hl + 1) * S], in_=oT[:]), "dz", 16)
            v.wait_ge(sem["dz"], tick("dz"))
        y.wait_ge(sem["dz"], tick("dz"))
        y.wait_ge(sem["dy"], tick("dy"))
        return ctx.finalize() if own else None



def build_D(ctx=None):
    NT4 = 4
    NIC = 11
    own = ctx is None
    with (Ctx("D") if own else ctx) as ctx:
        x1_d = ctx.din("x1", [TPC, D], BF16)
        xn1T_d = ctx.din("xn1T", [128, 16 * TPC], BF16)
        oT_d = ctx.din("oT", [128, 16 * TPC], BF16)
        cwrows_d = ctx.din("cwrows", [8, TPC], F32)
        wOT_d = ctx.din("wOT", [128, 16 * 16 * 128], F32)
        bOrow_d = ctx.din("bOrow", [1, D], F32)
        w1all_d = ctx.din("w1all", [128, NE * 16 * NIC * 128], BF16)
        w3all_d = ctx.din("w3all", [128, NE * 16 * NIC * 128], BF16)
        w2all_d = ctx.din("w2all", [128, NE * NIC * 16 * 128], BF16)
        sw1_d = ctx.din("sw1", [128, 16 * NIC * 128], BF16)
        sw3_d = ctx.din("sw3", [128, 16 * NIC * 128], BF16)
        sw2_d = ctx.din("sw2", [128, NIC * 16 * 128], BF16)
        selcol_d = ctx.din("selcol", [128, 2], F32)
        identin_d = ctx.din("identin", [128, 128], F32)
        out_d = ctx.dout("out", [TPC, D], F32)

        wO_scr = ctx.nc.dram_tensor("wO_scr", [128, 16 * 16 * 128], BF16).ap()

        ctx.alloc_sems(["t", "v", "s", "dy", "dg", "dz"])
        t, v, s, g, y = ctx.t, ctx.v, ctx.s, ctx.g, ctx.y
        sem, inc, tick = ctx.sem, ctx.inc, ctx.tick

        xn1T = ctx.sb("xn1Tsb", [128, 16 * TPC], BF16)
        hsb_e = ctx.sb("hsb_e", [128, NIC * TPC], BF16)
        a1_scr = ctx.nc.dram_tensor(ctx.prefix + "a1_scr", [TPC, D], BF16).ap()
        wst = [ctx.sb("wstD0", [128, 2 * 16 * 128], BF16)]
        otile = hsb_e[:, 0:2048]
        x1t = hsb_e[:, 0:2048]
        a1t = hsb_e[:, 2048:4096]
        outf = hsb_e[:].bitcast(F32)[:, 2048:4096]
        cwrep = ctx.sb("cwrepD", [128, TPC], BF16)
        ysbT = ctx.sb("ysbT", [128, 16 * TPC], BF16)
        w2q = ctx.sb("w2q", [128, NIC * 4 * 128], BF16)
        wmov = w2q[:, 0:2048]
        silu_g = ctx.sb("silugD", [128, 512], F32)
        tf32 = silu_g[:, 0:128]
        rowD = ctx.sb("rowD", [1, D], F32)
        bOrow = rowD
        cw1 = rowD
        ones1f = ctx.sb("ones1fD", [1, 128], F32)
        selcolD = ctx.sb("selcolD", [128, 2], F32)
        ident_f = ctx.sb("identfD", [128, 128], F32)

        psA = ctx.ps("psAD", [128, 1024], F32)
        psB = ctx.ps("psBD", [128, 1024], F32)
        psC = ctx.ps("psCD", [128, 2048], F32)

        FUSED_D = oT_d.shape[0] == 512
        nv_yz = inc(v.memset(ysbT[:], 0.0), "v")
        inc(y.dma_start(out=xn1T[:], in_=xn1T_d[:, :]), "dy", 16)
        inc(g.dma_start(out=bOrow[:], in_=bOrow_d[:, :]), "dg", 16)
        inc(g.dma_start(out=selcolD[:], in_=selcol_d[:, :]), "dg", 16)
        inc(g.dma_start(out=ident_f[:], in_=identin_d[:, :]), "dg", 16)
        ldc = tick("dg")
        v.wait_ge(sem["dg"], ldc)
        nv0 = inc(v.memset(ones1f[:], 1.0), "v")
        for fc in range(16):
            dst = wst[0][:, 0:2048]
            g.wait_ge(sem["dz"], max(0, fc) * 16)
            ld = inc(g.dma_start(out=dst, in_=wOT_d[:, fc * 2048:(fc + 1) * 2048]),
                     "dg", 16)
            y.wait_ge(sem["dg"], ld)
            inc(y.dma_start(out=wO_scr[:, fc * 2048:(fc + 1) * 2048], in_=dst), "dz", 16)
        wo_ready = tick("dz")
        xn1_ready = 16

        ev_y = nv0
        for tc in range(16):
            trow = tc * 128
            y.wait_ge(sem["t"], tick("t"))
            if FUSED_D:
                osc = hsb_e[:, 4096:6144]
                y.wait_ge(sem["v"], tick("v"))
                inc(y.dma_start(
                    out=otile[:].rearrange("p (f c) -> p f c", c=128)[:, 0:8, :],
                    in_=oT_d[0:128, :].rearrange("p (f t) -> p f t", t=TPC)[:, :, trow:trow + 128]), "dy", 16)
                inc(y.dma_start(
                    out=otile[:].rearrange("p (f c) -> p f c", c=128)[:, 8:16, :],
                    in_=oT_d[256:384, :].rearrange("p (f t) -> p f t", t=TPC)[:, :, trow:trow + 128]), "dy", 16)
                inc(y.dma_start(
                    out=osc.rearrange("p (f c) -> p f c", c=128)[:, 0:8, :],
                    in_=oT_d[128:256, :].rearrange("p (f t) -> p f t", t=TPC)[:, :, trow:trow + 128]), "dy", 16)
                lo4 = inc(y.dma_start(
                    out=osc.rearrange("p (f c) -> p f c", c=128)[:, 8:16, :],
                    in_=oT_d[384:512, :].rearrange("p (f t) -> p f t", t=TPC)[:, :, trow:trow + 128]),
                    "dy", 16)
                lo = lo4
                v.wait_ge(sem["dy"], lo4)
                v.tensor_scalar(otile, otile, selcolD[:, 0:1], None, ALU.mult)
                nbl = inc(v.scalar_tensor_tensor(otile, osc, selcolD[:, 1:2], otile,
                                                 ALU.mult, ALU.add), "v")
                t.wait_ge(sem["v"], nbl)
            else:
                lo = inc(y.dma_start(
                    out=otile[:].rearrange("p (f c) -> p f c", c=128),
                    in_=oT_d.rearrange("p (f t) -> p f t", t=TPC)[:, :, trow:trow + 128]),
                    "dy", 16)
            mm = None
            for fc in range(16):
                g.wait_ge(sem["t"], tick("t"))
                if fc == 0:
                    g.wait_ge(sem["dz"], wo_ready)
                lw = inc(g.dma_start(out=wmov, in_=wO_scr[:, fc * 2048:(fc + 1) * 2048]),
                         "dg", 16)
                t.wait_ge(sem["dg"], lw)
                t.wait_ge(sem["dy"], lo)
                if fc == 0:
                    t.wait_ge(sem["v"], ev_y)
                for ot in range(4):
                    mm = t.matmul(psC[:, ot * 512:(ot + 1) * 512],
                                  otile[:, fc * 128:(fc + 1) * 128],
                                  wmov[:, ot * 512:(ot + 1) * 512],
                                  start=(fc == 0), stop=False)
                inc(mm, "t")
            for ot in range(4):
                mm = t.matmul(psC[:, ot * 512:(ot + 1) * 512], ones1f[:],
                              bOrow[:, ot * 512:(ot + 1) * 512],
                              start=False, stop=True)
            n_t = inc(mm, "t")
            v.wait_ge(sem["t"], n_t)
            v.wait_ge(sem["dz"], tick("dz"))
            ev_y = inc(v.tensor_copy(a1t, psC[:]), "v")
            y.wait_ge(sem["v"], ev_y)
            inc(y.dma_start(out=a1_scr[trow:trow + 128, :], in_=a1t), "dz", 16)

        for e in range(NE + 1):
            shared = (e == NE)
            if shared:
                w1src, w3src, w2src = sw1_d, sw3_d, sw2_d
                eoff1 = 0
                eoff2 = 0
            else:
                w1src, w3src, w2src = w1all_d, w3all_d, w2all_d
                eoff1 = e * 16 * NIC * 128
                eoff2 = e * NIC * 16 * 128
            if not shared:
                g.wait_ge(sem["v"], tick("v"))
                g.wait_ge(sem["t"], tick("t"))
                lcw = inc(g.dma_start(out=cw1[:], in_=cwrows_d[e:e + 1, :]), "dg", 16)
                t.wait_ge(sem["dg"], lcw)
                for qt in range(4):
                    t.wait_ge(sem["v"], tick("v"))
                    n_t = inc(t.matmul(psB[:, 0:512], ones1f[:],
                                       cw1[:, qt * 512:(qt + 1) * 512],
                                       start=True, stop=True), "t")
                    v.wait_ge(sem["t"], n_t)
                    inc(v.tensor_copy(cwrep[:, qt * 512:(qt + 1) * 512], psB[:, 0:512]),
                        "v")
            cw_ready = tick("v")
            wst_last = [0]
            for ic in range(NIC):
                wb = wst[0]
                g.wait_ge(sem["t"], wst_last[0])
                l1 = inc(g.dma_start(
                    out=wb[:, 0:2048].rearrange("p (f o) -> p f o", o=128),
                    in_=w1src.rearrange("p (f o) -> p f o", o=128)[:, eoff1 // 128 + ic::NIC, :]
                    if False else
                    w1src.rearrange("p (f o) -> p f o", o=128)[:, eoff1 // 128 + ic: eoff1 // 128 + 16 * NIC: NIC, :]),
                    "dg", 16)
                l2 = inc(g.dma_start(
                    out=wb[:, 2048:4096].rearrange("p (f o) -> p f o", o=128),
                    in_=w3src.rearrange("p (f o) -> p f o", o=128)[:, eoff1 // 128 + ic: eoff1 // 128 + 16 * NIC: NIC, :]),
                    "dg", 16)
                t.wait_ge(sem["dg"], l2)
                for tt in range(NT4):
                    pg = psA[:, 0:512]
                    pu = psA[:, 512:1024]
                    t.wait_ge(sem["s"], tick("s"))
                    t.wait_ge(sem["v"], tick("v"))
                    if e == 0 and ic == 0 and tt == 0:
                        t.wait_ge(sem["dy"], 16)
                    mm = None
                    for fc in range(16):
                        mm = t.matmul(pg, wb[:, fc * 128:(fc + 1) * 128],
                                      xn1T[:, fc * TPC + tt * 512: fc * TPC + (tt + 1) * 512],
                                      start=(fc == 0), stop=(fc == 15))
                    n_gm = inc(mm, "t")
                    mm = None
                    for fc in range(16):
                        mm = t.matmul(pu, wb[:, 2048 + fc * 128:2048 + (fc + 1) * 128],
                                      xn1T[:, fc * TPC + tt * 512: fc * TPC + (tt + 1) * 512],
                                      start=(fc == 0), stop=(fc == 15))
                    n_um = inc(mm, "t")
                    s.wait_ge(sem["t"], n_gm)
                    n_sg = inc(s.activation(silu_g[:], pg, AF.Silu), "s")
                    v.wait_ge(sem["t"], n_um)
                    v.wait_ge(sem["s"], n_sg)
                    if not shared:
                        nv = inc(v.tensor_tensor(silu_g[:], silu_g[:],
                                                 cwrep[:, tt * 512:(tt + 1) * 512],
                                                 ALU.mult), "v")
                        v.wait_ge(sem["v"], nv)
                    inc(v.tensor_tensor(
                        hsb_e[:, ic * TPC + tt * 512: ic * TPC + (tt + 1) * 512],
                        pu, silu_g[:], ALU.mult), "v")
                wst_last[0] = tick("t")
            nv_h = tick("v")
            ev_yb = nv_yz
            for ocq in range(4):
                g.wait_ge(sem["t"], tick("t"))
                lwq = inc(g.dma_start(
                    out=w2q[:].rearrange("p (i o) -> p i o", o=512),
                    in_=w2src[:, eoff2:eoff2 + NIC * 16 * 128]
                    .rearrange("p (i o) -> p i o", o=16 * 128)[:, :, ocq * 512:(ocq + 1) * 512]),
                    "dg", 16)
                t.wait_ge(sem["dg"], lwq)
                for tt in range(NT4):
                    t.wait_ge(sem["v"], max(nv_h, ev_yb))
                    mm = None
                    for ic in range(NIC):
                        for j in range(4):
                            mm = t.matmul(psC[:, j * 512:(j + 1) * 512],
                                          w2q[:, (ic * 4 + j) * 128:(ic * 4 + j + 1) * 128],
                                          hsb_e[:, ic * TPC + tt * 512: ic * TPC + (tt + 1) * 512],
                                          start=(ic == 0), stop=(ic == NIC - 1))
                    n_t = inc(mm, "t")
                    v.wait_ge(sem["t"], n_t)
                    last = None
                    for j in range(4):
                        oc = ocq * 4 + j
                        last = v.tensor_tensor(
                            ysbT[:, oc * TPC + tt * 512: oc * TPC + (tt + 1) * 512],
                            ysbT[:, oc * TPC + tt * 512: oc * TPC + (tt + 1) * 512],
                            psC[:, j * 512:(j + 1) * 512], ALU.add)
                    ev_yb = inc(last, "v")

        t.wait_ge(sem["s"], tick("s"))
        ev_fin = tick("v")
        ev_tr = [tick("v")] * 2
        for tc in range(16):
            trow = tc * 128
            y.wait_ge(sem["v"], ev_fin)
            lx = inc(y.dma_start(out=x1t, in_=x1_d[trow:trow + 128, :]), "dy", 16)
            lx = inc(y.dma_start(out=a1t, in_=a1_scr[trow:trow + 128, :]), "dy", 16)
            v.wait_ge(sem["dy"], lx)
            v.wait_ge(sem["dz"], tick("dz"))
            nv = inc(v.tensor_tensor(outf[:], a1t, x1t, ALU.add), "v")
            for fc in range(16):
                v.wait_ge(sem["v"], nv if fc == 0 else 0)
                nvc = inc(v.tensor_copy(tf32, ysbT[:, fc * TPC + trow: fc * TPC + trow + 128]), "v")
                pb = (psA if fc % 2 == 0 else psB)[:, 0:128]
                t.wait_ge(sem["v"], max(nvc, ev_tr[fc % 2]))
                n_t = inc(t.transpose(pb, tf32, ident_f[:]), "t")
                v.wait_ge(sem["t"], n_t)
                nv = inc(v.tensor_tensor(outf[:, fc * 128:(fc + 1) * 128],
                                         outf[:, fc * 128:(fc + 1) * 128], pb, ALU.add),
                         "v")
                ev_tr[fc % 2] = nv
            y.wait_ge(sem["v"], nv)
            inc(y.dma_start(out=out_d[trow:trow + 128, :], in_=outf), "dz", 16)
            ev_fin = nv
        y.wait_ge(sem["dz"], tick("dz"))
        y.wait_ge(sem["dy"], tick("dy"))
        return ctx.finalize() if own else None



def _barrier_all(ctx):
    for eng in (ctx.t, ctx.v, ctx.s, ctx.g, ctx.y):
        for k in list(ctx.sem):
            vv = ctx.tick(k)
            if vv:
                eng.wait_ge(ctx.sem[k], vv)


def _wait_all(ctx, key, val):
    for eng in (ctx.t, ctx.v, ctx.s, ctx.g, ctx.y):
        eng.wait_ge(ctx.sem[key], val)


def build_fused():
    ctx = Ctx("FUSED")
    nc = ctx.nc
    ctx.alloc_sems(["t", "v", "s", "dy", "dg", "dz", "c1", "c2", "c3", "c4"])
    rg = [[0, 1], [2, 3], [4, 5], [6, 7]]
    BF_ = BF16
    ag1i = nc.dram_tensor("ag1i", [128, 16 * TPC], BF_).ap()
    ag1o = nc.dram_tensor("ag1o", [256, 16 * TPC], BF_).ap()
    ag2i = nc.dram_tensor("ag2i", [256, 8 * TPC], BF_).ap()
    ag2o = nc.dram_tensor("ag2o", [512, 8 * TPC], BF_).ap()
    ag3i = nc.dram_tensor("ag3i", [128, 28 * TPC], BF_).ap()
    ag3o = nc.dram_tensor("ag3o", [256, 28 * TPC], BF_).ap()
    ag4i = nc.dram_tensor("ag4i", [256, 8 * TPC], BF_).ap()
    ag4o = nc.dram_tensor("ag4o", [512, 8 * TPC], BF_).ap()
    x1_i = nc.dram_tensor("x1_i", [TPC, D], BF_).ap()
    xn1T_i = nc.dram_tensor("xn1T_i", [128, 16 * TPC], BF_).ap()
    cw_i = nc.dram_tensor("cw_i", [8, TPC], F32).ap()

    def agcall(key, ain, aout):
        ctx.g.wait_ge(ctx.sem["dz"], ctx.tick("dz"))
        ctx.g.wait_ge(ctx.sem["dy"], ctx.tick("dy"))
        ctx.inc(ctx.g.collective_compute(
            "AllGather", ALU.bypass, replica_groups=rg,
            ins=[ain.opt()], outs=[aout.opt()]), key, 1)
        _wait_all(ctx, key, ctx.tick(key))

    ctx.prefix = "a0_"; ctx.io = {"xn0T": ag1i}
    m = ctx.mark(); build_A0(ctx); ctx.release(m); _barrier_all(ctx)
    agcall("c1", ag1i, ag1o)
    ctx.prefix = "a1_"; ctx.io = {"xn0T": ag1o, "attnT": ag2i}
    m = ctx.mark(); build_A1(ctx); ctx.release(m); _barrier_all(ctx)
    agcall("c2", ag2i, ag2o)
    ctx.prefix = "b_"
    ctx.io = {"attnT": ag2o, "xn0T": ag1i, "x1": x1_i, "xn1T": xn1T_i, "cwrows": cw_i,
              "ckvT": ag3i[:, 0:4 * TPC], "cqT": ag3i[:, 4 * TPC:12 * TPC],
              "krT": ag3i[:, 12 * TPC:28 * TPC]}
    m = ctx.mark(); build_B(ctx); ctx.release(m); _barrier_all(ctx)
    agcall("c3", ag3i, ag3o)
    ctx.prefix = "c_"
    ctx.io = {"ckvT": ag3o[:, 0:4 * TPC], "cqT": ag3o[:, 4 * TPC:12 * TPC],
              "krT": ag3o[:, 12 * TPC:28 * TPC], "oT": ag4i}
    m = ctx.mark(); build_C(ctx); ctx.release(m); _barrier_all(ctx)
    agcall("c4", ag4i, ag4o)
    ctx.prefix = "d_"
    ctx.io = {"oT": ag4o, "x1": x1_i, "xn1T": xn1T_i, "cwrows": cw_i}
    m = ctx.mark(); build_D(ctx); ctx.release(m)
    ctx.prefix = ""
    ctx.io = {}
    return ctx.finalize()


def run_fused(inputs):
    con = host_tables()
    ncF = _prog("FUSED", build_fused)
    x = np.ascontiguousarray(
        np.asarray(inputs["L0_attn_hidden_states"]).reshape(B * S, D))
    qkv_w = inputs["L0_attn_qkv_w"]
    norm_w = np.asarray(inputs["L0_attn_norm_w"])
    wg8 = _u8view(np.ascontiguousarray(inputs["L0_ffn_gate_w"]))
    wu8 = _u8view(np.ascontiguousarray(inputs["L0_ffn_up_w"]))
    wd8 = _u8view(np.ascontiguousarray(inputs["L0_ffn_down_w"]))
    gatewT = p16w(np.ascontiguousarray(inputs["L0_attn_gate_w"].T))
    outwT = p16w(np.ascontiguousarray(inputs["L0_attn_out_w"].T))
    wg8T = p16w(np.ascontiguousarray(wg8.T))
    wu8T = p16w(np.ascontiguousarray(wu8.T))
    wd8T = p16w(np.ascontiguousarray(wd8.T))
    gscol = np.ascontiguousarray(inputs["L0_ffn_gate_scale"].reshape(64, 128).T)
    uscol = np.ascontiguousarray(inputs["L0_ffn_up_scale"].reshape(64, 128).T)
    dsrow = np.ascontiguousarray(inputs["L0_ffn_down_scale"].reshape(1, D))
    normcol = np.ascontiguousarray(norm_w.reshape(16, 128).T)
    gwT = np.ascontiguousarray(
        inputs["L1_ffn_gate_w"].T.reshape(16, 128, 8).transpose(1, 0, 2).reshape(128, 16 * 8))
    gbrep = np.ascontiguousarray(np.repeat(np.asarray(inputs["L1_ffn_gate_b"])[None, :], 128, 0))
    wdkvT = p16w(np.ascontiguousarray(inputs["L1_attn_W_DKV_w"].T))
    wdqT = p16w(np.ascontiguousarray(inputs["L1_attn_W_DQ_w"].T))
    wkrT = p16w(np.ascontiguousarray(inputs["L1_attn_W_KR_w"].T))
    bkvcol = np.ascontiguousarray(np.concatenate([
        np.asarray(inputs["L1_attn_W_DKV_b"]).reshape(4, 128).T,
        np.asarray(inputs["L1_attn_W_DQ_b"]).reshape(8, 128).T], axis=1).astype(np.float32))
    bkrrep = np.ascontiguousarray(
        np.repeat(np.asarray(inputs["L1_attn_W_KR_b"])[None, :], 128, 0))
    wOT = p16w(np.ascontiguousarray(inputs["L1_attn_W_O_w"].T))
    bOrow = np.ascontiguousarray(np.asarray(inputs["L1_attn_W_O_b"]).reshape(1, D))
    w1 = np.asarray(inputs["L1_ffn_w1"]); w3 = np.asarray(inputs["L1_ffn_w3"])
    w2 = np.asarray(inputs["L1_ffn_w2"])
    w1all = np.concatenate([p16w(np.ascontiguousarray(w1[e].T)) for e in range(NE)], 1)
    w3all = np.concatenate([p16w(np.ascontiguousarray(w3[e].T)) for e in range(NE)], 1)
    w2all = np.concatenate([p16w(np.ascontiguousarray(w2[e].T)) for e in range(NE)], 1)
    sw1 = p16w(np.ascontiguousarray(np.asarray(inputs["L1_ffn_sw1"]).T))
    sw3 = p16w(np.ascontiguousarray(np.asarray(inputs["L1_ffn_sw3"]).T))
    sw2 = p16w(np.ascontiguousarray(np.asarray(inputs["L1_ffn_sw2"]).T))
    in_maps = []
    for c in range(NCORE):
        b, half = c // 2, c % 2
        hh = 16 * half
        t0 = half * TPC
        wq = []
        for grp in range(2):
            rows = np.concatenate(
                [qkv_w[(hh + grp * 8 + i) * 192 + part * 64:
                       (hh + grp * 8 + i) * 192 + (part + 1) * 64]
                 for part in range(3) for i in range(8)], axis=0)
            wq.append(p16w(np.ascontiguousarray(rows.T)))
        bdcol = np.repeat(con["bd"][hh:hh + 16].reshape(1, 16), 64, 0).astype(np.float32)
        hh8 = 8 * half
        hrows = slice(hh8 * DH, (hh8 + 8) * DH)
        wuk = p16w(np.ascontiguousarray(inputs["L1_attn_W_UK_w"][hrows].T))
        wuv = p16w(np.ascontiguousarray(inputs["L1_attn_W_UV_w"][hrows].T))
        wuq = p16w(np.ascontiguousarray(inputs["L1_attn_W_UQ_w"][hrows].T))
        buk = np.ascontiguousarray(
            np.asarray(inputs["L1_attn_W_UK_b"])[hrows].reshape(8, 128).T.astype(np.float32))
        buq = np.ascontiguousarray(
            np.asarray(inputs["L1_attn_W_UQ_b"])[hrows].reshape(8, 128).T.astype(np.float32))
        buv = np.ascontiguousarray(np.repeat(
            np.asarray(inputs["L1_attn_W_UV_b"])[hrows].reshape(1, 8 * 128), 128, 0))
        sel = np.zeros((128, 2), np.float32)
        sel[:, half] = 1.0
        in_maps.append({
            "a0_xin": x[c * TPC:(c + 1) * TPC], "a0_identin": con["ident_bf"],
            "a1_wqkv0": wq[0], "a1_wqkv1": wq[1],
            "a1_qdrep": np.ascontiguousarray(con["qd_rep"][:, hh * BS0:(hh + 16) * BS0]),
            "a1_kdrep": np.ascontiguousarray(con["kd_rep"][:, hh * BS0:(hh + 16) * BS0]),
            "a1_ddT": np.ascontiguousarray(con["ddT"][:, hh * 2 * BS0:(hh + 16) * 2 * BS0]),
            "a1_bdcol": bdcol, "a1_identin": con["ident_bf"],
            "b_x": x[c * TPC:(c + 1) * TPC],
            "b_normcol": normcol, "b_gatewT": gatewT, "b_outwT": outwT,
            "b_wg8": wg8T, "b_wu8": wu8T, "b_wd8": wd8T,
            "b_gscol": gscol, "b_uscol": uscol, "b_dsrow": dsrow,
            "b_identin": con["ident_bf"],
            "b_ropeC": np.ascontiguousarray(con["ropeC"][t0:t0 + TPC]),
            "b_ropeS": np.ascontiguousarray(con["ropeS"][t0:t0 + TPC]),
            "b_gwT": gwT, "b_gbrep": gbrep,
            "b_wdkvT": wdkvT, "b_wdqT": wdqT, "b_wkrT": wkrT,
            "b_bkvcol": bkvcol, "b_bkrrep": bkrrep, "b_selcol": sel,
            "c_wukT": wuk, "c_wuvT": wuv, "c_wuqT": wuq,
            "c_bukcol": buk, "c_buvrep": buv, "c_buqcol": buq,
            "c_ropeCF": con["ropeC"], "c_ropeSF": con["ropeS"],
            "c_dmask": con["diagmask"], "c_identin": con["ident_bf"],
            "c_selcol": sel,
            "d_wOT": wOT, "d_bOrow": bOrow,
            "d_w1all": w1all, "d_w3all": w3all, "d_w2all": w2all,
            "d_sw1": sw1, "d_sw3": sw3, "d_sw2": sw2, "d_selcol": sel,
            "d_identin": con["ident_bf"],
        })
    res = run_bass_kernel_spmd(ncF, in_maps, core_ids=list(range(NCORE)))
    outs = [res.results[c]["d_out"] for c in range(NCORE)]
    return np.concatenate(outs, 0).reshape(B, S, D)



_P_CACHE = {}


def _prog(name, builder):
    if name not in _P_CACHE:
        _P_CACHE[name] = builder()
    return _P_CACHE[name]


def p16w(wT):
    F_, O_ = wT.shape
    return np.ascontiguousarray(
        wT.reshape(F_ // 128, 128, O_ // 128, 128).transpose(1, 0, 2, 3).reshape(128, -1))


def run_A0(x):
    ncA0 = _prog("A0", build_A0)
    con = host_tables()
    in_maps = [{"xin": x[c * TPC:(c + 1) * TPC], "identin": con["ident_bf"]}
               for c in range(NCORE)]
    res = run_bass_kernel_spmd(ncA0, in_maps, core_ids=list(range(NCORE)))
    return [res.results[c]["xn0T"] for c in range(NCORE)]


def _xn0T_full(xn0T, b):
    e, o = xn0T[2 * b], xn0T[2 * b + 1]
    out = np.empty((128, 16, S), dtype=e.dtype)
    out[:, :, :TPC] = e.reshape(128, 16, TPC)
    out[:, :, TPC:] = o.reshape(128, 16, TPC)
    return np.ascontiguousarray(out.reshape(128, 16 * S))


def run_A1(xn0T, inputs):
    con = host_tables()
    ncA1 = _prog("A1", build_A1)
    qkv_w = inputs["L0_attn_qkv_w"]
    in_maps = []
    for c in range(NCORE):
        b, half = c // 2, c % 2
        hh = 16 * half
        wq = []
        for grp in range(2):
            rows = np.concatenate(
                [qkv_w[(hh + grp * 8 + i) * 192 + part * 64:
                       (hh + grp * 8 + i) * 192 + (part + 1) * 64]
                 for part in range(3) for i in range(8)], axis=0)
            wq.append(p16w(np.ascontiguousarray(rows.T)))
        bdcol = np.repeat(con["bd"][hh:hh + 16].reshape(1, 16), 64, 0).astype(np.float32)
        in_maps.append({
            "xn0T": _xn0T_full(xn0T, b),
            "wqkv0": wq[0], "wqkv1": wq[1],
            "qdrep": np.ascontiguousarray(con["qd_rep"][:, hh * BS0:(hh + 16) * BS0]),
            "kdrep": np.ascontiguousarray(con["kd_rep"][:, hh * BS0:(hh + 16) * BS0]),
            "ddT": np.ascontiguousarray(con["ddT"][:, hh * 2 * BS0:(hh + 16) * 2 * BS0]),
            "bdcol": bdcol,
            "identin": con["ident_bf"],
        })
    res = run_bass_kernel_spmd(ncA1, in_maps, core_ids=list(range(NCORE)))
    return [res.results[c]["attnT"] for c in range(NCORE)]


def _u8view(w_int32):
    return w_int32.view(np.uint8).reshape(*w_int32.shape, 4)[..., 0]


def run_B(x, xn0T, attnT, inputs):
    con = host_tables()
    ncB = _prog("B", build_B)
    gate_w = inputs["L0_attn_gate_w"]; out_w = inputs["L0_attn_out_w"]
    norm_w = np.asarray(inputs["L0_attn_norm_w"])
    wg8 = _u8view(np.ascontiguousarray(inputs["L0_ffn_gate_w"]))
    wu8 = _u8view(np.ascontiguousarray(inputs["L0_ffn_up_w"]))
    wd8 = _u8view(np.ascontiguousarray(inputs["L0_ffn_down_w"]))
    gatewT = p16w(np.ascontiguousarray(gate_w.T))
    outwT = p16w(np.ascontiguousarray(out_w.T))
    wg8T = p16w(np.ascontiguousarray(wg8.T))
    wu8T = p16w(np.ascontiguousarray(wu8.T))
    wd8T = p16w(np.ascontiguousarray(wd8.T))
    gscol = np.ascontiguousarray(inputs["L0_ffn_gate_scale"].reshape(64, 128).T)
    uscol = np.ascontiguousarray(inputs["L0_ffn_up_scale"].reshape(64, 128).T)
    dsrow = np.ascontiguousarray(inputs["L0_ffn_down_scale"].reshape(1, D))
    normcol = np.ascontiguousarray(norm_w.reshape(16, 128).T)
    gwT = np.ascontiguousarray(
        inputs["L1_ffn_gate_w"].T.reshape(16, 128, 8).transpose(1, 0, 2).reshape(128, 16 * 8))
    gbrep = np.ascontiguousarray(np.repeat(np.asarray(inputs["L1_ffn_gate_b"])[None, :], 128, 0))
    wdkvT = p16w(np.ascontiguousarray(inputs["L1_attn_W_DKV_w"].T))
    wdqT = p16w(np.ascontiguousarray(inputs["L1_attn_W_DQ_w"].T))
    wkrT = p16w(np.ascontiguousarray(inputs["L1_attn_W_KR_w"].T))
    bkvcol = np.concatenate([
        np.asarray(inputs["L1_attn_W_DKV_b"]).reshape(4, 128).T,
        np.asarray(inputs["L1_attn_W_DQ_b"]).reshape(8, 128).T], axis=1)
    bkvcol = np.ascontiguousarray(bkvcol.astype(np.float32))
    bkrrep = np.ascontiguousarray(
        np.repeat(np.asarray(inputs["L1_attn_W_KR_b"])[None, :], 128, 0))
    in_maps = []
    for c in range(NCORE):
        b, half = c // 2, c % 2
        t0 = half * TPC
        at = np.empty((128, 16, TPC), dtype=attnT[0].dtype)
        for fc in range(8):
            at[:, fc, :] = attnT[2 * b].reshape(128, 8, S)[:, fc, t0:t0 + TPC]
            at[:, 8 + fc, :] = attnT[2 * b + 1].reshape(128, 8, S)[:, fc, t0:t0 + TPC]
        sel = np.zeros((128, 2), np.float32)
        sel[:, c % 2] = 1.0
        in_maps.append({
            "selcol": sel,
            "x": x[c * TPC:(c + 1) * TPC],
            "xn0T": xn0T[c],
            "attnT": np.ascontiguousarray(at.reshape(128, 16 * TPC)),
            "normcol": normcol, "gatewT": gatewT, "outwT": outwT,
            "wg8": wg8T, "wu8": wu8T, "wd8": wd8T,
            "gscol": gscol, "uscol": uscol, "dsrow": dsrow,
            "identin": con["ident_bf"],
            "ropeC": np.ascontiguousarray(con["ropeC"][t0:t0 + TPC]),
            "ropeS": np.ascontiguousarray(con["ropeS"][t0:t0 + TPC]),
            "gwT": gwT, "gbrep": gbrep,
            "wdkvT": wdkvT, "wdqT": wdqT, "wkrT": wkrT,
            "bkvcol": bkvcol, "bkrrep": bkrrep,
        })
    res = run_bass_kernel_spmd(ncB, in_maps, core_ids=list(range(NCORE)))
    keys = ["x1", "xn1T", "ckvT", "cqT", "krT", "cwrows"] + (["hdbg", "zdbg"] if DEBUG_B else [])
    return {k: [res.results[c][k] for c in range(NCORE)] for k in keys}


def _pair_full(slices, b, nch):
    e = np.asarray(slices[2 * b]).reshape(128, nch, TPC)
    o = np.asarray(slices[2 * b + 1]).reshape(128, nch, TPC)
    out = np.empty((128, nch, S), dtype=e.dtype)
    out[:, :, :TPC] = e
    out[:, :, TPC:] = o
    return np.ascontiguousarray(out.reshape(128, nch * S))


def run_C(rb, inputs):
    con = host_tables()
    ncC = _prog("C", build_C)
    in_maps = []
    for c in range(NCORE):
        b, half = c // 2, c % 2
        hh8 = 8 * half
        rows = slice(hh8 * DH, (hh8 + 8) * DH)
        wuk = p16w(np.ascontiguousarray(inputs["L1_attn_W_UK_w"][rows].T))
        wuv = p16w(np.ascontiguousarray(inputs["L1_attn_W_UV_w"][rows].T))
        wuq = p16w(np.ascontiguousarray(inputs["L1_attn_W_UQ_w"][rows].T))
        buk = np.ascontiguousarray(
            np.asarray(inputs["L1_attn_W_UK_b"])[rows].reshape(8, 128).T.astype(np.float32))
        buq = np.ascontiguousarray(
            np.asarray(inputs["L1_attn_W_UQ_b"])[rows].reshape(8, 128).T.astype(np.float32))
        buv = np.ascontiguousarray(np.repeat(
            np.asarray(inputs["L1_attn_W_UV_b"])[rows].reshape(1, 8 * 128), 128, 0))
        krT_full = _pair_full(rb["krT"], b, 16).reshape(128, 16, S)
        krT_slice = np.ascontiguousarray(
            krT_full[:, hh8:hh8 + 8, :].reshape(128, 8 * S))
        sel = np.zeros((128, 2), np.float32)
        sel[:, c % 2] = 1.0
        in_maps.append({
            "selcol": sel,
            "ckvT": _pair_full(rb["ckvT"], b, 4),
            "cqT": _pair_full(rb["cqT"], b, 8),
            "krT": krT_slice,
            "wukT": wuk, "wuvT": wuv, "wuqT": wuq,
            "bukcol": buk, "buvrep": buv, "buqcol": buq,
            "ropeCF": con["ropeC"], "ropeSF": con["ropeS"],
            "dmask": con["diagmask"], "identin": con["ident_bf"],
        })
    res = run_bass_kernel_spmd(ncC, in_maps, core_ids=list(range(NCORE)))
    return [res.results[c]["oT"] for c in range(NCORE)]


def run_D(rb, oT, inputs):
    ncD = _prog("D", build_D)
    wOT = p16w(np.ascontiguousarray(inputs["L1_attn_W_O_w"].T))
    bOrow = np.ascontiguousarray(np.asarray(inputs["L1_attn_W_O_b"]).reshape(1, D))
    w1 = np.asarray(inputs["L1_ffn_w1"]); w3 = np.asarray(inputs["L1_ffn_w3"])
    w2 = np.asarray(inputs["L1_ffn_w2"])
    w1all = np.concatenate([p16w(np.ascontiguousarray(w1[e].T)) for e in range(NE)], 1)
    w3all = np.concatenate([p16w(np.ascontiguousarray(w3[e].T)) for e in range(NE)], 1)
    w2all = np.concatenate([p16w(np.ascontiguousarray(w2[e].T)) for e in range(NE)], 1)
    sw1 = p16w(np.ascontiguousarray(np.asarray(inputs["L1_ffn_sw1"]).T))
    sw3 = p16w(np.ascontiguousarray(np.asarray(inputs["L1_ffn_sw3"]).T))
    sw2 = p16w(np.ascontiguousarray(np.asarray(inputs["L1_ffn_sw2"]).T))
    in_maps = []
    for c in range(NCORE):
        b, half = c // 2, c % 2
        t0 = half * TPC
        ot_own = np.empty((128, 16, TPC), dtype=oT[0].dtype)
        for fc in range(8):
            ot_own[:, fc, :] = np.asarray(oT[2 * b]).reshape(128, 8, S)[:, fc, t0:t0 + TPC]
            ot_own[:, 8 + fc, :] = np.asarray(oT[2 * b + 1]).reshape(128, 8, S)[:, fc, t0:t0 + TPC]
        sel = np.zeros((128, 2), np.float32)
        sel[:, c % 2] = 1.0
        in_maps.append({
            "selcol": sel,
            "x1": rb["x1"][c], "xn1T": rb["xn1T"][c],
            "oT": np.ascontiguousarray(ot_own.reshape(128, 16 * TPC)),
            "cwrows": rb["cwrows"][c],
            "wOT": wOT, "bOrow": bOrow,
            "w1all": w1all, "w3all": w3all, "w2all": w2all,
            "sw1": sw1, "sw3": sw3, "sw2": sw2,
            "identin": host_tables()["ident_bf"],
        })
    res = run_bass_kernel_spmd(ncD, in_maps, core_ids=list(range(NCORE)))
    return [res.results[c]["out"] for c in range(NCORE)]


def kernel(**inputs):
    return run_fused(inputs)


def kernel_unfused(**inputs):
    x = np.ascontiguousarray(
        np.asarray(inputs["L0_attn_hidden_states"]).reshape(B * S, D))
    xn0T = run_A0(x)
    attnT = run_A1(xn0T, inputs)
    rb = run_B(x, xn0T, attnT, inputs)
    oT = run_C(rb, inputs)
    outs = run_D(rb, oT, inputs)
    return np.concatenate(outs, 0).reshape(B, S, D)
